# version 04
# baseline (speedup 1.0000x reference)
import os
import math
import contextlib
import numpy as np

import jax

try:
    jax.config.update("jax_compilation_cache_dir", "/tmp/jax_neff_cache")
    jax.config.update("jax_persistent_cache_min_compile_time_secs", 1.0)
except Exception:
    pass

import ml_dtypes
import concourse.bass as bass
import concourse.mybir as mybir
from concourse.bass_utils import run_bass_kernel_spmd

F32 = mybir.dt.float32
BF16 = mybir.dt.bfloat16
I32 = mybir.dt.int32
AF = mybir.ActivationFunctionType
ALU = mybir.AluOpType

NCORE = 8
T = 2048
D = 1024
DC = 8
S = 16384
LN_EPS = 1e-5
NEXP = 8
HID = 4096

BF = ml_dtypes.bfloat16


class Sem:
    __slots__ = ("h", "n")

    def __init__(self, h):
        self.h = h
        self.n = 0


def inc(sem, instr, k=1):
    instr.then_inc(sem.h, k)
    sem.n += k
    return sem.n


def wt(eng, sem, val=None):
    v = sem.n if val is None else val
    if v > 0:
        eng.wait_ge(sem.h, v)


class Prog:
    def __init__(self, name="prog"):
        self.nc = bass.Bass("TRN2", target_bir_lowering=False, debug=False)
        self.es = contextlib.ExitStack()
        self.t = {}

    def din(self, name, shape, dt=F32):
        ap = self.nc.dram_tensor(name, list(shape), dt, kind="ExternalInput").ap()
        self.t[name] = ap
        return ap

    def dout(self, name, shape, dt=F32):
        ap = self.nc.dram_tensor(name, list(shape), dt, kind="ExternalOutput").ap()
        self.t[name] = ap
        return ap

    def dtmp(self, name, shape, dt=F32, shared=False):
        kw = {"addr_space": "Shared"} if shared else {}
        ap = self.nc.dram_tensor(name, list(shape), dt, **kw).ap()
        self.t[name] = ap
        return ap

    def sb(self, name, shape, dt):
        h = self.es.enter_context(self.nc.sbuf_tensor(name, list(shape), dt))
        self.t[name] = h
        return h

    def ps(self, name, shape, dt=F32):
        h = self.es.enter_context(self.nc.psum_tensor(name, list(shape), dt))
        self.t[name] = h
        return h

    def sem(self, name):
        return Sem(self.es.enter_context(self.nc.semaphore(name)))

    def block(self):
        self.blk = self.es.enter_context(self.nc.Block())
        return self.blk

    def close(self):
        self.es.close()



CONST_VALS = []


def register_const(p, value, dtype=F32):
    key = (dtype, float(value))
    cells = p.t.setdefault("_const_cells", {})
    if key in cells:
        return
    idx = len(cells)
    cells[key] = idx
    p.nc.const_aps.register(dtype, float(value), p.t["CONSTS"][0:1, idx:idx + 1])


def emit_consts(p, cs):
    blk = p.blk
    ini = cs["ini"]

    @blk.gpsimd
    def _(g):
        g.memset(p.t["ones1"][:, :], 1.0)
        g.memset(p.t["onesb"][:, :], 1.0)
        for (dt_, val), idx in p.t.get("_const_cells", {}).items():
            g.memset(p.t["CONSTS"][0:1, idx:idx + 1], val)
        if "idmat" in p.t:
            i = g.memset(p.t["idmat"][:, :], 1.0)
            inc(ini, i)
            wt(g, ini)
            g.affine_select(out=p.t["idmat"][:, :], in_=p.t["idmat"][:, :],
                            pattern=[[1, 128]], base=0, channel_multiplier=-1,
                            compare_op=ALU.is_equal, fill=0.0)
        i = g.memset(p.t["onec"][:, :], 1.0)
        inc(ini, i)



def emit_ln(p, cs, resid, xln, ready_tok, banks=None, sqc=None):
    blk = p.blk
    se, ve, pe = cs["se"], cs["ve"], cs["pe"]
    SQC = sqc if sqc is not None else p.t["SQC"]
    rows = p.t["ROWS"]
    if banks is None:
        PSQ = [p.t[f"PSQ{q}"] for q in range(4)]
        banks = []
        for q in range(4):
            banks += [PSQ[q][:, 0:512], PSQ[q][:, 512:1024]]
    F32S = p.t["F32S"]
    rsem, rval = ready_tok
    NJ = T // 512

    @blk.vector
    def _(v):
        wt(v, rsem, rval)
        for k in range(DC):
            i = v.tensor_copy(xln[:, k, :], resid[:, k, :])
        inc(ve, i)
    cast_done = ve.n

    se0 = se.n
    pe0 = pe.n

    @blk.scalar
    def _(s):
        wt(s, ve, cast_done)
        for k in range(DC):
            if k >= 2:
                wt(s, pe, pe0 + (k - 2) + 1)
            i = s.activation(SQC[:, k % 2, :], xln[:, k, :], AF.Square)
            inc(se, i)

    @blk.tensor
    def _(t):
        wt(t, cs["ini"], cs["ini"].n)
        wt(t, ve, cast_done)
        for k in range(DC):
            wt(t, se, se0 + k + 1)
            for j in range(NJ):
                t.matmul(banks[2 * j][0:1, :], p.t["onec"][:, 0:1],
                         xln[:, k, j * 512:(j + 1) * 512],
                         start=(k == 0), stop=(k == DC - 1), skip_group_check=True)
            for j in range(NJ):
                i = t.matmul(banks[2 * j + 1][0:1, :], p.t["onec"][:, 0:1],
                             SQC[:, k % 2, j * 512:(j + 1) * 512],
                             start=(k == 0), stop=(k == DC - 1), skip_group_check=True)
            inc(pe, i)
    stats_done = pe.n

    @blk.scalar
    def _(s):
        wt(s, pe, stats_done)
        for j in range(NJ):
            s.activation(rows[0:1, 512 * j:512 * j + 512], banks[2 * j][0:1, :],
                         AF.Copy, scale=1.0 / D)
        for j in range(NJ):
            i = s.activation(rows[0:1, 2048 + 512 * j:2048 + 512 * j + 512],
                             banks[2 * j + 1][0:1, :], AF.Copy, scale=1.0 / D)
        inc(se, i)
    r01 = se.n

    @blk.vector
    def _(v):
        wt(v, se, r01)
        for hh in range(2):
            sl = slice(1024 * hh, 1024 * hh + 1024)
            i = v.tensor_tensor(F32S[0:1, 0:1024], rows[0:1, sl], rows[0:1, sl], op=ALU.mult)
            inc(ve, i)
            wt(v, ve)
            i = v.tensor_tensor(rows[0:1, 2048 + 1024 * hh:2048 + 1024 * hh + 1024],
                                rows[0:1, 2048 + 1024 * hh:2048 + 1024 * hh + 1024],
                                F32S[0:1, 0:1024], op=ALU.subtract)
            inc(ve, i)
            wt(v, ve)
    var_done = ve.n

    @blk.scalar
    def _(s):
        wt(s, ve, var_done)
        wt(s, cs["ini"], cs["ini"].n)
        i = s.activation(rows[0:1, 2048:2048 + T], rows[0:1, 2048:2048 + T], AF.Ln, bias=LN_EPS)
        inc(se, i)
        wt(s, se)
        i = s.activation(rows[0:1, 2048:2048 + T], rows[0:1, 2048:2048 + T], AF.Exp, scale=-0.5)
        inc(se, i)
    rstd_done = se.n

    @blk.vector
    def _(v):
        wt(v, se, rstd_done)
        i = v.scalar_tensor_tensor(rows[0:1, 0:T], rows[0:1, 0:T], -1.0,
                                   rows[0:1, 2048:2048 + T], op0=ALU.mult, op1=ALU.mult)
        inc(ve, i)
    mr_done = ve.n

    @blk.tensor
    def _(t):
        wt(t, ve, mr_done)
        for j in range(NJ):
            t.matmul(banks[2 * j][:, :], p.t["onesb"][0:1, :],
                     rows[0:1, 2048 + j * 512:2048 + (j + 1) * 512], start=True, stop=True,
                     skip_group_check=True)
        for j in range(NJ):
            i = t.matmul(banks[2 * j + 1][:, :], p.t["onesb"][0:1, :],
                         rows[0:1, j * 512:(j + 1) * 512], start=True, stop=True,
                         skip_group_check=True)
        inc(pe, i)
    bc_done = pe.n

    @blk.vector
    def _(v):
        wt(v, pe, bc_done)
        for k in range(DC):
            for j in range(NJ):
                i = v.tensor_tensor(F32S[:, 0:512], resid[:, k, 512 * j:512 * j + 512],
                                    banks[2 * j][:, :], op=ALU.mult)
                inc(ve, i)
                wt(v, ve)
                i = v.tensor_tensor(xln[:, k, 512 * j:512 * j + 512], F32S[:, 0:512],
                                    banks[2 * j + 1][:, :], op=ALU.add)
                inc(ve, i)
                wt(v, ve)
    return (ve, ve.n)



def emit_l0ffn(p, cs, resid, xln, xln_tok):
    blk = p.blk
    se, ve, pe, po, ld, cc = cs["se"], cs["ve"], cs["pe"], cs["po"], cs["ld"], cs["cc"]
    wA, wB = cs["wA"], cs["wB"]
    WS = p.t["WS"]
    PSQ = [p.t[f"PSQ{q}"] for q in range(4)]
    KA = p.t["KA"]
    B0P = p.t["B0P"]
    F32S = p.t["F32S"]
    HXC = p.t["HXC"]; hvc = p.t["HVC"]
    mu_r = p.t["mu_r_sb"]; mu_k = p.t["mu_k_sb"]
    omu_r = p.t["omu_r_sb"]; omu_k = p.t["omu_k_sb"]
    vsem, vval = xln_tok
    NB = T // 256

    @blk.vector
    def _(v):
        wt(v, vsem, vval)
        i = v.tensor_copy(HXC[:, :], xln[:, :, T - 1:T])
        inc(ve, i)
    lastc_done = ve.n

    @blk.gpsimd
    def _(g):
        wt(g, ve, lastc_done)
        i = g.dma_start(out=p.t["cc_ffn_in"][:, :], in_=HXC[:, :])
        inc(ld, i, 16)
        wt(g, ld)
        i = g.collective_compute(
            "AllGather", ALU.bypass,
            ins=[p.t["cc_ffn_in"][:, :]], outs=[p.t["cc_ffn_out"][:, :]],
            replica_groups=[list(range(NCORE))])
        inc(cc, i)
        wt(g, cc)
        with g.register("hoff") as r:
            g.reg_load(r, p.t["idx_sb"][0:1, 1:2])
            off = g.snap(r)
            i = g.dma_start(out=HXC[:, :], in_=p.t["cc_ffn_out"][bass.ds(off, 128), :])
            inc(ld, i, 16)
        i = g.dma_start(out=hvc[:, 0:1], in_=p.t["hv"][0:1, 0:1].to_broadcast((128, 1)))
        inc(ld, i, 16)
    halo_ld = ld.n
    halo_bc = 0

    @blk.vector
    def _(v):
        wt(v, ld, halo_ld)
        wt(v, vsem, vval)
        for k in range(DC):
            v.tensor_copy(B0P[:, k, 1:256], xln[:, k, 0:255])
            i = v.tensor_scalar(B0P[:, k, 0:1], HXC[:, k:k + 1], hvc[:, 0:1], None, op0=ALU.mult)
        inc(ve, i)
        i = v.tensor_scalar(omu_k[:, :], mu_k[:, :], -1.0, 1.0, op0=ALU.mult, op1=ALU.add)
        inc(ve, i)
        wt(v, ve)
        i = v.tensor_scalar(omu_r[:, :], mu_r[:, :], -1.0, 1.0, op0=ALU.mult, op1=ALU.add)
        inc(ve, i)
        wt(v, ve)
    prep_done = ve.n

    @blk.gpsimd
    def _(g):
        i = g.dma_start(out=WS[:, 0, :, :],
                        in_=p.t["w_k"].rearrange("(c p) o -> p c o", p=128))
        inc(wA, i, 16)
    wk_ld = wA.n

    @blk.vector
    def _(v):
        wt(v, wA, wk_ld)
        wt(v, ve, prep_done)
        for k in range(DC):
            i = v.tensor_scalar(WS[:, 1, k, :], WS[:, 0, k, :], mu_k[:, k:k + 1], None, op0=ALU.mult)
            inc(ve, i)
            wt(v, ve)
            i = v.tensor_scalar(WS[:, 0, k, :], WS[:, 0, k, :], omu_k[:, k:k + 1], None, op0=ALU.mult)
            inc(ve, i)
            wt(v, ve)
    wk_prep = ve.n

    def src_prev(j, k):
        if j == 0:
            return B0P[:, k, 0:256]
        return xln[:, k, 256 * j - 1:256 * j + 255]

    pe0 = pe.n
    se0 = se.n

    @blk.tensor
    def _(t):
        wt(t, ve, wk_prep)
        for j in range(NB):
            if j >= 2:
                wt(t, se, se0 + 3 * (j - 2) + 2)
            for m in range(DC):
                q = PSQ[2 * (j % 2) + m // 4]
                co = 256 * (m % 4)
                for k in range(DC):
                    t.matmul(q[:, co:co + 256],
                             WS[:, 1, k, 128 * m:128 * m + 128],
                             xln[:, k, 256 * j:256 * j + 256],
                             start=(k == 0 and m % 2 == 0), stop=False,
                             skip_group_check=(m % 2 == 1))
                for k in range(DC):
                    i = t.matmul(q[:, co:co + 256],
                                 WS[:, 0, k, 128 * m:128 * m + 128],
                                 src_prev(j, k),
                                 start=False, stop=(k == DC - 1),
                                 skip_group_check=(m % 2 == 1))
            inc(pe, i)

    @blk.scalar
    def _(s):
        for j in range(NB):
            wt(s, pe, pe0 + j + 1)
            i = s.activation(KA[:, 0:4, 256 * j:256 * j + 256], PSQ[2 * (j % 2)][:, :], AF.Relu)
            inc(se, i)
            wt(s, se)
            i = s.activation(KA[:, 4:8, 256 * j:256 * j + 256], PSQ[2 * (j % 2) + 1][:, :], AF.Relu)
            inc(se, i)
            wt(s, se)
            i = s.activation(KA[:, :, 256 * j:256 * j + 256], KA[:, :, 256 * j:256 * j + 256], AF.Square)
            inc(se, i)
    ka_all = se0 + 3 * NB

    for H in range(2):
        wv_tok = wB.n + 16
        wr_tok = wA.n + 16

        @blk.gpsimd
        def _(g):
            if H == 0:
                wt(g, pe, pe0 + NB)
            else:
                wt(g, pe, cs["_ffn_h0_pe_end"])
            i = g.dma_start(out=WS[:, 0, 0:DC, 0:512],
                            in_=p.t["w_v"][:, 512 * H:512 * H + 512].rearrange("(c p) o -> p c o", p=128))
            inc(wB, i, 16)
            i = g.dma_start(out=WS[:, 0, 0:DC, 512:1024],
                            in_=p.t["w_r"][:, 512 * H:512 * H + 512].rearrange("(c p) o -> p c o", p=128))
            inc(wA, i, 16)

        ve_wr0 = ve.n

        @blk.vector
        def _(v):
            wt(v, wA, wr_tok)
            for k in range(DC):
                i = v.tensor_scalar(WS[:, 1, k, 0:512], WS[:, 0, k, 512:1024], mu_r[:, k:k + 1], None, op0=ALU.mult)
                inc(ve, i)
                wt(v, ve)
                i = v.tensor_scalar(WS[:, 1, k, 512:1024], WS[:, 0, k, 512:1024], omu_r[:, k:k + 1], None, op0=ALU.mult)
                inc(ve, i)
                wt(v, ve)
        wr_prep = ve.n

        peH = pe.n
        seH = se.n
        veH = ve.n

        @blk.tensor
        def _(t):
            wt(t, wB, wv_tok)
            wt(t, ve, wr_prep)
            if H == 0:
                wt(t, se, ka_all)
            else:
                wt(t, ve, cs["_ffn_h0_ve_end"])
                wt(t, se, cs["_ffn_h0_se_end"])
            for j in range(NB):
                if j >= 2:
                    wt(t, se, seH + 2 * (j - 2) + 1)
                    wt(t, ve, veH + 8 * (j - 1))
                qkv = PSQ[j % 2]
                for m in range(4):
                    for k in range(DC):
                        i = t.matmul(qkv[:, 256 * m:256 * m + 256],
                                     WS[:, 0, k, 128 * m:128 * m + 128],
                                     KA[:, k, 256 * j:256 * j + 256],
                                     start=(k == 0 and m % 2 == 0), stop=(k == DC - 1),
                                     skip_group_check=(m % 2 == 1))
                inc(pe, i)
                qr = PSQ[2 + j % 2]
                for m in range(4):
                    for k in range(DC):
                        t.matmul(qr[:, 256 * m:256 * m + 256],
                                 WS[:, 1, k, 128 * m:128 * m + 128],
                                 xln[:, k, 256 * j:256 * j + 256],
                                 start=(k == 0 and m % 2 == 0), stop=False,
                                 skip_group_check=(m % 2 == 1))
                    for k in range(DC):
                        i = t.matmul(qr[:, 256 * m:256 * m + 256],
                                     WS[:, 1, k, 512 + 128 * m:512 + 128 * m + 128],
                                     src_prev(j, k),
                                     start=False, stop=(k == DC - 1),
                                     skip_group_check=(m % 2 == 1))
                inc(pe, i)
        cs["_ffn_h0_pe_end"] = pe.n

        @blk.scalar
        def _(s):
            for j in range(NB):
                wt(s, pe, peH + 2 * j + 2)
                if j >= 1:
                    wt(s, ve, veH + 8 * j)
                i = s.activation(F32S[:, 0:1024], PSQ[2 + j % 2][:, :], AF.Sigmoid)
                inc(se, i)
                wt(s, se)
                i = s.nop()
                inc(se, i)

        @blk.vector
        def _(v):
            for j in range(NB):
                wt(v, se, seH + 2 * j + 1)
                wt(v, pe, peH + 2 * j + 1)
                for m in range(4):
                    kc = 4 * H + m
                    i = v.tensor_tensor(F32S[:, 1024 + 256 * m:1024 + 256 * m + 256],
                                        F32S[:, 256 * m:256 * m + 256],
                                        PSQ[j % 2][:, 256 * m:256 * m + 256], op=ALU.mult)
                    inc(ve, i)
                    wt(v, ve)
                    i = v.scalar_tensor_tensor(resid[:, kc, 256 * j:256 * j + 256],
                                               F32S[:, 1024 + 256 * m:1024 + 256 * m + 256], 0.5,
                                               resid[:, kc, 256 * j:256 * j + 256],
                                               op0=ALU.mult, op1=ALU.add)
                    inc(ve, i)
                    wt(v, ve)
        cs["_ffn_h0_ve_end"] = ve.n
        cs["_ffn_h0_se_end"] = se.n
    return (ve, ve.n)



def build_prog_ffn():
    p = Prog("ffnA")
    p.din("resid_in", [D, T], F32)
    p.din("w_r", [D, D], F32)
    p.din("w_k", [D, D], F32)
    p.din("w_v", [D, D], F32)
    p.din("mu_r", [128, DC], F32)
    p.din("mu_k", [128, DC], F32)
    p.din("idx", [1, 32], I32)
    p.din("hv", [1, 1], F32)
    p.dout("resid_out", [D, T], F32)
    p.dtmp("cc_ffn_in", [128, DC], BF16)
    p.dtmp("cc_ffn_out", [NCORE * 128, DC], BF16)

    p.sb("RES", [128, DC, T], F32)
    p.sb("XLN", [128, DC, T], BF16)
    p.sb("KA", [128, DC, T], BF16)
    p.sb("WS", [128, 2, DC, D], BF16)
    p.sb("B0P", [128, DC, 256], BF16)
    p.sb("SQC", [128, 2, 2048], BF16)
    p.sb("F32S", [128, T], F32)
    p.sb("ROWS", [1, 4096], F32)
    p.sb("ones1", [1, 128], BF16)
    p.sb("onesb", [1, 128], F32)
    p.sb("onec", [128, 1], BF16)
    p.sb("CONSTS", [1, 16], F32)
    p.sb("HXC", [128, DC], BF16)
    p.sb("HVC", [128, 1], F32)
    p.sb("mu_r_sb", [128, DC], F32)
    p.sb("mu_k_sb", [128, DC], F32)
    p.sb("omu_r_sb", [128, DC], F32)
    p.sb("omu_k_sb", [128, DC], F32)
    p.sb("idx_sb", [1, 32], I32)
    p.sb("hv_sb", [1, 1], F32)
    for _q in range(4):
        p.ps(f"PSQ{_q}", [128, 1024], F32)

    register_const(p, LN_EPS)
    cs = {k: p.sem(k) for k in ("se", "ve", "pe", "po", "ld", "cc", "wA", "wB", "ini", "ldr", "st")}
    blk = p.block()

    @blk.sync
    def _(y):
        i = y.dma_start(out=p.t["RES"][:, :, :],
                        in_=p.t["resid_in"].rearrange("(c p) t -> p c t", p=128))
        inc(cs["ldr"], i, 16)
        i = y.dma_start(out=p.t["mu_r_sb"][:, :], in_=p.t["mu_r"][:, :])
        inc(cs["ldr"], i, 16)
        i = y.dma_start(out=p.t["mu_k_sb"][:, :], in_=p.t["mu_k"][:, :])
        inc(cs["ldr"], i, 16)
        i = y.dma_start(out=p.t["idx_sb"][:, :], in_=p.t["idx"][:, :])
        inc(cs["ldr"], i, 16)
        i = y.dma_start(out=p.t["hv_sb"][:, :], in_=p.t["hv"][:, :])
        inc(cs["ldr"], i, 16)
    lds = cs["ldr"].n

    emit_consts(p, cs)
    tok = emit_ln(p, cs, p.t["RES"], p.t["XLN"], (cs["ldr"], lds))
    tok = emit_l0ffn(p, cs, p.t["RES"], p.t["XLN"], tok)

    @blk.sync
    def _(y):
        wt(y, tok[0], tok[1])
        i = y.dma_start(out=p.t["resid_out"].rearrange("(c p) t -> p c t", p=128),
                        in_=p.t["RES"][:, :, :])
        inc(cs["st"], i, 16)
        wt(y, cs["st"])
    p.close()
    return p



MOE_NE = NEXP
MOE_NS = 8
MOE_NB = 8
BIGNEG = -1.0e30


def emit_moe(p, cs, resid, xln, xln_tok):
    blk = p.blk
    se, ve, pe, ld = cs["se"], cs["ve"], cs["pe"], cs["ld"]
    wA, wB, ldb = cs["wA"], cs["wB"], cs["ldb"]
    PSQ = [p.t[f"PSQ{q}"] for q in range(4)]
    WG = p.t["WG"]; WU = p.t["WU"]; WD = p.t["WD"]
    CBC = p.t["CBC"]; T1 = p.t["T1"]; HSB = p.t["HSB"]
    RT = p.t["RT"]; CT = p.t["CT"]; CTIN = p.t["CTIN"]
    vsem, vval = xln_tok
    NIT = MOE_NS * MOE_NB
    NITALL = MOE_NE * NIT

    rt_pe0 = pe.n
    rt_ve0 = ve.n
    rt_se0 = se.n
    ct_done = rt_ve0 + 9 * 16

    def rt_pe_logits(i):
        return rt_pe0 + (1 if i == 0 else 2 * i)

    def rt_pe_transpose(i):
        return rt_pe0 + (2 * i + 3 if i < 15 else 32)

    def _emit_transpose(t, i):
        wt(t, ve, rt_ve0 + 9 * i + 8)
        if i >= 4:
            wt(t, ve, rt_ve0 + 9 * (i - 4) + 9)
        psq1_bf = PSQ[1][:, :].bitcast(BF16)
        mm = t.transpose(psq1_bf[0:7, 256 * (i % 4):256 * (i % 4) + 128],
                         CTIN[:, 7 * (i % 2):7 * (i % 2) + 7],
                         p.t["idmat"][:, :])
        inc(pe, mm)

    @blk.tensor
    def _(t):
        wt(t, vsem, vval)
        wt(t, cs["ini"], cs["ini"].n)
        wt(t, cs["ld"], cs["_rw_tok"])
        for i in range(16):
            if i >= 1:
                wt(t, ve, rt_ve0 + 9 * (i - 1) + 6)
            for k in range(DC):
                mm = t.matmul(PSQ[0][:, 0:7], xln[:, k, 128 * i:128 * i + 128],
                              p.t["rw_sb"][:, k, :], start=(k == 0), stop=(k == DC - 1))
            inc(pe, mm)
            if i >= 1:
                _emit_transpose(t, i - 1)
        _emit_transpose(t, 15)

    @blk.vector
    def _(v):
        for i in range(16):
            wt(v, pe, rt_pe0 + i + 1)
            i1 = v.reduce_max(RT[:, 0:1], PSQ[0][:, 0:7], axis=mybir.AxisListType.X)
            inc(ve, i1); wt(v, ve)
            i1 = v.tensor_scalar(RT[:, 8:15], PSQ[0][:, 0:7], RT[:, 0:1], None, op0=ALU.is_equal)
            inc(ve, i1); wt(v, ve)
            i1 = v.scalar_tensor_tensor(RT[:, 16:23], RT[:, 8:15], BIGNEG, PSQ[0][:, 0:7],
                                        op0=ALU.mult, op1=ALU.add)
            inc(ve, i1); wt(v, ve)
            i1 = v.reduce_max(RT[:, 1:2], RT[:, 16:23], axis=mybir.AxisListType.X)
            inc(ve, i1); wt(v, ve)
            i1 = v.tensor_tensor(RT[:, 2:3], RT[:, 0:1], RT[:, 1:2], op=ALU.subtract)
            inc(ve, i1); wt(v, ve)
            i1 = v.tensor_scalar(RT[:, 16:23], RT[:, 16:23], RT[:, 1:2], None, op0=ALU.is_equal)
            inc(ve, i1); wt(v, ve)
            wt(v, se, rt_se0 + 3 * (i + 1))
            i1 = v.tensor_scalar(RT[:, 16:23], RT[:, 16:23], RT[:, 4:5], None, op0=ALU.mult)
            inc(ve, i1); wt(v, ve)
            i1 = v.scalar_tensor_tensor(CTIN[:, 7 * (i % 2):7 * (i % 2) + 7],
                                        RT[:, 8:15], RT[:, 3:4], RT[:, 16:23],
                                        op0=ALU.mult, op1=ALU.add)
            inc(ve, i1); wt(v, ve)
            wt(v, pe, rt_pe_transpose(i))
            psq1_bf = PSQ[1][:, :].bitcast(BF16)
            i1 = v.tensor_copy(CT[0:7, i, :], psq1_bf[0:7, 256 * (i % 4):256 * (i % 4) + 128])
            inc(ve, i1); wt(v, ve)
    assert ve.n == ct_done, (ve.n, ct_done)

    @blk.scalar
    def _(s):
        for i in range(16):
            wt(s, ve, rt_ve0 + 9 * i + 5)
            i1 = s.activation(RT[:, 3:4], RT[:, 2:3], AF.Sigmoid)
            inc(se, i1); wt(s, se)
            i1 = s.activation(RT[:, 3:4], RT[:, 3:4], AF.Copy, scale=0.5)
            inc(se, i1); wt(s, se)
            i1 = s.activation(RT[:, 4:5], RT[:, 3:4], AF.Copy, scale=-1.0, bias=0.5)
            inc(se, i1); wt(s, se)

    @blk.gpsimd
    def _(g):
        wt(g, ve, ct_done)
        i = g.dma_start(out=p.t["c_dram"][0:7, :], in_=CT[0:7, :, :])
        inc(ld, i, 16)
    c_dram_tok = ld.n

    pe0 = pe.n
    se0 = se.n
    ve0 = ve.n

    def pe_gu(git):
        return pe0 + (1 if git == 0 else 3 * git)

    def pe_down(git):
        return pe0 + (3 * git + 4 if git < NITALL - 1 else 3 * NITALL)

    @blk.gpsimd
    def _(g):
        for gs in range(MOE_NE * MOE_NS):
            e, s_ = divmod(gs, MOE_NS)
            b = gs % 2
            ws = wA if b == 0 else wB
            if gs >= 2:
                wt(g, pe, pe_down(8 * (gs - 2) + 7))
            i = g.dma_start(out=WG[:, b, :, :],
                            in_=p.t["moe_g"][e, s_].rearrange("(c p) h -> p c h", p=128))
            inc(ws, i, 16)
            i = g.dma_start(out=WU[:, b, :, :],
                            in_=p.t["moe_u"][e, s_].rearrange("(c p) h -> p c h", p=128))
            inc(ws, i, 16)
            i = g.dma_start(out=WD[:, b, :, :],
                            in_=p.t["moe_d"][e, 512 * s_:512 * s_ + 512, :].rearrange("(c p) d -> p c d", p=128))
            inc(ws, i, 16)

    @blk.sync
    def _(y):
        for e in range(MOE_NE):
            wt(y, ld, c_dram_tok)
            if e >= 2:
                wt(y, ve, ve0 + 3 * NIT * (e - 1))
            srow = p.t["c_dram"][e:e + 1, :] if e < MOE_NE - 1 else p.t["half_dram"][0:1, :]
            i = y.dma_start(out=CBC[:, e % 2, :], in_=srow.to_broadcast((128, T)))
            inc(ldb if e % 2 == 0 else cs["ldb2"], i, 16)

    def emit_gu(t, git):
        e, it = divmod(git, NIT)
        s_, tb = divmod(it, MOE_NB)
        gs = e * MOE_NS + s_
        b = gs % 2
        if it == 0 and tb == 0 and git == 0:
            wt(t, ve, ct_done)
        if tb == 0:
            wt(t, wA if b == 0 else wB, 48 * (gs // 2 + 1))
        if git >= 1:
            wt(t, ve, ve0 + 3 * (git - 1) + 1)
        for hk in range(4):
            for k in range(DC):
                mm = t.matmul(PSQ[0][:, 256 * hk:256 * hk + 256],
                              WG[:, b, k, 128 * hk:128 * hk + 128],
                              xln[:, k, 256 * tb:256 * tb + 256],
                              start=(k == 0 and hk % 2 == 0), stop=(k == 7),
                              skip_group_check=(hk % 2 == 1))
        for hk in range(4):
            for k in range(DC):
                mm = t.matmul(PSQ[1][:, 256 * hk:256 * hk + 256],
                              WU[:, b, k, 128 * hk:128 * hk + 128],
                              xln[:, k, 256 * tb:256 * tb + 256],
                              start=(k == 0 and hk % 2 == 0), stop=(k == 7),
                              skip_group_check=(hk % 2 == 1))
        inc(pe, mm)

    def emit_down(t, git, half):
        e, it = divmod(git, NIT)
        s_, tb = divmod(it, MOE_NB)
        b = (e * MOE_NS + s_) % 2
        if half == 0:
            wt(t, ve, ve0 + 3 * git + 2)
        for m in range(4 * half, 4 * half + 4):
            q = PSQ[2 + m // 4]
            co = 256 * (m % 4)
            for k in range(4):
                mm = t.matmul(q[:, co:co + 256],
                              WD[:, b, k, 128 * m:128 * m + 128],
                              HSB[:, git % 2, 256 * k:256 * k + 256],
                              start=(k == 0 and m % 2 == 0), stop=(k == 3),
                              skip_group_check=(m % 2 == 1))
        inc(pe, mm)

    @blk.tensor
    def _(t):
        emit_gu(t, 0)
        for git in range(1, NITALL):
            emit_down(t, git - 1, 0)
            emit_gu(t, git)
            emit_down(t, git - 1, 1)
        emit_down(t, NITALL - 1, 0)
        emit_down(t, NITALL - 1, 1)

    @blk.scalar
    def _(s):
        for git in range(NITALL):
            wt(s, pe, pe_gu(git))
            if git >= 1:
                wt(s, ve, ve0 + 3 * (git - 1) + 2)
            i = s.activation(T1[:, :], PSQ[0][:, :], AF.Silu)
            inc(se, i)

    @blk.vector
    def _(v):
        for git in range(NITALL):
            e, it = divmod(git, NIT)
            tb = it % MOE_NB
            wt(v, se, se0 + git + 1)
            if it == 0:
                wt(v, ldb if e % 2 == 0 else cs["ldb2"], 16 * (e // 2 + 1))
            i = v.tensor_tensor(T1[:, :], T1[:, :], PSQ[1][:, :], op=ALU.mult)
            inc(ve, i)
            wt(v, ve)
            for hk in range(4):
                i = v.tensor_tensor(HSB[:, git % 2, 256 * hk:256 * hk + 256],
                                    T1[:, 256 * hk:256 * hk + 256],
                                    CBC[:, e % 2, 256 * tb:256 * tb + 256], op=ALU.mult)
            inc(ve, i)
            wt(v, ve)
            wt(v, pe, pe_down(git))
            for m in range(8):
                q = PSQ[2 + m // 4]
                co = 256 * (m % 4)
                i = v.tensor_tensor(resid[:, m, 256 * tb:256 * tb + 256],
                                    resid[:, m, 256 * tb:256 * tb + 256],
                                    q[:, co:co + 256], op=ALU.add)
            inc(ve, i)
            wt(v, ve)
    return (ve, ve.n)


def build_prog_moe():
    p = Prog("moe")
    p.din("resid_in", [D, T], F32)
    p.din("router_w", [128, DC, 7], F32)
    p.din("moe_g", [MOE_NE, MOE_NS, D, 512], F32)
    p.din("moe_u", [MOE_NE, MOE_NS, D, 512], F32)
    p.din("moe_d", [MOE_NE, HID, D], F32)
    p.din("half_dram", [1, T], BF16)
    p.dout("resid_out", [D, T], F32)
    p.dtmp("c_dram", [7, T], BF16)

    p.sb("RES", [128, DC, T], F32)
    p.sb("XLN", [128, DC, T], BF16)
    p.sb("WG", [128, 2, DC, 512], BF16)
    p.sb("WU", [128, 2, DC, 512], BF16)
    p.sb("WD", [128, 2, 4, D], BF16)
    p.sb("CBC", [128, 2, T], BF16)
    p.sb("T1", [128, 1024], F32)
    p.sb("HSB", [128, 2, 1024], BF16)
    p.sb("RT", [128, 24], F32)
    p.sb("CT", [8, 16, 128], BF16)
    p.sb("CTIN", [128, 14], BF16)
    p.sb("rw_sb", [128, DC, 7], BF16)
    p.sb("idmat", [128, 128], BF16)
    p.sb("SQC", [128, 2, 2048], BF16)
    p.sb("F32S", [128, T], F32)
    p.sb("ROWS", [1, 4096], F32)
    p.sb("ones1", [1, 128], BF16)
    p.sb("onesb", [1, 128], F32)
    p.sb("onec", [128, 1], BF16)
    p.sb("CONSTS", [1, 16], F32)
    for _q in range(4):
        p.ps(f"PSQ{_q}", [128, 1024], F32)

    register_const(p, LN_EPS)
    cs = {k: p.sem(k) for k in ("se", "ve", "pe", "po", "ld", "cc", "wA", "wB",
                                "ini", "ldr", "st", "ldb", "ldb2")}
    blk = p.block()

    @blk.sync
    def _(y):
        i = y.dma_start(out=p.t["RES"][:, :, :],
                        in_=p.t["resid_in"].rearrange("(c p) t -> p c t", p=128))
        inc(cs["ldr"], i, 16)
    cs["_ldr_all"] = cs["ldr"].n

    @blk.gpsimd
    def _(g):
        i = g.dma_start(out=p.t["rw_sb"][:, :, :], in_=p.t["router_w"][:, :, :])
        inc(cs["ld"], i, 16)
    cs["_rw_tok"] = cs["ld"].n

    emit_consts(p, cs)
    tok = emit_ln(p, cs, p.t["RES"], p.t["XLN"], (cs["ldr"], cs["_ldr_all"]))
    tok = emit_moe(p, cs, p.t["RES"], p.t["XLN"], tok)

    @blk.sync
    def _(y):
        wt(y, tok[0], tok[1])
        i = y.dma_start(out=p.t["resid_out"].rearrange("(c p) t -> p c t", p=128),
                        in_=p.t["RES"][:, :, :])
        inc(cs["st"], i, 16)
        wt(y, cs["st"])
    p.close()
    return p


def stage_moe_core(inp, c, resid):
    f32 = np.float32
    wg = np.asarray(inp["L1_ffn_w_gate"], f32)
    wu = np.asarray(inp["L1_ffn_w_up"], f32)
    wd = np.asarray(inp["L1_ffn_w_down"], f32)
    swg = np.asarray(inp["L1_ffn_sw_gate"], f32)
    swu = np.asarray(inp["L1_ffn_sw_up"], f32)
    swd = np.asarray(inp["L1_ffn_sw_down"], f32)

    def slice_gu(w):
        return np.ascontiguousarray(w.reshape(D, MOE_NS, 512).transpose(1, 0, 2))
    g_all = np.stack([slice_gu(wg[e]) for e in range(7)] + [slice_gu(swg[0])])
    u_all = np.stack([slice_gu(wu[e]) for e in range(7)] + [slice_gu(swu[0])])
    d_all = np.ascontiguousarray(np.stack(list(wd) + [swd[0]]))
    rw = np.ascontiguousarray(np.asarray(inp["L1_ffn_router_w"], f32).T.reshape(DC, 128, 7).transpose(1, 0, 2))
    d = {
        "resid_in": np.ascontiguousarray(resid[T * c:T * (c + 1)].T),
        "router_w": rw,
        "moe_g": g_all,
        "moe_u": u_all,
        "moe_d": d_all,
        "half_dram": np.full((1, T), 0.5, BF),
    }
    return d



NJC = 5632


def qblocks_of_core(c):
    return [c, 15 - c, 16 + c, 31 - c]


def jtiles_needed(b):
    jmax = (512 * (b + 1) - 1) // 3
    return (jmax + 2 + 127) // 128


SLOT_TILES = [max(jtiles_needed(qblocks_of_core(c)[s]) for c in range(NCORE))
              for s in range(4)]


def first_masked_tile(b):
    t = 0
    while 512 * b >= 3 * (128 * (t + 1) - 2) and t + 1 < jtiles_needed(b):
        t += 1
    return t


SLOT_MASK0 = [min(first_masked_tile(qblocks_of_core(c)[s]) for c in range(NCORE))
              for s in range(4)]
SWEEPS = [[0, 1, 2], [3, 4, 5], [6, 7]]
NMASK = sum(SLOT_TILES[s] - SLOT_MASK0[s] for s in range(4))
CONVB = [(512 * B, min(512, 5462 - 512 * B)) for B in range(11)]
ATT_PHASE = 9


def _win_segs(t0, length):
    segs = []
    g = max(t0, 0)
    while g < t0 + length:
        r = g // 2048
        hi = min(t0 + length, 2048 * (r + 1))
        segs.append((r, g % 2048, hi - g, g - t0))
        g = hi
    return segs


def emit_l0attn(p, cs, resid, xln, xln_tok):
    blk = p.blk
    se, ve, pe, ld, cc = cs["se"], cs["ve"], cs["pe"], cs["ld"], cs["cc"]
    ldy = cs["ldy"]
    A1 = p.t["A1"]
    A2 = p.t["A2"]
    WS = p.t["WS2"]
    KVB = p.t["KVB"]
    WCONV = p.t["wconv_sb"]
    F32S = p.t["F32S"]
    OB = [p.t[f"OB{i}"] for i in range(6)]
    SCB = p.t["SCB"]
    MXC = p.t["MXC"]
    MXB = p.t["MXB"]
    NEGB = p.t["NEGB"]
    VB = p.t["VB"]
    KSL = p.t["KSL"]
    VSL = p.t["VSL"]
    BCBV = p.t["BCBV"]
    vsem, vval = xln_tok
    KWIN = A1[:, :, 0:1544]
    QS = A2[:, :, 0:512]
    PT0 = A2[:, 0:2, 512:1536]
    banks = [SCB[:, 0:512], SCB[:, 512:1024]] + [OB[i][:, :] for i in range(6)]

    pe_p0 = pe.n
    se_p0 = se.n
    ldy_p0 = ldy.n

    @blk.gpsimd
    def _(g):
        for mi in range(3):
            if mi > 0:
                wt(g, pe, pe_p0 + 4 * mi)
            i = g.dma_start(out=WS[:, :, :],
                            in_=p.t["w_qkv0"][mi].rearrange("(c p) o -> p c o", p=128))
            inc(cs["wA"], i, 16)

    @blk.tensor
    def _(t):
        wt(t, vsem, vval)
        for mi in range(3):
            wt(t, cs["wA"], 16 * (mi + 1))
            for j in range(4):
                if not (mi == 0 and j == 0):
                    wt(t, se, se_p0 + 8 * (4 * mi + j))
                for m in range(DC):
                    for k in range(DC):
                        mm = t.matmul(banks[m][:, :], WS[:, k, 128 * m:128 * m + 128],
                                      xln[:, k, 512 * j:512 * j + 512],
                                      start=(k == 0), stop=(k == DC - 1),
                                      skip_group_check=True)
                inc(pe, mm)

    @blk.scalar
    def _(s):
        for mi in range(3):
            for j in range(4):
                wt(s, pe, pe_p0 + 4 * mi + j + 1)
                if mi > 0:
                    if j >= 2:
                        mip, jp = mi, j - 2
                    elif mi == 2:
                        mip, jp = 1, j + 2
                    else:
                        mip, jp = 0, 0
                    if mip > 0:
                        order = 2 * (mip - 1) + jp // 2 + 1
                        wt(s, ldy if jp % 2 == 0 else cs["ldyB"],
                           ldy_p0 + 16 * order)
                for m in range(DC):
                    if mi == 0:
                        i = s.activation(A2[:, m, 512 * j:512 * j + 512], banks[m][:, :], AF.Copy)
                    else:
                        i = s.activation(KVB[:, j % 2, m, :], banks[m][:, :], AF.Copy)
                    inc(se, i)
                    wt(s, se)

    @blk.sync
    def _(y):
        for mi in range(1, 3):
            cio = p.t["cc_k_in"] if mi == 1 else p.t["cc_v_in"]
            for j in range(4):
                wt(y, se, se_p0 + 8 * (4 * mi + j + 1))
                i = y.dma_start(out=cio[:, 512 * j:512 * j + 512]
                                .rearrange("(c p) t -> p c t", p=128),
                                in_=KVB[:, j % 2, :, :])
                inc(ldy if j % 2 == 0 else cs["ldyB"], i, 16)
    kv_in_tokA = ldy.n
    kv_in_tokB = cs["ldyB"].n

    if ATT_PHASE < 2:
        return (se, se_p0 + 96)

    @blk.vector
    def _(v):
        wt(v, se, se_p0 + 8 * 4)
        for m in range(DC):
            i = v.reduce_max(MXC[:, m:m + 1], A2[:, m, :], axis=mybir.AxisListType.X,
                             apply_absolute_value=True)
            inc(ve, i); wt(v, ve)
        i = v.tensor_copy(MXB[:, 0:8], MXC[:, 0:8])
        inc(ve, i); wt(v, ve)
    qmx_ve = ve.n

    @blk.gpsimd
    def _(g):
        wt(g, se, se_p0 + 8 * 4)
        i = g.dma_start(out=p.t["cc_q_in"].rearrange("(c p) t -> p c t", p=128),
                        in_=A2[:, :, :])
        inc(ld, i, 16)
    q_in_tok = ld.n

    pe_m0 = pe.n

    @blk.tensor
    def _(t):
        wt(t, ve, qmx_ve)
        wt(t, se, se_p0 + 8 * 12)
        scb_bf = SCB[:, :].bitcast(BF16)
        mm = t.transpose(scb_bf[0:8, 0:128], MXB[:, 0:8], p.t["idmat"][:, :])
        inc(pe, mm)

    @blk.vector
    def _(v):
        wt(v, pe, pe_m0 + 1)
        scb_bf = SCB[:, :].bitcast(BF16)
        i = v.reduce_max(p.t["QPM"][0:8, 0:1],
                         scb_bf[0:8, 0:128].rearrange("a (b c) -> a b c", b=2),
                         axis=mybir.AxisListType.XY)
        inc(ve, i); wt(v, ve)
    qpm_ve = ve.n

    @blk.gpsimd
    def _(g):
        wt(g, ld, q_in_tok)
        wt(g, ldy, kv_in_tokA)
        wt(g, cs["ldyB"], kv_in_tokB)
        for nm in ("q", "k", "v"):
            onm = "cc_vout" if nm == "v" else f"cc_{nm}_out"
            i = g.collective_compute(
                "AllGather", ALU.bypass,
                ins=[p.t[f"cc_{nm}_in"][:, :]], outs=[p.t[onm][:, :]],
                replica_groups=[list(range(NCORE))])
            inc(cc, i)
        wt(g, ve, qpm_ve)
        i = g.dma_start(out=p.t["cc_m_in"][:, :], in_=p.t["QPM"][0:8, 0:1])
        inc(ld, i, 16)
        wt(g, ld)
        i = g.collective_compute(
            "AllReduce", ALU.max,
            ins=[p.t["cc_m_in"][:, :]], outs=[p.t["cc_m_out"][:, :]],
            replica_groups=[list(range(NCORE))])
        inc(cc, i)
        wt(g, cc)
        i = g.dma_start(out=p.t["QPM"][0:8, 0:1], in_=p.t["cc_m_out"][:, :])
        inc(ld, i, 16)
        i = g.dma_start(out=p.t["kj_dram"][:, 0:1].rearrange("(c p) w -> p c w", p=128),
                        in_=p.t["nullk_sb"][:, :].unsqueeze(2),
                        allow_slow_non_contiguous=True)
        inc(ld, i, 16)
        i = g.memset(KVB[:, 0, :, :], 0.0)
        inc(cs["po"], i)
        wt(g, cs["po"])
        kvb_flat = KVB[:, 0, :, :].rearrange("p a b -> p (a b)")
        i = g.dma_start(out=p.t["kj_dram"][:, 5463:NJC].rearrange("(c p) w -> p c w", p=128),
                        in_=KVB[:, 0, :, 0:169])
        inc(ld, i, 16)
        i = g.dma_start(out=p.t["vj_dram"][5463:5591, :], in_=kvb_flat[:, 0:1040])
        inc(ld, i, 16)
        i = g.dma_start(out=p.t["vj_dram"][5591:NJC, :], in_=kvb_flat[0:41, 0:1040])
        inc(ld, i, 16)
        i = g.dma_start(out=p.t["vj_dram"][0:1, :], in_=p.t["nvr_sb"][0:1, :])
        inc(ld, i, 16)
    ag1_cc = cc.n
    fills_tok = ld.n

    if ATT_PHASE < 3:
        return (ld, fills_tok)

    pe_b0 = pe.n

    @blk.tensor
    def _(t):
        wt(t, cs["ldr"], cs["_ldr_all"])
        wt(t, pe, pe_m0 + 1)
        wt(t, ve, qpm_ve)
        for jj in range(2):
            mm = t.matmul(SCB[:, 512 * jj:512 * jj + 512], p.t["onesb"][0:1, :],
                          p.t["bcvrow_sb"][0:1, 512 * jj:512 * jj + 512],
                          start=True, stop=True, skip_group_check=True)
        inc(pe, mm)

    @blk.vector
    def _(v):
        wt(v, pe, pe_b0 + 1)
        i = v.tensor_copy(BCBV[:, :], SCB[:, :])
        inc(ve, i); wt(v, ve)
        i = v.memset(MXC[:, 8:16], 0.0)
        inc(ve, i); wt(v, ve)
        for h in range(16):
            for par in range(2):
                i = v.memset(VB[:, par, 65 * h + 64:65 * h + 65], 1.0)
        inc(ve, i); wt(v, ve)
    bcbv_ve = ve.n

    pe_c0 = pe.n
    VT = [min(4, (CONVB[B][1] + 127) // 128) for B in range(11)]
    vconv_end = pe_c0 + 88 + sum(VT)

    win_ldy = {}
    run = cs["ldw"].n
    for which in range(2):
        for B in range(11):
            run += 16 * len(_win_segs(3 * CONVB[B][0] - 2, 3 * CONVB[B][1]))
            win_ldy[(which, B)] = run

    def win_gate_pe(which, B):
        g = 11 * which + B
        if g == 0:
            return 0
        gp = g - 1
        if gp < 11:
            return pe_c0 + 8 * (gp + 1)
        return pe_c0 + 88 + sum(VT[:gp - 11 + 1])

    if ATT_PHASE < 4:
        return (ve, bcbv_ve)

    @blk.sync
    def _(y):
        for which in range(2 if ATT_PHASE >= 5 else 1):
            gsrc = p.t["cc_k_out"] if which == 0 else p.t["cc_vout"]
            for B in range(11):
                bk0, nb = CONVB[B]
                t0 = 3 * bk0 - 2
                wt(y, cc, ag1_cc)
                gate = win_gate_pe(which, B)
                if gate:
                    wt(y, pe, gate)
                else:
                    wt(y, ve, bcbv_ve)
                for (r, col, n, off) in _win_segs(t0, 3 * nb):
                    i = y.dma_start(
                        out=KWIN[:, :, off:off + n],
                        in_=gsrc[1024 * r:1024 * (r + 1), col:col + n]
                        .rearrange("(c p) t -> p c t", p=128))
                    inc(cs["ldw"], i, 16)

    @blk.vector
    def _(v):
        wt(v, ve, bcbv_ve)
        i = v.memset(KWIN[:, :, 0:2], 0.0)
        inc(ve, i); wt(v, ve)
    b0z_ve = ve.n
    kc_ve0 = ve.n
    def kc_ve(B, hp):
        return kc_ve0 + 3 * (8 * B + hp + 1)

    @blk.tensor
    def _(t):
        for B in range(11):
            bk0, nb = CONVB[B]
            wt(t, cs["ldw"], win_ldy[(0, B)])
            if B == 0:
                wt(t, ve, b0z_ve)
            for hp in range(8):
                g = 8 * B + hp
                half = SCB[:, 512 * (g % 2):512 * (g % 2) + 512]
                if g >= 2:
                    wt(t, ve, kc_ve0 + 3 * (g - 1))
                for hh in range(2):
                    base = 64 * hh
                    for tau in range(3):
                        mm = t.matmul(
                            half[base:base + 64, 0:nb],
                            WCONV[base:base + 64, tau, hp, base:base + 64],
                            KWIN[base:base + 64, hp, tau:tau + 3 * nb:3],
                            start=(tau == 0 and hh == 0), stop=(tau == 2),
                            skip_group_check=(hh == 1),
                            tile_position=(base, base))
                inc(pe, mm)

    @blk.vector
    def _(v):
        wt(v, ld, fills_tok)
        for B in range(11):
            bk0, nb = CONVB[B]
            for hp in range(8):
                g = 8 * B + hp
                wt(v, pe, pe_c0 + g + 1)
                if g >= 2:
                    wt(v, cs["lkA"] if g % 2 == 0 else cs["lkB"],
                       16 * ((g - 2) // 2 + 1))
                half = SCB[:, 512 * (g % 2):512 * (g % 2) + 512]
                i = v.tensor_scalar(KVB[:, g % 2, hp, 0:nb], half[:, 0:nb],
                                    p.t["bconv_sb"][:, hp:hp + 1], None, op0=ALU.add)
                inc(ve, i); wt(v, ve)
                i = v.reduce_max(MXC[:, 8 + hp:9 + hp] if False else F32S[:, 0:1],
                                 KVB[:, g % 2, hp, 0:nb], axis=mybir.AxisListType.X,
                                 apply_absolute_value=True)
                inc(ve, i); wt(v, ve)
                i = v.tensor_tensor(MXC[:, 8 + hp:9 + hp], MXC[:, 8 + hp:9 + hp],
                                    F32S[:, 0:1], op=ALU.max)
                inc(ve, i); wt(v, ve)

    @blk.gpsimd
    def _(g_):
        for B in range(11):
            bk0, nb = CONVB[B]
            for hp in range(8):
                g = 8 * B + hp
                wt(g_, ve, kc_ve0 + 3 * g + 1)
                i = g_.dma_start(
                    out=p.t["kj_dram"][128 * hp:128 * (hp + 1), 1 + bk0:1 + bk0 + nb],
                    in_=KVB[:, g % 2, hp, 0:nb])
                inc(cs["lkA"] if g % 2 == 0 else cs["lkB"], i, 16)
    kst_endA = cs["lkA"].n
    kst_endB = cs["lkB"].n

    if ATT_PHASE < 5:
        return (ve, kc_ve0 + 3 * 88)

    NTV = sum(VT)
    vc_pe0 = pe.n
    vc_ve0 = kc_ve0 + 3 * 88

    @blk.tensor
    def _(t):
        n = 0
        for B in range(11):
            bk0, nb = CONVB[B]
            wt(t, cs["ldw"], win_ldy[(1, B)])
            for tile in range(VT[B]):
                jt0 = 128 * tile
                jn = min(128, nb - jt0)
                if n == 0:
                    wt(t, ve, kc_ve0 + 3 * 88)
                if n >= 1:
                    wt(t, ve, vc_ve0 + n)
                w0 = 3 * jt0
                for hp in range(8):
                    for tau in range(3):
                        mm = t.matmul(
                            SCB[0:jn, 128 * hp:128 * hp + 128],
                            KWIN[0:128, hp, tau + w0:tau + w0 + 3 * jn:3],
                            WCONV[0:128, tau, hp, :],
                            start=(tau == 0), stop=(tau == 2),
                            skip_group_check=(hp % 4 != 0))
                inc(pe, mm)
                n += 1

    @blk.vector
    def _(v):
        n = 0
        for B in range(11):
            bk0, nb = CONVB[B]
            for tile in range(VT[B]):
                jn = min(128, nb - 128 * tile)
                wt(v, pe, vc_pe0 + n + 1)
                if n >= 2:
                    wt(v, cs["lvA"] if n % 2 == 0 else cs["lvB"],
                       16 * ((n - 2) // 2 + 1))
                i = v.tensor_tensor(
                    VB[0:jn, n % 2, :].rearrange("p (h c) -> p h c", h=16)[:, :, 0:64],
                    SCB[0:jn, :].rearrange("p (h c) -> p h c", h=16),
                    BCBV[0:jn, :].rearrange("p (h c) -> p h c", h=16),
                    op=ALU.add)
                inc(ve, i); wt(v, ve)
                n += 1

    @blk.gpsimd
    def _(g_):
        n = 0
        for B in range(11):
            bk0, nb = CONVB[B]
            for tile in range(VT[B]):
                jt0 = 128 * tile
                jn = min(128, nb - jt0)
                wt(g_, ve, vc_ve0 + n + 1)
                i = g_.dma_start(out=p.t["vj_dram"][1 + bk0 + jt0:1 + bk0 + jt0 + jn, :],
                                 in_=VB[0:jn, n % 2, :])
                inc(cs["lvA"] if n % 2 == 0 else cs["lvB"], i, 16)
                n += 1
    vst_endA = cs["lvA"].n
    vst_endB = cs["lvB"].n

    if ATT_PHASE < 6:
        return (ve, vc_ve0 + NTV)

    kconv_ve_end = kc_ve0 + 3 * 88

    @blk.vector
    def _(v):
        wt(v, ve, kconv_ve_end)
        i = v.tensor_copy(MXB[:, 8:16], MXC[:, 8:16])
        inc(ve, i); wt(v, ve)
    kmb_ve = ve.n

    pe_k0 = pe.n

    @blk.tensor
    def _(t):
        wt(t, ve, kmb_ve)
        wt(t, pe, vc_pe0 + NTV)
        wt(t, ve, vc_ve0 + NTV)
        scb_bf = SCB[:, :].bitcast(BF16)
        mm = t.transpose(scb_bf[0:8, 0:128], MXB[:, 8:16], p.t["idmat"][:, :])
        inc(pe, mm)

    @blk.vector
    def _(v):
        wt(v, pe, pe_k0 + 1)
        scb_bf = SCB[:, :].bitcast(BF16)
        i = v.reduce_max(p.t["QPM"][0:8, 1:2],
                         scb_bf[0:8, 0:128].rearrange("a (b c) -> a b c", b=2),
                         axis=mybir.AxisListType.XY)
        inc(ve, i); wt(v, ve)
        wt(v, ld, fills_tok)
        i = v.scalar_tensor_tensor(p.t["QPM"][0:8, 2:3], p.t["QPM"][0:8, 0:1], -2.0,
                                   p.t["QPM"][0:8, 1:2], op0=ALU.mult, op1=ALU.mult)
        inc(ve, i); wt(v, ve)
        i = v.tensor_copy(p.t["QPMB"][0:8, 0:1], p.t["QPM"][0:8, 2:3])
        inc(ve, i); wt(v, ve)
    negv_ve = ve.n

    pe_k1 = pe.n

    @blk.tensor
    def _(t):
        wt(t, ve, negv_ve)
        scb_bf = SCB[:, :].bitcast(BF16)
        mm = t.transpose(scb_bf[0:1, 0:8], p.t["QPMB"][0:8, 0:1], p.t["idmat"][0:8, 0:8])
        inc(pe, mm)

    @blk.vector
    def _(v):
        wt(v, pe, pe_k1 + 1)
        scb_bf = SCB[:, :].bitcast(BF16)
        i = v.tensor_copy(p.t["NEGR"][0:1, 0:8], scb_bf[0:1, 0:8])
        inc(ve, i); wt(v, ve)
    negr_ve = ve.n

    pe_k2 = pe.n

    @blk.tensor
    def _(t):
        wt(t, ve, negr_ve)
        mm = t.matmul(SCB[:, 0:8], p.t["ones1"][0:1, :], p.t["NEGR"][0:1, 0:8],
                      start=True, stop=True, skip_group_check=True)
        inc(pe, mm)

    @blk.vector
    def _(v):
        wt(v, pe, pe_k2 + 1)
        i = v.tensor_copy(NEGB[:, 0:8], SCB[:, 0:8])
        inc(ve, i); wt(v, ve)
    negb_ve = ve.n

    if ATT_PHASE < 7:
        return (ve, negb_ve)

    mtab = {}
    mi = 0
    for s in range(4):
        for tj in range(SLOT_MASK0[s], SLOT_TILES[s]):
            mtab[(s, tj)] = mi
            mi += 1

    att_pe0 = pe.n
    att_se0 = se.n
    att_po0 = cs["po"].n
    att_ldq0 = cs["ldq"].n
    att_ve0 = ve.n
    div_recip = {}
    div_copy = {}
    div_ve = {}
    _vn = att_ve0
    for _s in range(4):
        for _w, _pairs in enumerate(SWEEPS):
            for _hi in range(2 * len(_pairs)):
                div_recip[(_s, _w, _hi)] = _vn + 1
                div_copy[(_s, _w, _hi)] = _vn + 2
                _vn += 3
            div_ve[(_s, _w)] = _vn
    att_div_end_pred = _vn

    pe_pos = {}
    se_pos = {}
    po_pos = {}
    ldy_pos = {}
    pe_n = att_pe0
    se_n = att_se0
    po_n = att_po0
    ls_n = [cs["lsA"].n, cs["lsB"].n]
    for s in range(4):
        for w, pairs in enumerate(SWEEPS):
            P = len(pairs)
            for tj in range(SLOT_TILES[s]):
                ls_n[tj % 2] += 16 * (P + 1)
                ldy_pos[(s, w, tj)] = ls_n[tj % 2]
                for ci in range(P):
                    pe_n += 1
                    pe_pos[(s, w, tj, ci, 0)] = pe_n
                    se_n += 1
                    se_pos[(s, w, tj, ci)] = se_n
                    if tj >= SLOT_MASK0[s]:
                        po_n += 2
                        po_pos[(s, w, tj, ci)] = po_n
                    pe_n += 1
                    pe_pos[(s, w, tj, ci, 1)] = pe_n
            for hi in range(2 * P):
                pe_n += 1
                pe_pos[(s, w, hi, -1, 2)] = pe_n
    att_pe_end = pe_n

    def prev_q(s, w, tj, ci):
        P = len(SWEEPS[w])
        q = tj * P + ci
        if q == 0:
            return None
        return ((q - 1) // P, (q - 1) % P)

    @blk.sync
    def _(y):
        for s in range(4):
            for w, pairs in enumerate(SWEEPS):
                P = len(pairs)
                for tj in range(SLOT_TILES[s]):
                    if tj >= 2:
                        wt(y, pe, pe_pos[(s, w, tj - 2, P - 1, 1)])
                    elif (s, w) != (0, 0):
                        pw = (s, w - 1) if w > 0 else (s - 1, len(SWEEPS) - 1)
                        lp = len(SWEEPS[pw[1]]) - 1
                        wt(y, pe, pe_pos[(pw[0], pw[1], SLOT_TILES[pw[0]] - 1, lp, 1)])
                    else:
                        wt(y, cs["lvA"], vst_endA)
                        wt(y, cs["lvB"], vst_endB)
                        wt(y, cs["lkA"], kst_endA)
                        wt(y, cs["lkB"], kst_endB)
                        wt(y, ve, negb_ve)
                    lsem = cs["lsA"] if tj % 2 == 0 else cs["lsB"]
                    for ci, hp in enumerate(pairs):
                        i = y.dma_start(out=KSL[:, tj % 2, ci, :],
                                        in_=p.t["kj_dram"][128 * hp:128 * (hp + 1),
                                                           128 * tj:128 * tj + 128])
                        inc(lsem, i, 16)
                    c0 = 65 * 2 * pairs[0]
                    cn = 65 * 2 * P
                    i = y.dma_start(out=VSL[:, tj % 2, 0:cn],
                                    in_=p.t["vj_dram"][128 * tj:128 * tj + 128, c0:c0 + cn])
                    inc(lsem, i, 16)
                    assert lsem.n == ldy_pos[(s, w, tj)]

    ldm_tab = {}
    _n = 0
    for s in range(4):
        for w in range(len(SWEEPS)):
            for tj in range(SLOT_MASK0[s], SLOT_TILES[s]):
                _n += 16
                ldm_tab[(s, w, tj)] = _n

    @blk.tensor
    def _(t):
        for s in range(4):
            wt(t, cs["ldq"], att_ldq0 + 16 * (s + 1))
            for w, pairs in enumerate(SWEEPS):
                P = len(pairs)
                for tj in range(SLOT_TILES[s]):
                    wt(t, cs["lsA"] if tj % 2 == 0 else cs["lsB"], ldy_pos[(s, w, tj)])
                    for ci, hp in enumerate(pairs):
                        pq = prev_q(s, w, tj, ci)
                        if pq is not None:
                            wt(t, se, se_pos[(s, w, pq[0], pq[1])])
                        elif (s, w) != (0, 0):
                            pw = (s, w - 1) if w > 0 else (s - 1, len(SWEEPS) - 1)
                            wt(t, se, se_pos[(pw[0], pw[1], SLOT_TILES[pw[0]] - 1,
                                              len(SWEEPS[pw[1]]) - 1)])
                            wt(t, ve, div_ve[pw])
                        for e in range(2):
                            for mh in (0, 64):
                                mm = t.matmul(SCB[mh:mh + 64, 512 * e:512 * e + 512],
                                              KSL[64 * e:64 * e + 64, tj % 2, ci,
                                                  mh:mh + 64],
                                              QS[64 * e:64 * e + 64, hp, :],
                                              start=True, stop=True,
                                              skip_group_check=True,
                                              tile_position=(64 * e, mh))
                        inc(pe, mm)
                        wt(t, se, se_pos[(s, w, tj, ci)])
                        if tj >= SLOT_MASK0[s]:
                            wt(t, cs["po"], po_pos[(s, w, tj, ci)])
                        if tj == 0 and (s, w) != (0, 0):
                            dv = div_ve[(s, w - 1) if w > 0 else (s - 1, len(SWEEPS) - 1)]
                            wt(t, ve, dv)
                        pt = PT0[:, (tj * P + ci) % 2, :]
                        for e in range(2):
                            obi = 2 * ci + e
                            mm = t.matmul(OB[obi][0:65, :],
                                          VSL[:, tj % 2, 65 * (2 * ci + e):65 * (2 * ci + e) + 65],
                                          pt[:, 512 * e:512 * e + 512],
                                          start=(tj == 0), stop=(tj == SLOT_TILES[s] - 1),
                                          skip_group_check=True)
                        inc(pe, mm)
                for ci in range(P):
                    for e in range(2):
                        hi = 2 * ci + e
                        h = 2 * pairs[ci] + e
                        obi = hi
                        wt(t, ve, div_recip[(s, w, hi)])
                        mm = t.matmul(SCB[0:64, 0:512], p.t["ONER"][64:65, 0:64],
                                      F32S[64:65, 0:512],
                                      start=True, stop=True, skip_group_check=True,
                                      tile_position=(64, 0))
                        inc(pe, mm)

    @blk.scalar
    def _(s_):
        for s in range(4):
            for w, pairs in enumerate(SWEEPS):
                P = len(pairs)
                for tj in range(SLOT_TILES[s]):
                    for ci, hp in enumerate(pairs):
                        q = tj * P + ci
                        wt(s_, pe, pe_pos[(s, w, tj, ci, 0)])
                        if q >= 2:
                            wt(s_, pe, pe_pos[(s, w, (q - 2) // P, (q - 2) % P, 1)])
                        pt = PT0[:, q % 2, :]
                        s_.activation(pt[:, 0:512], SCB[:, 0:512], AF.Exp,
                                      scale=1.0 / 32.0, bias=NEGB[:, hp:hp + 1])
                        i = s_.activation(pt[:, 512:1024], SCB[:, 512:1024], AF.Exp,
                                          scale=1.0 / 32.0, bias=NEGB[:, hp:hp + 1])
                        inc(se, i)

    @blk.gpsimd
    def _(g):
        wt(g, cc, ag1_cc)
        for s in range(4):
            if s > 0:
                lw = len(SWEEPS) - 1
                lt = SLOT_TILES[s - 1] - 1
                lc = len(SWEEPS[lw]) - 1
                wt(g, pe, pe_pos[(s - 1, lw, lt, lc, 0)])
            with g.register(f"qro{s}") as r1, g.register(f"qco{s}") as r2:
                g.reg_load(r1, p.t["idx_sb"][0:1, 8 + s:9 + s])
                g.reg_load(r2, p.t["idx_sb"][0:1, 12 + s:13 + s])
                o1 = g.snap(r1)
                o2 = g.snap(r2)
                i = g.dma_start(out=QS[:, :, :],
                                in_=p.t["cc_q_out"][bass.ds(o1, 1024), bass.ds(o2, 512)]
                                .rearrange("(c p) t -> p c t", p=128))
                inc(cs["ldq"], i, 16)
            for w, pairs in enumerate(SWEEPS):
                P = len(pairs)
                for tj in range(SLOT_MASK0[s], SLOT_TILES[s]):
                    pv2 = cs.setdefault("_msk_last", {}).get(tj % 2)
                    if pv2:
                        wt(g, cs["po"], pv2)
                    i = g.dma_start(out=p.t["MSKB"][:, tj % 2, :],
                                    in_=p.t["l0_masks"][mtab[(s, tj)], :, :])
                    inc(cs["ldm"], i, 16)
                    wt(g, cs["ldm"], cs["ldm"].n)
                    for ci, hp in enumerate(pairs):
                        q = tj * P + ci
                        wt(g, se, se_pos[(s, w, tj, ci)])
                        pt = PT0[:, q % 2, :]
                        i = g.tensor_tensor(pt[:, 0:512], pt[:, 0:512],
                                            p.t["MSKB"][:, tj % 2, :], op=ALU.mult)
                        inc(cs["po"], i)
                        i = g.tensor_tensor(pt[:, 512:1024], pt[:, 512:1024],
                                            p.t["MSKB"][:, tj % 2, :], op=ALU.mult)
                        inc(cs["po"], i)
                        assert cs["po"].n == po_pos[(s, w, tj, ci)]
                        cs["_msk_last"][tj % 2] = cs["po"].n

    @blk.vector
    def _(v):
        for s in range(4):
            for w, pairs in enumerate(SWEEPS):
                P = len(pairs)
                for ci in range(P):
                    for e in range(2):
                        hi = 2 * ci + e
                        h = 2 * pairs[ci] + e
                        obi = hi
                        wt(v, pe, pe_pos[(s, w, SLOT_TILES[s] - 1, ci, 1)])
                        i = v.reciprocal(F32S[64:65, 0:512], OB[obi][64:65, :])
                        inc(ve, i); wt(v, ve)
                        assert ve.n == div_recip[(s, w, hi)]
                        wt(v, pe, pe_pos[(s, w, hi, -1, 2)])
                        i = v.tensor_copy(F32S[0:64, 0:512], SCB[0:64, 0:512])
                        inc(ve, i); wt(v, ve)
                        i = v.tensor_tensor(A1[64 * (h % 2):64 * (h % 2) + 64, h // 2,
                                               512 * s:512 * s + 512],
                                            OB[obi][0:64, :], F32S[0:64, 0:512],
                                            op=ALU.mult)
                        inc(ve, i); wt(v, ve)
                assert ve.n == div_ve[(s, w)]
    att_div_end = att_div_end_pred

    if ATT_PHASE < 8:
        return (ve, att_div_end)

    @blk.gpsimd
    def _(g):
        wt(g, ve, att_div_end)
        i = g.dma_start(out=WS[:, :, :],
                        in_=p.t["w_out0"].rearrange("(c p) o -> p c o", p=128))
        inc(cs["wA"], i, 16)
    wout_tok = cs["wA"].n

    op_pe0 = pe.n
    op_ve0 = ve.n

    @blk.tensor
    def _(t):
        wt(t, cs["wA"], wout_tok)
        for s in range(4):
            if s >= 1:
                wt(t, ve, op_ve0 + 8 * s)
            for m in range(DC):
                for k in range(DC):
                    mm = t.matmul(banks[m][:, :], WS[:, k, 128 * m:128 * m + 128],
                                  A1[:, k, 512 * s:512 * s + 512],
                                  start=(k == 0), stop=(k == DC - 1),
                                  skip_group_check=True)
            inc(pe, mm)

    @blk.vector
    def _(v):
        for s in range(4):
            wt(v, pe, op_pe0 + s + 1)
            if s >= 1:
                wt(v, cs["lo"], 16 * s)
            for m in range(DC):
                i = v.tensor_scalar(A2[:, m, 1536:2048],
                                    banks[m][:, :], p.t["bout_sb"][:, m:m + 1], None,
                                    op0=ALU.add)
                inc(ve, i); wt(v, ve)
    @blk.gpsimd
    def _(g):
        for s in range(4):
            wt(g, ve, op_ve0 + 8 * (s + 1))
            i = g.dma_start(out=p.t["cc_o_in"][:, 512 * s:512 * s + 512]
                            .rearrange("(c p) t -> p c t", p=128),
                            in_=A2[:, :, 1536:2048])
            inc(cs["lo"], i, 16)
        wt(g, cs["lo"])
        i = g.collective_compute(
            "AllGather", ALU.bypass,
            ins=[p.t["cc_o_in"][:, :]], outs=[p.t["cc_o_out"][:, :]],
            replica_groups=[list(range(NCORE))])
        inc(cc, i)
        wt(g, cc)
        xadd0 = op_ve0 + 32
        for i4 in range(4):
            with g.register(f"xr{i4}") as r1, g.register(f"xc{i4}") as r2:
                g.reg_load(r1, p.t["idx_sb"][0:1, 16 + i4:17 + i4])
                g.reg_load(r2, p.t["idx_sb"][0:1, 20 + i4:21 + i4])
                o1 = g.snap(r1)
                o2 = g.snap(r2)
                if i4 >= 2:
                    wt(g, ve, xadd0 + 8 * (i4 - 1))
                i = g.dma_start(out=p.t["A2"][:, :, 512 * (i4 % 2):512 * (i4 % 2) + 512],
                                in_=p.t["cc_o_out"][bass.ds(o1, 1024), bass.ds(o2, 512)]
                                .rearrange("(c p) t -> p c t", p=128))
                inc(cs["lxA"] if i4 % 2 == 0 else cs["lxB"], i, 16)

    @blk.vector
    def _(v):
        for i4 in range(4):
            wt(v, cs["lxA"] if i4 % 2 == 0 else cs["lxB"], 16 * (i4 // 2 + 1))
            for m in range(DC):
                i = v.tensor_tensor(resid[:, m, 512 * i4:512 * i4 + 512],
                                    resid[:, m, 512 * i4:512 * i4 + 512],
                                    p.t["A2"][:, m, 512 * (i4 % 2):512 * (i4 % 2) + 512],
                                    op=ALU.add)
                inc(ve, i); wt(v, ve)
    return (ve, ve.n)


def build_prog_att():
    p = Prog("att")
    p.din("resid_in", [D, T], F32)
    p.din("w_qkv0", [3, D, D], F32)
    p.din("w_out0", [D, D], F32)
    p.din("wconv", [128, 3, 8, 128], F32)
    p.din("bconv", [128, DC], F32)
    p.din("bconv_vrow", [1, D], F32)
    p.din("bout", [128, DC], F32)
    p.din("null_k", [128, DC], F32)
    p.din("null_v_row", [1, 1040], F32)
    p.din("l0_masks", [NMASK, 128, 512], BF16)
    p.din("idx", [1, 32], I32)
    p.dout("resid_out", [D, T], F32)
    for nm, shp, dt in (("cc_q_in", [D, T], BF16), ("cc_k_in", [D, T], BF16),
                        ("cc_v_in", [D, T], BF16), ("cc_m_in", [8, 1], F32),
                        ("cc_o_in", [D, T], BF16)):
        p.dtmp(nm, shp, dt)
    p.dtmp("cc_q_out", [NCORE * D, T], BF16, shared=True)
    p.dtmp("cc_k_out", [NCORE * D, T], BF16, shared=True)
    p.dtmp("cc_vout", [NCORE * D, T], BF16, shared=True)
    p.dtmp("cc_m_out", [8, 1], F32, shared=True)
    p.dtmp("cc_o_out", [NCORE * D, T], BF16, shared=True)
    p.dtmp("kj_dram", [D, NJC], BF16)
    p.dtmp("vj_dram", [NJC, 1040], BF16)

    p.sb("RES", [128, DC, T], F32)
    p.sb("A1", [128, DC, T], BF16)
    p.sb("A2", [128, DC, T], BF16)
    p.sb("WS2", [128, DC, 1024], BF16)
    p.sb("KVB", [128, 2, DC, 512], BF16)
    p.sb("VB", [128, 2, 1040], BF16)
    p.sb("KSL", [128, 2, 3, 128], BF16)
    p.sb("VSL", [128, 2, 392], BF16)
    p.sb("MSKB", [128, 2, 512], BF16)
    p.sb("MXC", [128, 16], F32)
    p.sb("MXB", [128, 16], BF16)
    p.sb("NEGB", [128, 8], F32)
    p.sb("QPM", [128, 4], F32)
    p.sb("QPMB", [128, 1], BF16)
    p.sb("NEGR", [1, 8], BF16)
    p.sb("ONER", [128, 64], F32)
    p.sb("BCBV", [128, 1024], F32)
    p.sb("wconv_sb", [128, 3, 8, 128], BF16)
    p.sb("bconv_sb", [128, DC], F32)
    p.sb("bcvrow_sb", [1, 1024], F32)
    p.sb("bout_sb", [128, DC], F32)
    p.sb("nullk_sb", [128, DC], BF16)
    p.sb("nvr_sb", [1, 1040], BF16)
    p.sb("idmat", [128, 128], BF16)
    p.sb("F32S", [128, 1024], F32)
    p.sb("ROWS", [1, 4096], F32)
    p.sb("ones1", [1, 128], BF16)
    p.sb("onesb", [1, 128], F32)
    p.sb("onec", [128, 1], BF16)
    p.sb("CONSTS", [1, 16], F32)
    p.sb("idx_sb", [1, 32], I32)
    p.ps("SCB", [128, 1024], F32)
    for _i in range(6):
        p.ps(f"OB{_i}", [128, 512], F32)

    register_const(p, LN_EPS)
    cs = {k: p.sem(k) for k in ("se", "ve", "pe", "po", "ld", "cc", "wA", "wB",
                                "ini", "ldr", "st", "ldy", "ldq", "ldm",
                                "ldyB", "ldw", "lsA", "lsB", "lkA", "lkB", "lvA", "lvB", "lo", "lxA", "lxB")}
    blk = p.block()

    @blk.sync
    def _(y):
        i = y.dma_start(out=p.t["RES"][:, :, :],
                        in_=p.t["resid_in"].rearrange("(c p) t -> p c t", p=128))
        inc(cs["ldr"], i, 16)
        i = y.dma_start(out=p.t["bconv_sb"][:, :], in_=p.t["bconv"][:, :])
        inc(cs["ldr"], i, 16)
        i = y.dma_start(out=p.t["bcvrow_sb"][0:1, :], in_=p.t["bconv_vrow"][0:1, :])
        inc(cs["ldr"], i, 16)
        i = y.dma_start(out=p.t["bout_sb"][:, :], in_=p.t["bout"][:, :])
        inc(cs["ldr"], i, 16)
        i = y.dma_start(out=p.t["idx_sb"][:, :], in_=p.t["idx"][:, :])
        inc(cs["ldr"], i, 16)
    cs["_ldr_all"] = cs["ldr"].n

    @blk.gpsimd
    def _(g):
        i = g.dma_start(out=p.t["wconv_sb"][:, :, :, :], in_=p.t["wconv"][:, :, :, :])
        inc(cs["ld"], i, 16)
        i = g.dma_start(out=p.t["nullk_sb"][:, :], in_=p.t["null_k"][:, :])
        inc(cs["ld"], i, 16)
        i = g.dma_start(out=p.t["nvr_sb"][0:1, :], in_=p.t["null_v_row"][0:1, :])
        inc(cs["ld"], i, 16)
        i = g.memset(p.t["ONER"][:, :], 1.0)
        inc(cs["po"], i)
    cs["_att_pre_ld"] = cs["ld"].n

    emit_consts(p, cs)
    tok = emit_ln(p, cs, p.t["RES"], p.t["A1"], (cs["ldr"], cs["_ldr_all"]),
                  banks=[p.t["SCB"][:, 0:512], p.t["SCB"][:, 512:1024]]
                  + [p.t[f"OB{i}"][:, :] for i in range(6)],
                  sqc=p.t["A2"][:, 0:2, 0:2048])
    tok = emit_l0attn(p, cs, p.t["RES"], p.t["A1"], tok)

    @blk.sync
    def _(y):
        wt(y, tok[0], tok[1])
        i = y.dma_start(out=p.t["resid_out"].rearrange("(c p) t -> p c t", p=128),
                        in_=p.t["RES"][:, :, :])
        inc(cs["st"], i, 16)
        wt(y, cs["st"])
    p.close()
    return p


def stage_att_core(inp, c, resid):
    f32 = np.float32
    wqkv = np.asarray(inp["L0_attn_w_qkv"], f32)
    wq = np.stack([np.ascontiguousarray(wqkv[1024 * i:1024 * (i + 1)].T) for i in range(3)])
    wc = np.asarray(inp["L0_attn_w_conv"], f32)
    wconv = np.zeros((128, 3, 8, 128), f32)
    for h in range(16):
        for tau in range(3):
            base = 64 * (h % 2)
            wconv[base:base + 64, tau, h // 2, base:base + 64] = \
                wc[64 * h:64 * h + 64, :, tau].T
    bconv = np.asarray(inp["L0_attn_b_conv"], f32).reshape(DC, 128).T
    bcv = np.zeros((1, 1024), f32)
    bcv[0, :] = np.asarray(inp["L0_attn_b_conv"], f32)
    bout = np.asarray(inp["L0_attn_b_out"], f32).reshape(DC, 128).T
    nullk = np.asarray(inp["L0_attn_null_k"], f32).reshape(D).reshape(DC, 128).T
    nv = np.asarray(inp["L0_attn_null_v"], f32).reshape(D)
    nvr = np.zeros((1, 1040), f32)
    for h in range(16):
        nvr[0, 65 * h:65 * h + 64] = nv[64 * h:64 * h + 64]
        nvr[0, 65 * h + 64] = 1.0
    masks = []
    for s in range(4):
        b = qblocks_of_core(c)[s]
        q0 = 512 * b
        for tj in range(SLOT_MASK0[s], SLOT_TILES[s]):
            bk = 128 * tj + np.arange(128) - 1
            qg = q0 + np.arange(512)
            m = (qg[None, :] >= 3 * bk[:, None]).astype(BF)
            masks.append(m)
    idx = np.zeros(32, np.int32)
    idx[1] = 128 * ((c + 7) % 8)
    for s, b in enumerate(qblocks_of_core(c)):
        idx[8 + s] = 1024 * (b // 4)
        idx[12 + s] = 512 * (b % 4)
    for i4 in range(4):
        g = 4 * c + i4
        if g < 16:
            rc = g if g < 8 else 15 - g
        else:
            rc = g - 16 if g - 16 < 8 else 31 - g
        sc = qblocks_of_core(rc).index(g)
        idx[16 + i4] = 1024 * rc
        idx[20 + i4] = 512 * sc
    d = {
        "resid_in": np.ascontiguousarray(resid[T * c:T * (c + 1)].T),
        "w_qkv0": wq,
        "w_out0": np.ascontiguousarray(np.asarray(inp["L0_attn_w_out"], f32).T),
        "wconv": wconv,
        "bconv": np.ascontiguousarray(bconv),
        "bconv_vrow": bcv,
        "bout": np.ascontiguousarray(bout),
        "null_k": np.ascontiguousarray(nullk),
        "null_v_row": nvr,
        "l0_masks": np.ascontiguousarray(np.stack(masks)),
        "idx": idx.reshape(1, 32),
    }
    return d



def stage_ffn_core(inp, c, resid=None):
    f32 = np.float32
    x = resid if resid is not None else np.asarray(inp["L0_attn_x"], f32).reshape(S, D)
    d = {
        "resid_in": np.ascontiguousarray(x[T * c:T * (c + 1)].T),
        "w_r": np.ascontiguousarray(np.asarray(inp["L0_ffn_W_r"], f32).T),
        "w_k": np.ascontiguousarray(np.asarray(inp["L0_ffn_W_k"], f32).T),
        "w_v": np.ascontiguousarray(np.asarray(inp["L0_ffn_W_v"], f32).T),
        "mu_r": np.ascontiguousarray(np.asarray(inp["L0_ffn_mu_r"], f32).reshape(DC, 128).T),
        "mu_k": np.ascontiguousarray(np.asarray(inp["L0_ffn_mu_k"], f32).reshape(DC, 128).T),
        "idx": np.asarray([[0, ((c + 7) % 8) * 128] + [0] * 30], np.int32),
        "hv": np.asarray([[0.0 if c == 0 else 1.0]], f32),
    }
    return d


NHB = 4
DHB = 256
DHID = 512
ATTB_NCH = 8
NS_COEFFS = ((4.0848, -6.8946, 2.927), (3.9505, -6.3029, 2.6377),
             (3.7418, -5.5913, 2.3037), (2.8769, -3.1427, 1.2046),
             (2.8366, -3.0525, 1.2012))
BASE_LR_INV = float(np.log(np.expm1(0.001)))
BASE_E = float(np.expm1(0.001))
RMS_EPS = 1e-6
FRO_EPS = 1e-7
REN_EPS = 1e-5


def emit_l1attn(p, cs, tok):
    blk = p.blk
    se, ve, pe = cs["se"], cs["ve"], cs["pe"]
    ld, ldr, lq, cc, st = cs["ld"], cs["ldr"], cs["lq"], cs["cc"], cs["st"]
    PB = [p.t[f"PB{i}"] for i in range(8)]
    RES = p.t["RES"]
    AR3 = p.t["AR3"]
    QFC = AR3[:, 0:2, :]
    KFC = AR3[:, 2:4, :]
    VFC = AR3[:, 4:6, :]
    OBUF = AR3[:, 6:8, :]
    PAB = p.t["PAB"]
    XLT = PAB[:, 0:8, :]
    QWB = PAB[:, 8:16, :]
    SQW2 = p.t["SQW2"]
    LRF = SQW2[0:3, :, :].rearrange("p a f -> p (a f)")
    WQS = p.t["WQS"]
    LRW = p.t["LRW"]
    SCR = p.t["SCR"]
    ROWS = p.t["ROWS"]
    W0S, W1S, W2S = p.t["W0S"], p.t["W1S"], p.t["W2S"]
    W0B, W1B, W2B = p.t["W0B"], p.t["W1B"], p.t["W2B"]
    W1TB = p.t["W1TB"]
    PSB = p.t["PSB"]
    MP = [p.t["MPD"], p.t["MPH"], p.t["MPB"]]
    LCOL = p.t["LCOL"]
    FSC = p.t["FSC"]
    R1C = p.t["R1C"]
    idm = p.t["idmat"]

    def XSv(m, dc):
        return PAB[:, 2 * m + dc, :]

    def XTv(m, hc):
        q = 4 * m + hc
        return PAB[:, 6 + q // 2, 256 * (q % 2):256 * (q % 2) + 256]

    def ASv(m, kc):
        q = 2 * m + kc
        return PAB[:, 12 + q // 2, 256 * (q % 2):256 * (q % 2) + 256]

    def BMv(m, kc):
        q = 2 * m + kc
        return PAB[:, 15 + q // 2, 256 * (q % 2):256 * (q % 2) + 256]

    KTM = RES[:, 0, :].bitcast(BF16).rearrange("p (c f) -> p c f", f=256)
    VTM = RES[:, 1, :].bitcast(BF16).rearrange("p (c f) -> p c f", f=256)
    rsem, rval = tok

    @blk.gpsimd
    def _(g):
        g.memset(p.t["ECONS"][:, 0:1], BASE_LR_INV)
        g.memset(p.t["ECONS"][:, 1:2], REN_EPS)
        i = g.memset(p.t["ECONS"][:, 2:3], 1.0)
        inc(cs["po"], i)
    econ_tok = cs["po"].n

    for tb in range(4):
        ts_ = slice(512 * tb, 512 * tb + 512)

        @blk.vector
        def _(v, tb=tb, ts_=ts_):
            if tb == 0:
                wt(v, rsem, rval)
                wt(v, cs["ini"], cs["ini"].n)
            else:
                wt(v, pe, cs[f"_pa_lrpe{tb-1}"])
            for k in range(DC):
                i = v.tensor_copy(XLT[:, k, :], RES[:, k, ts_])
                inc(ve, i); wt(v, ve)
        cast_done = ve.n

        ln_pe0 = pe.n

        @blk.scalar
        def _(s_, ln_pe0=ln_pe0):
            wt(s_, ve, cast_done)
            for k in range(DC):
                if k >= 2:
                    wt(s_, pe, ln_pe0 + (k - 2) + 1)
                i = s_.activation(SQW2[:, k % 2, :], XLT[:, k, :], AF.Square)
                inc(se, i)
                wt(s_, se)
        ln_se0 = se.n - DC

        @blk.tensor
        def _(t, ln_se0=ln_se0):
            wt(t, ve, cast_done)
            for k in range(DC):
                wt(t, se, ln_se0 + k + 1)
                t.matmul(PB[6][0:1, :], p.t["onec"][:, 0:1], XLT[:, k, :],
                         start=(k == 0), stop=(k == DC - 1), skip_group_check=True)
                mm = t.matmul(PB[7][0:1, :], p.t["onec"][:, 0:1], SQW2[:, k % 2, :],
                              start=(k == 0), stop=(k == DC - 1),
                              skip_group_check=True)
                inc(pe, mm)
        ln_stats = pe.n

        @blk.scalar
        def _(s_):
            wt(s_, pe, ln_stats)
            i = s_.activation(ROWS[0:1, 0:512], PB[6][0:1, :], AF.Copy, scale=1.0 / D)
            inc(se, i); wt(s_, se)
        ln_m = se.n

        @blk.vector
        def _(v):
            wt(v, se, ln_m)
            i = v.tensor_tensor(ROWS[0:1, 512:1024], ROWS[0:1, 0:512],
                                ROWS[0:1, 0:512], op=ALU.mult)
            inc(ve, i); wt(v, ve)
            i = v.scalar_tensor_tensor(ROWS[0:1, 512:1024], PB[7][0:1, :], 1.0 / D,
                                       ROWS[0:1, 512:1024], op0=ALU.mult,
                                       op1=ALU.subtract)
            inc(ve, i); wt(v, ve)
        ln_var = ve.n

        @blk.scalar
        def _(s_):
            wt(s_, ve, ln_var)
            i = s_.activation(ROWS[0:1, 512:1024], ROWS[0:1, 512:1024], AF.Ln,
                              bias=LN_EPS)
            inc(se, i); wt(s_, se)
            i = s_.activation(ROWS[0:1, 512:1024], ROWS[0:1, 512:1024], AF.Exp,
                              scale=-0.5)
            inc(se, i); wt(s_, se)
        ln_rstd = se.n

        @blk.vector
        def _(v):
            wt(v, se, ln_rstd)
            i = v.scalar_tensor_tensor(ROWS[0:1, 0:512], ROWS[0:1, 0:512], -1.0,
                                       ROWS[0:1, 512:1024], op0=ALU.mult,
                                       op1=ALU.mult)
            inc(ve, i); wt(v, ve)
        ln_mr = ve.n

        @blk.tensor
        def _(t):
            wt(t, ve, ln_mr)
            t.matmul(PB[6][:, :], p.t["onesb"][0:1, :], ROWS[0:1, 512:1024],
                     start=True, stop=True, skip_group_check=True)
            mm = t.matmul(PB[7][:, :], p.t["onesb"][0:1, :], ROWS[0:1, 0:512],
                          start=True, stop=True, skip_group_check=True)
            inc(pe, mm)
        ln_bc = pe.n

        @blk.vector
        def _(v, ts_=ts_):
            wt(v, pe, ln_bc)
            for k in range(DC):
                i = v.tensor_tensor(SCR[:, 0, :], RES[:, k, ts_], PB[6][:, :],
                                    op=ALU.mult)
                inc(ve, i); wt(v, ve)
                i = v.tensor_tensor(XLT[:, k, :], SCR[:, 0, :], PB[7][:, :],
                                    op=ALU.add)
                inc(ve, i); wt(v, ve)
        xln_done = ve.n

        for which in range(3):
            nrm = p.t["qnw_sb"] if which == 0 else (
                p.t["knw_sb"] if which == 1 else None)
            pe_p0 = pe.n
            ve_e0 = ve.n
            ve_guard = ve.n
            se_guard = se.n

            def proj_pe(oc):
                if which == 2:
                    return pe_p0 + oc + 1
                return pe_p0 + (oc + 1 if oc < 3 else 2 * oc - 1)

            def rows_pe(j):
                return pe_p0 + (4 + 2 * j if j <= 5 else 9 + j)

            @blk.tensor
            def _(t, which=which, pe_p0=pe_p0, ve_e0=ve_e0, se_guard=se_guard,
                  ve_guard=ve_guard):
                wt(t, ve, xln_done)
                wt(t, cs["ldr"], cs["_ldr_all"])
                wt(t, ld, cs["_wq_ld"])
                wt(t, se, se_guard)
                wt(t, ve, ve_guard)
                if which > 0:
                    wt(t, *cs[f"_pa_st{tb}_{which-1}"])
                elif tb > 0:
                    wt(t, *cs[f"_pa_st{tb-1}_2"])

                def emit_rows(j):
                    wt(t, se, se_guard + 2 * j + 2)
                    t.matmul(PB[7][0:1, :], p.t["onec"][:, 0:1], SQW2[:, 0, :],
                             start=(j == 0), stop=(j == DC - 1),
                             skip_group_check=True)
                    mm = t.matmul(PB[3 + j // 2][0:1, :], p.t["onec"][:, 0:1],
                                  SQW2[:, 1, :], start=(j % 2 == 0),
                                  stop=(j % 2 == 1), skip_group_check=True)
                    inc(pe, mm)

                for oc in range(DC):
                    if oc >= 3:
                        wt(t, ve, ve_e0 + (oc - 3) + 1)
                        if which != 2:
                            wt(t, se, se_guard + 2 * (oc - 3) + 2)
                    for kc in range(DC):
                        mm = t.matmul(PB[oc % 3][:, :], WQS[:, which, kc, oc, :],
                                      XLT[:, kc, :], start=(kc == 0),
                                      stop=(kc == DC - 1), skip_group_check=True)
                    inc(pe, mm)
                    if which != 2 and oc >= 2:
                        emit_rows(oc - 2)
                if which != 2:
                    emit_rows(6)
                    emit_rows(7)

            @blk.vector
            def _(v, which=which, nrm=nrm, ve_guard=ve_guard):
                for oc in range(DC):
                    wt(v, pe, proj_pe(oc))
                    if which == 2:
                        i = v.tensor_copy(QWB[:, oc, :], PB[oc % 3][:, :])
                    else:
                        i = v.tensor_scalar(QWB[:, oc, :], PB[oc % 3][:, :],
                                            nrm[:, oc:oc + 1], None, op0=ALU.mult)
                    inc(ve, i); wt(v, ve)
            evac_done = ve.n

            if which != 2:
                @blk.scalar
                def _(s_, ve_e0=ve_e0, ve_guard=ve_guard):
                    wt(s_, ve, ve_guard)
                    for oc in range(DC):
                        wt(s_, pe, proj_pe(oc))
                        if oc >= 1:
                            wt(s_, pe, rows_pe(oc - 1))
                        i = s_.activation(SQW2[:, 0, :], PB[oc % 3][:, :], AF.Square)
                        inc(se, i); wt(s_, se)
                        wt(s_, ve, ve_e0 + oc + 1)
                        i = s_.activation(SQW2[:, 1, :], QWB[:, oc, :], AF.Square)
                        inc(se, i); wt(s_, se)
                rows_done = rows_pe(7)

                @blk.scalar
                def _(s_):
                    wt(s_, pe, rows_done)
                    i = s_.activation(ROWS[0:1, 1024:1536], PB[7][0:1, :], AF.Ln,
                                      scale=1.0 / D, bias=RMS_EPS)
                    inc(se, i); wt(s_, se)
                    i = s_.activation(ROWS[0:1, 1024:1536], ROWS[0:1, 1024:1536],
                                      AF.Exp, scale=-0.5)
                    inc(se, i); wt(s_, se)
                rstd_done = se.n

                apply_done = None
                for h in range(NHB):
                    @blk.scalar
                    def _(s_, h=h):
                        if h == 0:
                            wt(s_, se, rstd_done)
                        else:
                            wt(s_, pe, cs[f"_pa_b{tb}_{which}_{h-1}"])
                        i = s_.activation(ROWS[0:1, 0:512], PB[3 + h][0:1, :],
                                          AF.Sqrt)
                        inc(se, i); wt(s_, se)
                    s_done = se.n

                    @blk.vector
                    def _(v):
                        wt(v, se, s_done)
                        i = v.tensor_tensor(ROWS[0:1, 0:512], ROWS[0:1, 0:512],
                                            ROWS[0:1, 1024:1536], op=ALU.mult)
                        inc(ve, i); wt(v, ve)
                    d_done = ve.n

                    @blk.scalar
                    def _(s_):
                        wt(s_, ve, d_done)
                        i = s_.activation(ROWS[0:1, 0:512], ROWS[0:1, 0:512],
                                          AF.Ln, bias=REN_EPS)
                        inc(se, i); wt(s_, se)
                        i = s_.activation(ROWS[0:1, 0:512], ROWS[0:1, 0:512],
                                          AF.Exp, scale=-1.0)
                        inc(se, i); wt(s_, se)
                    r_done = se.n

                    @blk.vector
                    def _(v):
                        wt(v, se, r_done)
                        i = v.tensor_tensor(ROWS[0:1, 0:512], ROWS[0:1, 0:512],
                                            ROWS[0:1, 1024:1536], op=ALU.mult)
                        inc(ve, i); wt(v, ve)
                    sc_done = ve.n

                    @blk.tensor
                    def _(t, h=h):
                        wt(t, ve, sc_done)
                        mm = t.matmul(PB[3 + h][:, :], p.t["onesb"][0:1, :],
                                      ROWS[0:1, 0:512], start=True, stop=True,
                                      skip_group_check=True)
                        inc(pe, mm)
                    cs[f"_pa_b{tb}_{which}_{h}"] = pe.n
                    bch = pe.n

                    @blk.vector
                    def _(v, h=h, bch=bch):
                        wt(v, pe, bch)
                        for k in (2 * h, 2 * h + 1):
                            i = v.tensor_tensor(QWB[:, k, :], QWB[:, k, :],
                                                PB[3 + h][:, :], op=ALU.mult)
                            inc(ve, i); wt(v, ve)
                apply_done = ve.n
            else:
                apply_done = evac_done

            stsem = cs[("stq", "stk", "stv")[which]]

            @blk.gpsimd
            def _(g, which=which, apply_done=apply_done, ts_=ts_, stsem=stsem):
                wt(g, ve, apply_done)
                nm = ("cc_q_in", "cc_k_in", "cc_v_in")[which]
                i = g.dma_start(out=p.t[nm][:, ts_].rearrange("(c p) t -> p c t",
                                                              p=128),
                                in_=QWB[:, :, :])
                inc(stsem, i, 16)
            cs[f"_pa_st{tb}_{which}"] = (stsem, stsem.n)

        lr_ve_guard = ve.n

        @blk.tensor
        def _(t, lr_ve_guard=lr_ve_guard):
            wt(t, ve, lr_ve_guard)
            wt(t, ld, cs["_wq_ld"])
            for kc in range(DC):
                mm = t.matmul(PB[0][0:12, :], LRW[:, kc, :], XLT[:, kc, :],
                              start=(kc == 0), stop=(kc == DC - 1),
                              skip_group_check=True)
            inc(pe, mm)
        lr_pe = pe.n
        cs[f"_pa_lrpe{tb}"] = lr_pe

        @blk.vector
        def _(v, tb=tb):
            wt(v, pe, lr_pe)
            if tb > 0:
                wt(v, cs["stl"], cs[f"_pa_lrst{tb-1}"])
            i = v.tensor_scalar(SCR[0:12, 0, :], PB[0][0:12, :],
                                p.t["lrb_sb"][0:12, 0:1], None, op0=ALU.add)
            inc(ve, i); wt(v, ve)
        lr_add = ve.n

        @blk.scalar
        def _(s_, lr_add=lr_add):
            wt(s_, ve, lr_add)
            wt(s_, cs["po"], econ_tok)
            i = s_.activation(SCR[0:12, 1, :], SCR[0:12, 0, :], AF.Exp)
            inc(se, i); wt(s_, se)
            i = s_.activation(SCR[0:12, 1, :], SCR[0:12, 1, :], AF.Ln,
                              scale=BASE_E, bias=p.t["ECONS"][0:12, 2:3])
            inc(se, i); wt(s_, se)
        cs[f"_pa_lr{tb}"] = se.n

        @blk.gpsimd
        def _(g, tb=tb, ts_=ts_):
            wt(g, se, cs[f"_pa_lr{tb}"])
            i = g.dma_start(out=p.t["cc_l_in"][:, ts_], in_=SCR[0:12, 1, :])
            inc(cs["stl"], i, 16)
        cs[f"_pa_lrst{tb}"] = cs["stl"].n

    @blk.vector
    def _(v):
        wt(v, ldr, cs["_ldr_all"])
        for (src, dst) in ((W0S, W0B), (W1S, W1B), (W2S, W2B)):
            for dc in range(2):
                i = v.tensor_copy(dst[:, dc, :], src[:, dc, :])
                inc(ve, i); wt(v, ve)
    wcast_done = ve.n

    @blk.scalar
    def _(s_):
        wt(s_, se, cs["_pa_lr3"])
        wt(s_, ldr, cs["_ldr_all"])
        for dc in range(2):
            i = s_.activation(SQW2[:, dc, :], W0S[:, dc, :], AF.Square)
            inc(se, i); wt(s_, se)
    n0sq = se.n

    @blk.tensor
    def _(t):
        wt(t, se, n0sq)
        wt(t, *cs["_pa_st3_2"])
        for dc in range(2):
            mm = t.matmul(PB[6][0:1, :], p.t["onec"][:, 0:1], SQW2[:, dc, :],
                          start=(dc == 0), stop=(dc == 1), skip_group_check=True)
        inc(pe, mm)
    n0mm = pe.n

    @blk.scalar
    def _(s_):
        wt(s_, pe, n0mm)
        i = s_.activation(ROWS[0:1, 0:512], PB[6][0:1, :], AF.Sqrt)
        inc(se, i); wt(s_, se)
        for dc in range(2):
            i = s_.activation(SQW2[:, dc, :], W2S[:, dc, :], AF.Square)
            inc(se, i); wt(s_, se)
    n2sq = se.n

    @blk.tensor
    def _(t):
        wt(t, se, n2sq)
        for dc in range(2):
            mm = t.matmul(PB[6][0:1, :], p.t["onec"][:, 0:1], SQW2[:, dc, :],
                          start=(dc == 0), stop=(dc == 1), skip_group_check=True)
        inc(pe, mm)
    n2mm = pe.n

    @blk.scalar
    def _(s_):
        wt(s_, pe, n2mm)
        i = s_.activation(ROWS[0:1, 512:1024], PB[6][0:1, :], AF.Sqrt)
        inc(se, i); wt(s_, se)
        for dc in range(2):
            i = s_.activation(SQW2[:, dc, :], W1S[:, dc, :], AF.Square)
            inc(se, i); wt(s_, se)
    n1sq = se.n

    @blk.vector
    def _(v):
        wt(v, se, n1sq)
        for dc in range(2):
            i = v.reduce_sum(SCR[:, 0, 0:1], SQW2[:, dc, :],
                             axis=mybir.AxisListType.X)
            inc(ve, i); wt(v, ve)
            i = v.tensor_copy(R1C[:, 2 + dc:3 + dc], SCR[:, 0, 0:1])
            inc(ve, i); wt(v, ve)
    n1red = ve.n

    @blk.scalar
    def _(s_):
        wt(s_, ve, n1red)
        i = s_.activation(R1C[:, 2:4], R1C[:, 2:4], AF.Sqrt)
        inc(se, i); wt(s_, se)
    n1_done = se.n

    def emit_w1t(gate_sems):
        @blk.tensor
        def _(t):
            for sem_, val_ in gate_sems:
                wt(t, sem_, val_)
            for hc in range(4):
                bank_bf = (PB[6] if hc < 2 else PB[7])[:, :].bitcast(BF16)
                for oc in range(2):
                    c0 = 256 * (hc % 2) + 128 * oc
                    mm = t.transpose(bank_bf[:, c0:c0 + 128],
                                     W1B[:, oc, 128 * hc:128 * hc + 128], idm[:, :])
            inc(pe, mm)
        tdone = pe.n

        @blk.vector
        def _(v, tdone=tdone):
            wt(v, pe, tdone)
            for b_ in range(2):
                bank_bf = (PB[6] if b_ == 0 else PB[7])[:, :].bitcast(BF16)
                i = v.tensor_copy(W1TB[:, 2 * b_:2 * b_ + 2, :],
                                  bank_bf[:, 0:512].rearrange("p (a f) -> p a f",
                                                              f=256))
                inc(ve, i); wt(v, ve)
        return ve.n

    w1t_done = emit_w1t([(ve, wcast_done), (se, n1_done)])

    @blk.gpsimd
    def _(g):
        wt(g, cs["stq"], cs["stq"].n)
        wt(g, cs["stk"], cs["stk"].n)
        wt(g, cs["stv"], cs["stv"].n)
        wt(g, cs["stl"], cs["stl"].n)
        for nm in ("q", "k", "v", "l"):
            onm = "cc_vout" if nm == "v" else f"cc_{nm}_out"
            i = g.collective_compute(
                "AllGather", ALU.bypass,
                ins=[p.t[f"cc_{nm}_in"][:, :]], outs=[p.t[onm][:, :]],
                replica_groups=[list(range(NCORE))])
            inc(cc, i)
    ag_done = cc.n

    @blk.gpsimd
    def _(g):
        wt(g, cc, ag_done)
        with g.register("hrow") as r1, g.register("lrow") as r2:
            g.reg_load(r1, p.t["idx_sb"][0:1, 1:2])
            g.reg_load(r2, p.t["idx_sb"][0:1, 2:3])
            o1 = g.snap(r1)
            o2 = g.snap(r2)
            for nm, dst in (("cc_q_out", "my_q"), ("cc_k_out", "my_k"),
                            ("cc_vout", "my_v")):
                i = g.dma_start(
                    out=p.t[dst].rearrange("r (c t) -> r c t", t=T),
                    in_=p.t[nm].rearrange("(c r) t -> r c t", r=D)
                    [bass.ds(o1, 256), :, :])
                inc(lq, i, 16)
            i = g.dma_start(
                out=p.t["my_l"].rearrange("r (c t) -> r c t", t=T),
                in_=p.t["cc_l_out"].rearrange("(c r) t -> r c t", r=12)
                [bass.ds(o2, 3), :, :])
            inc(lq, i, 16)
    hx_done = lq.n

    prev = {"renorm_ve": w1t_done, "dw_pe": pe.n, "ost": st.n, "osc_ve": ve.n}
    for ci in range(ATTB_NCH):
        @blk.gpsimd
        def _(g, ci=ci):
            wt(g, lq, hx_done)
            wt(g, pe, prev["dw_pe"])
            wt(g, ve, prev["renorm_ve"])
            for buf, nm in ((QFC, "my_q"), (KFC, "my_k"), (VFC, "my_v")):
                i = g.dma_start(out=buf,
                                in_=p.t[nm][:, 2048 * ci:2048 * ci + 2048]
                                .rearrange("(c p) t -> p c t", p=128))
                inc(lq, i, 16)
            i = g.dma_start(out=LRF,
                            in_=p.t["my_l"][:, 2048 * ci:2048 * ci + 1024])
            inc(lq, i, 16)
        loads_done = lq.n

        lcol_half_ve = {}
        for lh in range(2):
            if lh == 1:
                @blk.gpsimd
                def _(g, ci=ci):
                    wt(g, ve, lcol_half_ve[0])
                    i = g.dma_start(out=LRF,
                                    in_=p.t["my_l"][:, 2048 * ci + 1024:
                                                    2048 * ci + 2048])
                    inc(cs["lq2"], i, 16)
            lh_lq = (lq, loads_done) if lh == 0 else (cs["lq2"], cs["lq2"].n)

            @blk.tensor
            def _(t, lh=lh, lh_lq=lh_lq):
                wt(t, *lh_lq)
                if lh == 0:
                    wt(t, ve, prev["osc_ve"])
                for cb in range(8):
                    mm = t.matmul(PB[2][:, 3 * cb:3 * cb + 3],
                                  LRF[0:3, 128 * cb:128 * cb + 128],
                                  idm[0:3, 0:3], start=True, stop=True,
                                  skip_group_check=True)
                inc(pe, mm)
            ltr_pe = pe.n

            @blk.vector
            def _(v, lh=lh, ltr_pe=ltr_pe):
                wt(v, pe, ltr_pe)
                i = v.tensor_copy(LCOL[:, 8 * lh:8 * lh + 8, :],
                                  PB[2][:, 0:24].rearrange("p (c j) -> p c j",
                                                           j=3))
                inc(ve, i); wt(v, ve)
            lcol_half_ve[lh] = ve.n
        lcol_done = ve.n

        tr_ve0 = ve.n
        for src, dst, base in ((KFC, KTM, 0), (VFC, VTM, 1)):
            for cb2 in range(8):
                @blk.tensor
                def _(t, src=src, cb2=cb2, base=base):
                    wt(t, lq, loads_done)
                    if base == 0 and cb2 < 2:
                        wt(t, ve, prev["osc_ve"])
                    g_ = base * 8 + cb2
                    if g_ >= 2:
                        wt(t, ve, tr_ve0 + (g_ - 2) + 1)
                    bank_bf = PB[cb2 % 2][:, :].bitcast(BF16)
                    for j in range(2):
                        cb = 2 * cb2 + j
                        for dc in range(2):
                            c0 = 256 * j + 128 * dc
                            mm = t.transpose(bank_bf[:, c0:c0 + 128],
                                             src[:, dc, 128 * cb:128 * cb + 128],
                                             idm[:, :])
                    inc(pe, mm)
                tp_pe = pe.n

                @blk.vector
                def _(v, dst=dst, cb2=cb2, tp_pe=tp_pe):
                    wt(v, pe, tp_pe)
                    bank_bf = PB[cb2 % 2][:, :].bitcast(BF16)
                    i = v.tensor_copy(dst[:, 2 * cb2:2 * cb2 + 2, :],
                                      bank_bf[:, 0:512]
                                      .rearrange("p (a f) -> p a f", f=256))
                    inc(ve, i); wt(v, ve)
        tm_done = ve.n

        for tb in range(4):
            @blk.tensor
            def _(t, tb=tb):
                wt(t, lq, loads_done)
                wt(t, ve, tm_done)
                wt(t, ve, prev["renorm_ve"])
                if tb >= 1:
                    wt(t, ve, cs[f"_sc_oe{ci}_{tb-1}"])
                for r in range(4):
                    for dc in range(2):
                        t.matmul(PB[r][:, :], W0B[:, dc, 128 * r:128 * r + 128],
                                 QFC[:, dc, 512 * tb:512 * tb + 512],
                                 start=(dc == 0), stop=(dc == 1),
                                 skip_group_check=True)
                    for dc in range(2):
                        mm = t.matmul(PB[4 + r][:, :],
                                      W2B[:, dc, 128 * r:128 * r + 128],
                                      QFC[:, dc, 512 * tb:512 * tb + 512],
                                      start=(dc == 0), stop=(dc == 1),
                                      skip_group_check=True)
                inc(pe, mm)
            gqhq = pe.n

            @blk.scalar
            def _(s_):
                wt(s_, pe, gqhq)
                for r in range(4):
                    i = s_.activation(SCR[:, r, :], PB[r][:, :], AF.Silu)
                    inc(se, i)
                    wt(s_, se)
            silu_done = se.n

            @blk.vector
            def _(v, tb=tb):
                wt(v, se, silu_done)
                if tb >= 1:
                    wt(v, pe, cs[f"_sc_o{ci}_{tb-1}"])
                for r in range(4):
                    i = v.tensor_tensor(PSB[:, r, :], SCR[:, r, :],
                                        PB[4 + r][:, :], op=ALU.mult)
                    inc(ve, i); wt(v, ve)
            p_done = ve.n

            @blk.tensor
            def _(t, tb=tb, p_done=p_done):
                wt(t, ve, p_done)
                for m in range(2):
                    for kc in range(4):
                        mm = t.matmul(PB[m][:, :],
                                      W1TB[:, kc, 128 * m:128 * m + 128],
                                      PSB[:, kc, :], start=(kc == 0),
                                      stop=(kc == 3), skip_group_check=True)
                    inc(pe, mm)
            cs[f"_sc_o{ci}_{tb}"] = pe.n
            o_pe = pe.n

            @blk.vector
            def _(v, tb=tb, o_pe=o_pe, ci=ci):
                wt(v, pe, o_pe)
                if tb == 0 and ci > 0:
                    wt(v, st, prev["ost"])
                for m in range(2):
                    i = v.tensor_copy(OBUF[:, m, 512 * tb:512 * tb + 512],
                                      PB[m][:, :])
                    inc(ve, i); wt(v, ve)
            cs[f"_sc_oe{ci}_{tb}"] = ve.n
        osc_ve = ve.n
        prev["osc_ve"] = osc_ve

        @blk.gpsimd
        def _(g, ci=ci):
            wt(g, ve, osc_ve)
            i = g.dma_start(out=p.t["o_dram"][:, 2048 * ci:2048 * ci + 2048]
                            .rearrange("(c p) t -> p c t", p=128),
                            in_=OBUF)
            inc(st, i, 16)
        prev["ost"] = st.n

        dwe_ve = {}
        pass_chain_end = {}
        for h2 in range(2):
            chain0 = ve.n
            for pc in range(16):
                @blk.tensor
                def _(t, h2=h2, pc=pc, chain0=chain0):
                    if h2 == 0 and pc == 0:
                        wt(t, ve, osc_ve)
                        wt(t, ve, lcol_done)
                    if h2 == 1 and pc == 0:
                        wt(t, ve, dwe_ve[0])
                    if h2 == 1 and pc < 2:
                        wt(t, ve, pass_chain_end[0])
                    if pc >= 2:
                        wt(t, ve, chain0 + 7 * (pc - 1))
                    co = 256 * (pc % 2)
                    for dc in range(2):
                        t.matmul(PB[3][:, co:co + 256],
                                 KFC[:, dc, 128 * pc:128 * pc + 128],
                                 W0B[:, dc, 256 * h2:256 * h2 + 256],
                                 start=(dc == 0), stop=(dc == 1),
                                 skip_group_check=True)
                    for dc in range(2):
                        t.matmul(PB[4][:, co:co + 256],
                                 KFC[:, dc, 128 * pc:128 * pc + 128],
                                 W2B[:, dc, 256 * h2:256 * h2 + 256],
                                 start=(dc == 0), stop=(dc == 1),
                                 skip_group_check=True)
                    for oc in range(2):
                        mm = t.matmul(PB[5][:, co:co + 256],
                                      VFC[:, oc, 128 * pc:128 * pc + 128],
                                      W1B[:, oc, 256 * h2:256 * h2 + 256],
                                      start=(oc == 0), stop=(oc == 1),
                                      skip_group_check=True)
                    inc(pe, mm)
                maps_pe = pe.n

                @blk.scalar
                def _(s_, h2=h2, pc=pc, maps_pe=maps_pe, chain0=chain0):
                    wt(s_, pe, maps_pe)
                    if pc >= 2:
                        wt(s_, ve, chain0 + 7 * (pc - 1))
                    if h2 == 1 and pc < 2:
                        wt(s_, ve, pass_chain_end[0])
                    co = 256 * (pc % 2)
                    i = s_.activation(SCR[:, 0, co:co + 256], PB[3][:, co:co + 256],
                                      AF.Sigmoid)
                    inc(se, i)
                sig_done = se.n

                @blk.vector
                def _(v, h2=h2, pc=pc, sig_done=sig_done, chain0=chain0):
                    wt(v, se, sig_done)
                    if pc >= 2:
                        wt(v, pe, cs[f"_dw{ci}_{h2}_{pc-2}"])
                    co = 256 * (pc % 2)
                    G = PB[3][:, co:co + 256]
                    H = PB[4][:, co:co + 256]
                    DH_ = PB[5][:, co:co + 256]
                    sig = SCR[:, 0, co:co + 256]
                    sg = SCR[:, 1, co:co + 256]
                    u = SCR[:, 2, co:co + 256]
                    dg = SCR[:, 3, co:co + 256]
                    i = v.tensor_tensor(sg, G, sig, op=ALU.mult)
                    inc(ve, i); wt(v, ve)
                    i = v.scalar_tensor_tensor(MP[1][:, pc % 2, :], H,
                                               LCOL[:, pc, 1:2], sg,
                                               op0=ALU.mult, op1=ALU.mult)
                    inc(ve, i); wt(v, ve)
                    i = v.tensor_tensor(dg, H, sig, op=ALU.mult)
                    inc(ve, i); wt(v, ve)
                    i = v.scalar_tensor_tensor(u, G, 1.0, sg,
                                               op0=ALU.add, op1=ALU.subtract)
                    inc(ve, i); wt(v, ve)
                    i = v.tensor_tensor(dg, DH_, dg, op=ALU.mult)
                    inc(ve, i); wt(v, ve)
                    i = v.scalar_tensor_tensor(MP[0][:, pc % 2, :], dg,
                                               LCOL[:, pc, 0:1], u,
                                               op0=ALU.mult, op1=ALU.mult)
                    inc(ve, i); wt(v, ve)
                    i = v.scalar_tensor_tensor(MP[2][:, pc % 2, :], DH_,
                                               LCOL[:, pc, 2:3], sg,
                                               op0=ALU.mult, op1=ALU.mult)
                    inc(ve, i); wt(v, ve)
                chain_done = ve.n
                assert chain_done == chain0 + 7 * (pc + 1)

                @blk.tensor
                def _(t, pc=pc, chain_done=chain_done):
                    wt(t, ve, chain_done)
                    for mi, (lhs_tm, mpi) in enumerate(((KTM, 0), (VTM, 1),
                                                        (KTM, 2))):
                        for dc in range(2):
                            mm = t.matmul(PB[mi][:, 256 * dc:256 * dc + 256],
                                          lhs_tm[:, pc, 128 * dc:128 * dc + 128],
                                          MP[mpi][:, pc % 2, :],
                                          start=(pc == 0 and dc == 0),
                                          stop=(pc == 15),
                                          skip_group_check=(dc == 1))
                    inc(pe, mm)
                cs[f"_dw{ci}_{h2}_{pc}"] = pe.n
            pass_chain_end[h2] = ve.n
            dw_half_pe = pe.n

            @blk.vector
            def _(v, h2=h2, dw_half_pe=dw_half_pe):
                wt(v, pe, dw_half_pe)
                for m in range(3):
                    i = v.tensor_copy(
                        PAB[:, 2 * m:2 * m + 2, 256 * h2:256 * h2 + 256],
                        PB[m][:, :].rearrange("p (a f) -> p a f", f=256))
                    inc(ve, i); wt(v, ve)
            dwe_ve[h2] = ve.n
        prev["dw_pe"] = pe.n
        dw_all_ve = dwe_ve[1]

        fro_ve = {}
        for m in range(3):
            @blk.scalar
            def _(s_, m=m):
                wt(s_, ve, dw_all_ve)
                if m > 0:
                    wt(s_, pe, cs[f"_fr{ci}_{m-1}"])
                for dc in range(2):
                    i = s_.activation(SQW2[:, dc, :], XSv(m, dc), AF.Square)
                    inc(se, i); wt(s_, se)
            fsq = se.n

            @blk.tensor
            def _(t, m=m, fsq=fsq):
                wt(t, se, fsq)
                for dc in range(2):
                    mm = t.matmul(PB[6][0:1, :], p.t["onec"][:, 0:1],
                                  SQW2[:, dc, :], start=(dc == 0), stop=(dc == 1),
                                  skip_group_check=True)
                inc(pe, mm)
            fmm = pe.n

            @blk.vector
            def _(v, fmm=fmm):
                wt(v, pe, fmm)
                i = v.reduce_sum(ROWS[0:1, 1024:1025], PB[6][0:1, :],
                                 axis=mybir.AxisListType.X)
                inc(ve, i); wt(v, ve)
            fred = ve.n

            @blk.scalar
            def _(s_, fred=fred):
                wt(s_, ve, fred)
                i = s_.activation(ROWS[0:1, 1025:1026], ROWS[0:1, 1024:1025],
                                  AF.Sqrt)
                inc(se, i); wt(s_, se)
                i = s_.activation(ROWS[0:1, 1026:1027], ROWS[0:1, 1025:1026],
                                  AF.Ln, bias=FRO_EPS)
                inc(se, i); wt(s_, se)
                i = s_.activation(ROWS[0:1, 1026:1027], ROWS[0:1, 1026:1027],
                                  AF.Exp, scale=-1.0)
                inc(se, i); wt(s_, se)
            fs_done = se.n

            @blk.tensor
            def _(t, m=m, fs_done=fs_done):
                wt(t, se, fs_done)
                if m > 0:
                    wt(t, ve, fro_ve[m - 1])
                mm = t.matmul(PB[7][:, 0:1], p.t["onesb"][0:1, :],
                              ROWS[0:1, 1026:1027], start=True, stop=True,
                              skip_group_check=True)
                inc(pe, mm)
            cs[f"_fr{ci}_{m}"] = pe.n
            fbc = pe.n

            @blk.vector
            def _(v, m=m, fbc=fbc):
                wt(v, pe, fbc)
                i = v.tensor_copy(FSC[:, m:m + 1], PB[7][:, 0:1])
                inc(ve, i); wt(v, ve)
                for dc in range(2):
                    i = v.tensor_scalar(XSv(m, dc), XSv(m, dc), FSC[:, m:m + 1],
                                        None, op0=ALU.mult)
                    inc(ve, i); wt(v, ve)
            fro_ve[m] = ve.n

        xt_ve = {}
        for m in range(3):
            @blk.tensor
            def _(t, m=m):
                wt(t, ve, fro_ve[m])
                for hc in range(4):
                    bank_bf = (PB[2 * m] if hc < 2 else PB[2 * m + 1])[:, :] \
                        .bitcast(BF16)
                    for dc in range(2):
                        c0 = 256 * (hc % 2) + 128 * dc
                        mm = t.transpose(bank_bf[:, c0:c0 + 128],
                                         XSv(m, dc)[:, 128 * hc:128 * hc + 128],
                                         idm[:, :])
                inc(pe, mm)
            xtp = pe.n

            @blk.vector
            def _(v, m=m, xtp=xtp):
                wt(v, pe, xtp)
                for b_ in range(2):
                    bank_bf = (PB[2 * m] if b_ == 0 else PB[2 * m + 1])[:, :] \
                        .bitcast(BF16)
                    for j in range(2):
                        i = v.tensor_copy(XTv(m, 2 * b_ + j),
                                          bank_bf[:, 256 * j:256 * j + 256])
                        inc(ve, i); wt(v, ve)
            xt_ve[m] = ve.n

        ns_tok = {m: {"x_ve": fro_ve[m], "xt_ve": xt_ve[m]} for m in range(3)}
        for it, (a_c, b_c, c_c) in enumerate(NS_COEFFS):
            a_pe, a_ve, a2_pe, bm_ve, bx_pe = {}, {}, {}, {}, {}
            for m in range(3):
                @blk.tensor
                def _(t, m=m):
                    wt(t, ve, ns_tok[m]["xt_ve"])
                    for mi in range(2):
                        for nc_ in range(4):
                            mm = t.matmul(PB[2 * m][:, 256 * mi:256 * mi + 256],
                                          XTv(m, nc_)[:, 128 * mi:128 * mi + 128],
                                          XTv(m, nc_),
                                          start=(nc_ == 0 and mi == 0),
                                          stop=(nc_ == 3),
                                          skip_group_check=(mi == 1))
                    inc(pe, mm)
                a_pe[m] = pe.n
            for m in range(3):
                @blk.vector
                def _(v, m=m):
                    wt(v, pe, a_pe[m])
                    for kc in range(2):
                        i = v.tensor_copy(ASv(m, kc),
                                          PB[2 * m][:, 256 * kc:256 * kc + 256])
                        inc(ve, i); wt(v, ve)
                a_ve[m] = ve.n
            for m in range(3):
                @blk.tensor
                def _(t, m=m):
                    wt(t, ve, a_ve[m])
                    for mi in range(2):
                        for kc in range(2):
                            mm = t.matmul(PB[2 * m + 1][:, 256 * mi:256 * mi + 256],
                                          ASv(m, kc)[:, 128 * mi:128 * mi + 128],
                                          ASv(m, kc),
                                          start=(kc == 0 and mi == 0),
                                          stop=(kc == 1),
                                          skip_group_check=(mi == 1))
                    inc(pe, mm)
                a2_pe[m] = pe.n
            for m in range(3):
                @blk.vector
                def _(v, m=m, b_c=b_c, c_c=c_c):
                    wt(v, pe, a2_pe[m])
                    i = v.tensor_scalar(SCR[:, 0, :], PB[2 * m + 1][:, :], c_c,
                                        None, op0=ALU.mult)
                    inc(ve, i); wt(v, ve)
                    for kc in range(2):
                        i = v.scalar_tensor_tensor(
                            BMv(m, kc), PB[2 * m][:, 256 * kc:256 * kc + 256],
                            b_c, SCR[:, 0, 256 * kc:256 * kc + 256],
                            op0=ALU.mult, op1=ALU.add)
                        inc(ve, i); wt(v, ve)
                bm_ve[m] = ve.n
            for m in range(3):
                @blk.tensor
                def _(t, m=m):
                    wt(t, ve, bm_ve[m])
                    for mi in range(2):
                        for jc in range(2):
                            mm = t.matmul(PB[2 * m + mi][:, :],
                                          BMv(m, jc)[:, 128 * mi:128 * mi + 128],
                                          XSv(m, jc), start=(jc == 0),
                                          stop=(jc == 1), skip_group_check=True)
                    inc(pe, mm)
                bx_pe[m] = pe.n
            for m in range(3):
                @blk.vector
                def _(v, m=m, a_c=a_c):
                    wt(v, pe, bx_pe[m])
                    for mi in range(2):
                        i = v.scalar_tensor_tensor(XSv(m, mi), XSv(m, mi), a_c,
                                                   PB[2 * m + mi][:, :],
                                                   op0=ALU.mult, op1=ALU.add)
                        inc(ve, i); wt(v, ve)
                ns_tok[m]["x_ve"] = ve.n
            if it < len(NS_COEFFS) - 1:
                for m in range(3):
                    @blk.tensor
                    def _(t, m=m):
                        wt(t, ve, ns_tok[m]["x_ve"])
                        for hc in range(4):
                            bank_bf = (PB[2 * m] if hc < 2
                                       else PB[2 * m + 1])[:, :].bitcast(BF16)
                            for dc in range(2):
                                c0 = 256 * (hc % 2) + 128 * dc
                                mm = t.transpose(
                                    bank_bf[:, c0:c0 + 128],
                                    XSv(m, dc)[:, 128 * hc:128 * hc + 128],
                                    idm[:, :])
                        inc(pe, mm)
                    xtp = pe.n

                    @blk.vector
                    def _(v, m=m, xtp=xtp):
                        wt(v, pe, xtp)
                        for b_ in range(2):
                            bank_bf = (PB[2 * m] if b_ == 0
                                       else PB[2 * m + 1])[:, :].bitcast(BF16)
                            for j in range(2):
                                i = v.tensor_copy(
                                    XTv(m, 2 * b_ + j),
                                    bank_bf[:, 256 * j:256 * j + 256])
                                inc(ve, i); wt(v, ve)
                    ns_tok[m]["xt_ve"] = ve.n

        for m, (WS_, WB_) in enumerate(((W0S, W0B), (W1S, W1B), (W2S, W2B))):
            @blk.vector
            def _(v, m=m, WS_=WS_):
                wt(v, ve, ns_tok[m]["x_ve"])
                for dc in range(2):
                    i = v.tensor_tensor(WS_[:, dc, :], WS_[:, dc, :], XSv(m, dc),
                                        op=ALU.add)
                    inc(ve, i); wt(v, ve)
            wadd = ve.n

            @blk.scalar
            def _(s_, m=m, WS_=WS_, wadd=wadd):
                wt(s_, ve, wadd)
                if m > 0:
                    wt(s_, ve, cs[f"_renap{ci}_{m-1}"])
                for dc in range(2):
                    i = s_.activation(SQW2[:, dc, :], WS_[:, dc, :], AF.Square)
                    inc(se, i); wt(s_, se)
            wsq = se.n

            if m != 1:
                @blk.tensor
                def _(t, wsq=wsq, m=m):
                    wt(t, se, wsq)
                    if m == 2:
                        wt(t, ve, cs[f"_renap{ci}_0"])
                    for dc in range(2):
                        mm = t.matmul(PB[6][0:1, :], p.t["onec"][:, 0:1],
                                      SQW2[:, dc, :], start=(dc == 0),
                                      stop=(dc == 1), skip_group_check=True)
                    inc(pe, mm)
                rmm = pe.n

                @blk.scalar
                def _(s_, rmm=rmm):
                    wt(s_, pe, rmm)
                    i = s_.activation(ROWS[0:1, 1024:1536], PB[6][0:1, :], AF.Sqrt)
                    inc(se, i); wt(s_, se)
                    i = s_.activation(ROWS[0:1, 1024:1536], ROWS[0:1, 1024:1536],
                                      AF.Ln, bias=REN_EPS)
                    inc(se, i); wt(s_, se)
                    i = s_.activation(ROWS[0:1, 1024:1536], ROWS[0:1, 1024:1536],
                                      AF.Exp, scale=-1.0)
                    inc(se, i); wt(s_, se)
                rrec = se.n

                @blk.vector
                def _(v, m=m, rrec=rrec):
                    wt(v, se, rrec)
                    nrow = ROWS[0:1, 0:512] if m == 0 else ROWS[0:1, 512:1024]
                    i = v.tensor_tensor(ROWS[0:1, 1024:1536], ROWS[0:1, 1024:1536],
                                        nrow, op=ALU.mult)
                    inc(ve, i); wt(v, ve)
                rsc = ve.n

                @blk.tensor
                def _(t, rsc=rsc):
                    wt(t, ve, rsc)
                    mm = t.matmul(PB[6][:, :], p.t["onesb"][0:1, :],
                                  ROWS[0:1, 1024:1536], start=True, stop=True,
                                  skip_group_check=True)
                    inc(pe, mm)
                rbc = pe.n

                @blk.vector
                def _(v, m=m, WS_=WS_, WB_=WB_, rbc=rbc):
                    wt(v, pe, rbc)
                    for dc in range(2):
                        i = v.tensor_tensor(WS_[:, dc, :], WS_[:, dc, :],
                                            PB[6][:, :], op=ALU.mult)
                        inc(ve, i); wt(v, ve)
                        i = v.tensor_copy(WB_[:, dc, :], WS_[:, dc, :])
                        inc(ve, i); wt(v, ve)
                cs[f"_renap{ci}_{m}"] = ve.n
            else:
                @blk.vector
                def _(v, wsq=wsq):
                    wt(v, se, wsq)
                    for dc in range(2):
                        i = v.reduce_sum(SCR[:, 0, 0:1], SQW2[:, dc, :],
                                         axis=mybir.AxisListType.X)
                        inc(ve, i); wt(v, ve)
                        i = v.tensor_copy(R1C[:, dc:dc + 1], SCR[:, 0, 0:1])
                        inc(ve, i); wt(v, ve)
                w1red = ve.n

                @blk.scalar
                def _(s_, w1red=w1red):
                    wt(s_, ve, w1red)
                    i = s_.activation(R1C[:, 0:2], R1C[:, 0:2], AF.Sqrt)
                    inc(se, i); wt(s_, se)
                    wt(s_, cs["po"], econ_tok)
                    i = s_.activation(R1C[:, 0:2], R1C[:, 0:2], AF.Ln,
                                      bias=p.t["ECONS"][:, 1:2])
                    inc(se, i); wt(s_, se)
                    i = s_.activation(R1C[:, 0:2], R1C[:, 0:2], AF.Exp,
                                      scale=-1.0)
                    inc(se, i); wt(s_, se)
                w1rec = se.n

                @blk.vector
                def _(v, WS_=WS_, WB_=WB_, w1rec=w1rec):
                    wt(v, se, w1rec)
                    i = v.tensor_tensor(R1C[:, 0:2], R1C[:, 0:2], R1C[:, 2:4],
                                        op=ALU.mult)
                    inc(ve, i); wt(v, ve)
                    for dc in range(2):
                        i = v.tensor_scalar(WS_[:, dc, :], WS_[:, dc, :],
                                            R1C[:, dc:dc + 1], None, op0=ALU.mult)
                        inc(ve, i); wt(v, ve)
                        i = v.tensor_copy(WB_[:, dc, :], WS_[:, dc, :])
                        inc(ve, i); wt(v, ve)
                cs[f"_renap{ci}_{m}"] = ve.n

        w1t2 = emit_w1t([(ve, cs[f"_renap{ci}_1"]), (ve, cs[f"_renap{ci}_2"])])
        prev["renorm_ve"] = w1t2

    @blk.gpsimd
    def _(g):
        wt(g, st, prev["ost"])
        i = g.collective_compute(
            "AllGather", ALU.bypass,
            ins=[p.t["o_dram"][:, :]], outs=[p.t["cc_oo"][:, :]],
            replica_groups=[[0, 1, 2, 3], [4, 5, 6, 7]])
        inc(cc, i)
    oag = cc.n

    @blk.sync
    def _(y):
        wt(y, pe, prev["dw_pe"])
        wt(y, ve, prev["renorm_ve"])
        i = y.dma_start(out=RES[:, :, :],
                        in_=p.t["resid_in"].rearrange("(c p) t -> p c t", p=128))
        inc(ldr, i, 16)
    res2 = ldr.n

    add_ve0 = ve.n

    @blk.gpsimd
    def _(g):
        wt(g, cc, oag)
        wt(g, pe, prev["dw_pe"])
        wt(g, ve, prev["osc_ve"])
        with g.register("oex") as r1:
            g.reg_load(r1, p.t["idx_sb"][0:1, 0:1])
            o1 = g.snap(r1)
            for h in range(NHB):
                if h >= 2:
                    wt(g, ve, add_ve0 + 2 * (h - 1))
                i = g.dma_start(out=AR3[:, 2 * (h % 2):2 * (h % 2) + 2, :],
                                in_=p.t["cc_oo"][256 * h:256 * h + 256,
                                                 bass.ds(o1, 2048)]
                                .rearrange("(c p) t -> p c t", p=128))
                inc(cs["lxA"] if h % 2 == 0 else cs["lxB"], i, 16)

    @blk.vector
    def _(v):
        for h in range(NHB):
            wt(v, cs["lxA"] if h % 2 == 0 else cs["lxB"], 16 * (h // 2 + 1))
            wt(v, ldr, res2)
            for j in range(2):
                i = v.tensor_tensor(RES[:, 2 * h + j, :], RES[:, 2 * h + j, :],
                                    AR3[:, 2 * (h % 2) + j, :], op=ALU.add)
                inc(ve, i); wt(v, ve)
    return (ve, ve.n)


def build_prog_attB():
    p = Prog("attB")
    p.din("resid_in", [D, T], F32)
    p.din("wqkvT", [3, D, D], F32)
    p.din("lrwT", [D, 12], F32)
    p.din("lrb", [12, 1], F32)
    p.din("qnw", [128, DC], F32)
    p.din("knw", [128, DC], F32)
    p.din("w0T_in", [DHB, DHID], F32)
    p.din("w1_in", [DHB, DHID], F32)
    p.din("w2T_in", [DHB, DHID], F32)
    p.din("idx", [1, 32], I32)
    p.dout("resid_out", [D, T], F32)
    for nm in ("cc_q_in", "cc_k_in", "cc_v_in"):
        p.dtmp(nm, [D, T], BF16)
    p.dtmp("cc_l_in", [12, T], BF16)
    p.dtmp("cc_q_out", [NCORE * D, T], BF16, shared=True)
    p.dtmp("cc_k_out", [NCORE * D, T], BF16, shared=True)
    p.dtmp("cc_vout", [NCORE * D, T], BF16, shared=True)
    p.dtmp("cc_l_out", [NCORE * 12, T], BF16, shared=True)
    p.dtmp("o_dram", [DHB, S], BF16)
    p.dtmp("cc_oo", [4 * DHB, S], BF16)
    p.dtmp("my_q", [DHB, S], BF16)
    p.dtmp("my_k", [DHB, S], BF16)
    p.dtmp("my_v", [DHB, S], BF16)
    p.dtmp("my_l", [3, S], BF16)

    p.sb("RES", [128, DC, T], F32)
    p.sb("AR3", [128, DC, T], BF16)
    p.sb("WQS", [128, 3, DC, DC, 128], BF16)
    p.sb("PAB", [128, 18, 512], BF16)
    p.sb("SQW2", [128, 2, 512], BF16)
    p.sb("LRW", [128, DC, 12], BF16)
    p.sb("lrb_sb", [128, 1], F32)
    p.sb("W0S", [128, 2, 512], F32)
    p.sb("W1S", [128, 2, 512], F32)
    p.sb("W2S", [128, 2, 512], F32)
    p.sb("W0B", [128, 2, 512], BF16)
    p.sb("W1B", [128, 2, 512], BF16)
    p.sb("W2B", [128, 2, 512], BF16)
    p.sb("W1TB", [128, 4, 256], BF16)
    p.sb("PSB", [128, 4, 512], BF16)
    p.sb("MPD", [128, 2, 256], BF16)
    p.sb("MPH", [128, 2, 256], BF16)
    p.sb("MPB", [128, 2, 256], BF16)
    p.sb("SCR", [128, 4, 512], F32)
    p.sb("LCOL", [128, 16, 3], F32)
    p.sb("FSC", [128, 4], F32)
    p.sb("R1C", [128, 4], F32)
    p.sb("ECONS", [128, 4], F32)
    p.sb("qnw_sb", [128, DC], F32)
    p.sb("knw_sb", [128, DC], F32)
    p.sb("ROWS", [1, 1536], F32)
    p.sb("idmat", [128, 128], BF16)
    p.sb("ones1", [1, 128], BF16)
    p.sb("onesb", [1, 128], F32)
    p.sb("onec", [128, 1], BF16)
    p.sb("CONSTS", [1, 16], F32)
    p.sb("idx_sb", [1, 32], I32)
    for _i in range(8):
        p.ps(f"PB{_i}", [128, 512], F32)

    register_const(p, LN_EPS)
    register_const(p, RMS_EPS)
    register_const(p, FRO_EPS)
    register_const(p, REN_EPS)
    register_const(p, BASE_LR_INV)
    cs = {k: p.sem(k) for k in ("se", "ve", "pe", "po", "ld", "ldr", "lq",
                                "lq2", "cc", "st", "stf", "ini", "stq", "stk",
                                "stv", "stl", "lxA", "lxB")}
    blk = p.block()

    @blk.sync
    def _(y):
        i = y.dma_start(out=p.t["RES"][:, :, :],
                        in_=p.t["resid_in"].rearrange("(c p) t -> p c t", p=128))
        inc(cs["ldr"], i, 16)
        i = y.dma_start(out=p.t["lrb_sb"][0:12, 0:1], in_=p.t["lrb"][:, :])
        inc(cs["ldr"], i, 16)
        i = y.dma_start(out=p.t["qnw_sb"][:, :], in_=p.t["qnw"][:, :])
        inc(cs["ldr"], i, 16)
        i = y.dma_start(out=p.t["knw_sb"][:, :], in_=p.t["knw"][:, :])
        inc(cs["ldr"], i, 16)
        for nm, dst in (("w0T_in", "W0S"), ("w1_in", "W1S"), ("w2T_in", "W2S")):
            i = y.dma_start(out=p.t[dst][:, :, :],
                            in_=p.t[nm].rearrange("(a p) h -> p a h", p=128))
            inc(cs["ldr"], i, 16)
        i = y.dma_start(out=p.t["idx_sb"][:, :], in_=p.t["idx"][:, :])
        inc(cs["ldr"], i, 16)
    cs["_ldr_all"] = cs["ldr"].n

    @blk.gpsimd
    def _(g):
        for wi in range(3):
            for kc in range(DC):
                i = g.dma_start(
                    out=p.t["WQS"][:, wi, kc, :, :].rearrange("p a f -> p (a f)"),
                    in_=p.t["wqkvT"][wi, 128 * kc:128 * kc + 128, :])
                inc(cs["ld"], i, 16)
        i = g.dma_start(out=p.t["LRW"][:, :, :],
                        in_=p.t["lrwT"].rearrange("(c p) j -> p c j", p=128))
        inc(cs["ld"], i, 16)
    cs["_wq_ld"] = cs["ld"].n

    emit_consts(p, cs)
    tok = emit_l1attn(p, cs, (cs["ldr"], cs["_ldr_all"]))

    @blk.sync
    def _(y):
        wt(y, tok[0], tok[1])
        i = y.dma_start(out=p.t["resid_out"].rearrange("(c p) t -> p c t", p=128),
                        in_=p.t["RES"][:, :, :])
        inc(cs["stf"], i, 16)
        wt(y, cs["stf"])
    p.close()
    return p


def stage_attB_core(inp, c, resid):
    f32 = np.float32
    hd = c % NHB
    qkvw = np.asarray(inp["L1_attn_qkv_w"], f32)
    wq = np.stack([np.ascontiguousarray(qkvw[1024 * i:1024 * (i + 1)].T)
                   for i in range(3)])
    lrw = np.asarray(inp["L1_attn_lr_proj_w"], f32)
    lrb = np.asarray(inp["L1_attn_lr_proj_b"], f32).reshape(12, 1)
    d = {
        "resid_in": np.ascontiguousarray(resid[T * c:T * (c + 1)].T),
        "wqkvT": wq,
        "lrwT": np.ascontiguousarray(lrw.T),
        "lrb": lrb,
        "qnw": np.ascontiguousarray(
            np.asarray(inp["L1_attn_q_norm_w"], f32).reshape(DC, 128).T),
        "knw": np.ascontiguousarray(
            np.asarray(inp["L1_attn_k_norm_w"], f32).reshape(DC, 128).T),
        "w0T_in": np.ascontiguousarray(np.asarray(inp["L1_attn_w0"], f32)[hd].T),
        "w1_in": np.ascontiguousarray(np.asarray(inp["L1_attn_w1"], f32)[hd]),
        "w2T_in": np.ascontiguousarray(np.asarray(inp["L1_attn_w2"], f32)[hd].T),
        "idx": np.asarray([[2048 * c, 256 * hd, 3 * hd] + [0] * 29],
                          np.int32),
    }
    return d


_PROGS = {}


def _get_prog(name):
    if name not in _PROGS:
        _PROGS[name] = {"ffn": build_prog_ffn, "att": build_prog_att,
                        "moe": build_prog_moe, "attB": build_prog_attB}[name]()
    return _PROGS[name]


def _launch(name, stage_fn, inputs, resid):
    p = _get_prog(name)
    in_maps = [stage_fn(inputs, c, resid) for c in range(NCORE)]
    res = run_bass_kernel_spmd(p.nc, in_maps, core_ids=list(range(NCORE)))
    return np.concatenate(
        [np.asarray(res.results[c]["resid_out"]).T for c in range(NCORE)],
        axis=0)


def kernel(**inputs):
    x = np.asarray(inputs["L0_attn_x"], np.float32).reshape(S, D)
    s1 = _launch("ffn", lambda i, c, r: stage_ffn_core(i, c, r), inputs, x)
    s2 = _launch("att", stage_att_core, inputs, s1)
    s3 = _launch("ffn", lambda i, c, r: stage_ffn_core(i, c, r), inputs, s2)
    s4 = _launch("moe", stage_moe_core, inputs, s3)
    s5 = _launch("attB", stage_attB_core, inputs, s4)
    s6 = _launch("moe", stage_moe_core, inputs, s5)
    return np.ascontiguousarray(s6.reshape(1, S, D), dtype=np.float32)


if __name__ == "__main__":
    pass


# version 05 (final)
# speedup vs baseline: 1.0610x; 1.0610x over previous
import os
import math
import contextlib
import numpy as np

import jax

try:
    jax.config.update("jax_compilation_cache_dir", "/tmp/jax_neff_cache")
    jax.config.update("jax_persistent_cache_min_compile_time_secs", 1.0)
except Exception:
    pass

import ml_dtypes
import concourse.bass as bass
import concourse.mybir as mybir
from concourse.bass_utils import run_bass_kernel_spmd

F32 = mybir.dt.float32
BF16 = mybir.dt.bfloat16
I32 = mybir.dt.int32
AF = mybir.ActivationFunctionType
ALU = mybir.AluOpType

NCORE = 8
T = 2048
D = 1024
DC = 8
S = 16384
LN_EPS = 1e-5
NEXP = 8
HID = 4096

BF = ml_dtypes.bfloat16


class Sem:
    __slots__ = ("h", "n")

    def __init__(self, h):
        self.h = h
        self.n = 0


def inc(sem, instr, k=1):
    instr.then_inc(sem.h, k)
    sem.n += k
    return sem.n


def wt(eng, sem, val=None):
    v = sem.n if val is None else val
    if v > 0:
        eng.wait_ge(sem.h, v)


class Prog:
    def __init__(self, name="prog"):
        self.nc = bass.Bass("TRN2", target_bir_lowering=False, debug=False)
        self.es = contextlib.ExitStack()
        self.t = {}

    def din(self, name, shape, dt=F32):
        ap = self.nc.dram_tensor(name, list(shape), dt, kind="ExternalInput").ap()
        self.t[name] = ap
        return ap

    def dout(self, name, shape, dt=F32):
        ap = self.nc.dram_tensor(name, list(shape), dt, kind="ExternalOutput").ap()
        self.t[name] = ap
        return ap

    def dtmp(self, name, shape, dt=F32, shared=False):
        kw = {"addr_space": "Shared"} if shared else {}
        ap = self.nc.dram_tensor(name, list(shape), dt, **kw).ap()
        self.t[name] = ap
        return ap

    def sb(self, name, shape, dt):
        h = self.es.enter_context(self.nc.sbuf_tensor(name, list(shape), dt))
        self.t[name] = h
        return h

    def ps(self, name, shape, dt=F32):
        h = self.es.enter_context(self.nc.psum_tensor(name, list(shape), dt))
        self.t[name] = h
        return h

    def sem(self, name):
        return Sem(self.es.enter_context(self.nc.semaphore(name)))

    def block(self):
        self.blk = self.es.enter_context(self.nc.Block())
        return self.blk

    def close(self):
        self.es.close()



CONST_VALS = []


def register_const(p, value, dtype=F32):
    key = (dtype, float(value))
    cells = p.t.setdefault("_const_cells", {})
    if key in cells:
        return
    idx = len(cells)
    cells[key] = idx
    p.nc.const_aps.register(dtype, float(value), p.t["CONSTS"][0:1, idx:idx + 1])


def emit_consts(p, cs):
    blk = p.blk
    ini = cs["ini"]

    @blk.gpsimd
    def _(g):
        g.memset(p.t["ones1"][:, :], 1.0)
        g.memset(p.t["onesb"][:, :], 1.0)
        for (dt_, val), idx in p.t.get("_const_cells", {}).items():
            g.memset(p.t["CONSTS"][0:1, idx:idx + 1], val)
        if "idmat" in p.t:
            i = g.memset(p.t["idmat"][:, :], 1.0)
            inc(ini, i)
            wt(g, ini)
            g.affine_select(out=p.t["idmat"][:, :], in_=p.t["idmat"][:, :],
                            pattern=[[1, 128]], base=0, channel_multiplier=-1,
                            compare_op=ALU.is_equal, fill=0.0)
        i = g.memset(p.t["onec"][:, :], 1.0)
        inc(ini, i)



def emit_ln(p, cs, resid, xln, ready_tok, banks=None, sqc=None):
    blk = p.blk
    se, ve, pe = cs["se"], cs["ve"], cs["pe"]
    SQC = sqc if sqc is not None else p.t["SQC"]
    rows = p.t["ROWS"]
    if banks is None:
        PSQ = [p.t[f"PSQ{q}"] for q in range(4)]
        banks = []
        for q in range(4):
            banks += [PSQ[q][:, 0:512], PSQ[q][:, 512:1024]]
    F32S = p.t["F32S"]
    rsem, rval = ready_tok
    NJ = T // 512

    @blk.vector
    def _(v):
        wt(v, rsem, rval)
        for k in range(DC):
            i = v.tensor_copy(xln[:, k, :], resid[:, k, :])
        inc(ve, i)
    cast_done = ve.n

    se0 = se.n
    pe0 = pe.n

    @blk.scalar
    def _(s):
        wt(s, ve, cast_done)
        for k in range(DC):
            if k >= 2:
                wt(s, pe, pe0 + (k - 2) + 1)
            i = s.activation(SQC[:, k % 2, :], xln[:, k, :], AF.Square)
            inc(se, i)

    @blk.tensor
    def _(t):
        wt(t, cs["ini"], cs["ini"].n)
        wt(t, ve, cast_done)
        for k in range(DC):
            wt(t, se, se0 + k + 1)
            for j in range(NJ):
                t.matmul(banks[2 * j][0:1, :], p.t["onec"][:, 0:1],
                         xln[:, k, j * 512:(j + 1) * 512],
                         start=(k == 0), stop=(k == DC - 1), skip_group_check=True)
            for j in range(NJ):
                i = t.matmul(banks[2 * j + 1][0:1, :], p.t["onec"][:, 0:1],
                             SQC[:, k % 2, j * 512:(j + 1) * 512],
                             start=(k == 0), stop=(k == DC - 1), skip_group_check=True)
            inc(pe, i)
    stats_done = pe.n

    @blk.scalar
    def _(s):
        wt(s, pe, stats_done)
        for j in range(NJ):
            s.activation(rows[0:1, 512 * j:512 * j + 512], banks[2 * j][0:1, :],
                         AF.Copy, scale=1.0 / D)
        for j in range(NJ):
            i = s.activation(rows[0:1, 2048 + 512 * j:2048 + 512 * j + 512],
                             banks[2 * j + 1][0:1, :], AF.Copy, scale=1.0 / D)
        inc(se, i)
    r01 = se.n

    @blk.vector
    def _(v):
        wt(v, se, r01)
        for hh in range(2):
            sl = slice(1024 * hh, 1024 * hh + 1024)
            i = v.tensor_tensor(F32S[0:1, 0:1024], rows[0:1, sl], rows[0:1, sl], op=ALU.mult)
            inc(ve, i)
            wt(v, ve)
            i = v.tensor_tensor(rows[0:1, 2048 + 1024 * hh:2048 + 1024 * hh + 1024],
                                rows[0:1, 2048 + 1024 * hh:2048 + 1024 * hh + 1024],
                                F32S[0:1, 0:1024], op=ALU.subtract)
            inc(ve, i)
            wt(v, ve)
    var_done = ve.n

    @blk.scalar
    def _(s):
        wt(s, ve, var_done)
        wt(s, cs["ini"], cs["ini"].n)
        i = s.activation(rows[0:1, 2048:2048 + T], rows[0:1, 2048:2048 + T], AF.Ln, bias=LN_EPS)
        inc(se, i)
        wt(s, se)
        i = s.activation(rows[0:1, 2048:2048 + T], rows[0:1, 2048:2048 + T], AF.Exp, scale=-0.5)
        inc(se, i)
    rstd_done = se.n

    @blk.vector
    def _(v):
        wt(v, se, rstd_done)
        i = v.scalar_tensor_tensor(rows[0:1, 0:T], rows[0:1, 0:T], -1.0,
                                   rows[0:1, 2048:2048 + T], op0=ALU.mult, op1=ALU.mult)
        inc(ve, i)
    mr_done = ve.n

    @blk.tensor
    def _(t):
        wt(t, ve, mr_done)
        for j in range(NJ):
            t.matmul(banks[2 * j][:, :], p.t["onesb"][0:1, :],
                     rows[0:1, 2048 + j * 512:2048 + (j + 1) * 512], start=True, stop=True,
                     skip_group_check=True)
        for j in range(NJ):
            i = t.matmul(banks[2 * j + 1][:, :], p.t["onesb"][0:1, :],
                         rows[0:1, j * 512:(j + 1) * 512], start=True, stop=True,
                         skip_group_check=True)
        inc(pe, i)
    bc_done = pe.n

    @blk.vector
    def _(v):
        wt(v, pe, bc_done)
        for k in range(DC):
            for j in range(NJ):
                i = v.tensor_tensor(F32S[:, 0:512], resid[:, k, 512 * j:512 * j + 512],
                                    banks[2 * j][:, :], op=ALU.mult)
                inc(ve, i)
                wt(v, ve)
                i = v.tensor_tensor(xln[:, k, 512 * j:512 * j + 512], F32S[:, 0:512],
                                    banks[2 * j + 1][:, :], op=ALU.add)
                inc(ve, i)
                wt(v, ve)
    return (ve, ve.n)



def emit_l0ffn(p, cs, resid, xln, xln_tok):
    blk = p.blk
    se, ve, pe, po, ld, cc = cs["se"], cs["ve"], cs["pe"], cs["po"], cs["ld"], cs["cc"]
    wA, wB = cs["wA"], cs["wB"]
    WS = p.t["WS"]
    PSQ = [p.t[f"PSQ{q}"] for q in range(4)]
    KA = p.t["KA"]
    B0P = p.t["B0P"]
    F32S = p.t["F32S"]
    HXC = p.t["HXC"]; hvc = p.t["HVC"]
    mu_r = p.t["mu_r_sb"]; mu_k = p.t["mu_k_sb"]
    omu_r = p.t["omu_r_sb"]; omu_k = p.t["omu_k_sb"]
    vsem, vval = xln_tok
    NB = T // 256

    @blk.vector
    def _(v):
        wt(v, vsem, vval)
        i = v.tensor_copy(HXC[:, :], xln[:, :, T - 1:T])
        inc(ve, i)
    lastc_done = ve.n

    @blk.gpsimd
    def _(g):
        wt(g, ve, lastc_done)
        i = g.dma_start(out=p.t["cc_ffn_in"][:, :], in_=HXC[:, :])
        inc(ld, i, 16)
        wt(g, ld)
        i = g.collective_compute(
            "AllGather", ALU.bypass,
            ins=[p.t["cc_ffn_in"][:, :]], outs=[p.t["cc_ffn_out"][:, :]],
            replica_groups=[list(range(NCORE))])
        inc(cc, i)
        wt(g, cc)
        with g.register("hoff") as r:
            g.reg_load(r, p.t["idx_sb"][0:1, 1:2])
            off = g.snap(r)
            i = g.dma_start(out=HXC[:, :], in_=p.t["cc_ffn_out"][bass.ds(off, 128), :])
            inc(ld, i, 16)
        i = g.dma_start(out=hvc[:, 0:1], in_=p.t["hv"][0:1, 0:1].to_broadcast((128, 1)))
        inc(ld, i, 16)
    halo_ld = ld.n
    halo_bc = 0

    @blk.vector
    def _(v):
        wt(v, ld, halo_ld)
        wt(v, vsem, vval)
        for k in range(DC):
            v.tensor_copy(B0P[:, k, 1:256], xln[:, k, 0:255])
            i = v.tensor_scalar(B0P[:, k, 0:1], HXC[:, k:k + 1], hvc[:, 0:1], None, op0=ALU.mult)
        inc(ve, i)
        i = v.tensor_scalar(omu_k[:, :], mu_k[:, :], -1.0, 1.0, op0=ALU.mult, op1=ALU.add)
        inc(ve, i)
        wt(v, ve)
        i = v.tensor_scalar(omu_r[:, :], mu_r[:, :], -1.0, 1.0, op0=ALU.mult, op1=ALU.add)
        inc(ve, i)
        wt(v, ve)
    prep_done = ve.n

    @blk.gpsimd
    def _(g):
        i = g.dma_start(out=WS[:, 0, :, :],
                        in_=p.t["w_k"].rearrange("(c p) o -> p c o", p=128))
        inc(wA, i, 16)
    wk_ld = wA.n

    @blk.vector
    def _(v):
        wt(v, wA, wk_ld)
        wt(v, ve, prep_done)
        for k in range(DC):
            i = v.tensor_scalar(WS[:, 1, k, :], WS[:, 0, k, :], mu_k[:, k:k + 1], None, op0=ALU.mult)
            inc(ve, i)
            wt(v, ve)
            i = v.tensor_scalar(WS[:, 0, k, :], WS[:, 0, k, :], omu_k[:, k:k + 1], None, op0=ALU.mult)
            inc(ve, i)
            wt(v, ve)
    wk_prep = ve.n

    def src_prev(j, k):
        if j == 0:
            return B0P[:, k, 0:256]
        return xln[:, k, 256 * j - 1:256 * j + 255]

    pe0 = pe.n
    se0 = se.n

    @blk.tensor
    def _(t):
        wt(t, ve, wk_prep)
        for j in range(NB):
            if j >= 2:
                wt(t, se, se0 + 3 * (j - 2) + 2)
            for m in range(DC):
                q = PSQ[2 * (j % 2) + m // 4]
                co = 256 * (m % 4)
                for k in range(DC):
                    t.matmul(q[:, co:co + 256],
                             WS[:, 1, k, 128 * m:128 * m + 128],
                             xln[:, k, 256 * j:256 * j + 256],
                             start=(k == 0 and m % 2 == 0), stop=False,
                             skip_group_check=(m % 2 == 1))
                for k in range(DC):
                    i = t.matmul(q[:, co:co + 256],
                                 WS[:, 0, k, 128 * m:128 * m + 128],
                                 src_prev(j, k),
                                 start=False, stop=(k == DC - 1),
                                 skip_group_check=(m % 2 == 1))
            inc(pe, i)

    @blk.scalar
    def _(s):
        for j in range(NB):
            wt(s, pe, pe0 + j + 1)
            i = s.activation(KA[:, 0:4, 256 * j:256 * j + 256], PSQ[2 * (j % 2)][:, :], AF.Relu)
            inc(se, i)
            wt(s, se)
            i = s.activation(KA[:, 4:8, 256 * j:256 * j + 256], PSQ[2 * (j % 2) + 1][:, :], AF.Relu)
            inc(se, i)
            wt(s, se)
            i = s.activation(KA[:, :, 256 * j:256 * j + 256], KA[:, :, 256 * j:256 * j + 256], AF.Square)
            inc(se, i)
    ka_all = se0 + 3 * NB

    for H in range(2):
        wv_tok = wB.n + 16
        wr_tok = wA.n + 16

        @blk.gpsimd
        def _(g):
            if H == 0:
                wt(g, pe, pe0 + NB)
            else:
                wt(g, pe, cs["_ffn_h0_pe_end"])
            i = g.dma_start(out=WS[:, 0, 0:DC, 0:512],
                            in_=p.t["w_v"][:, 512 * H:512 * H + 512].rearrange("(c p) o -> p c o", p=128))
            inc(wB, i, 16)
            i = g.dma_start(out=WS[:, 0, 0:DC, 512:1024],
                            in_=p.t["w_r"][:, 512 * H:512 * H + 512].rearrange("(c p) o -> p c o", p=128))
            inc(wA, i, 16)

        ve_wr0 = ve.n

        @blk.vector
        def _(v):
            wt(v, wA, wr_tok)
            for k in range(DC):
                i = v.tensor_scalar(WS[:, 1, k, 0:512], WS[:, 0, k, 512:1024], mu_r[:, k:k + 1], None, op0=ALU.mult)
                inc(ve, i)
                wt(v, ve)
                i = v.tensor_scalar(WS[:, 1, k, 512:1024], WS[:, 0, k, 512:1024], omu_r[:, k:k + 1], None, op0=ALU.mult)
                inc(ve, i)
                wt(v, ve)
        wr_prep = ve.n

        peH = pe.n
        seH = se.n
        veH = ve.n

        @blk.tensor
        def _(t):
            wt(t, wB, wv_tok)
            wt(t, ve, wr_prep)
            if H == 0:
                wt(t, se, ka_all)
            else:
                wt(t, ve, cs["_ffn_h0_ve_end"])
                wt(t, se, cs["_ffn_h0_se_end"])
            for j in range(NB):
                if j >= 2:
                    wt(t, se, seH + 2 * (j - 2) + 1)
                    wt(t, ve, veH + 8 * (j - 1))
                qkv = PSQ[j % 2]
                for m in range(4):
                    for k in range(DC):
                        i = t.matmul(qkv[:, 256 * m:256 * m + 256],
                                     WS[:, 0, k, 128 * m:128 * m + 128],
                                     KA[:, k, 256 * j:256 * j + 256],
                                     start=(k == 0 and m % 2 == 0), stop=(k == DC - 1),
                                     skip_group_check=(m % 2 == 1))
                inc(pe, i)
                qr = PSQ[2 + j % 2]
                for m in range(4):
                    for k in range(DC):
                        t.matmul(qr[:, 256 * m:256 * m + 256],
                                 WS[:, 1, k, 128 * m:128 * m + 128],
                                 xln[:, k, 256 * j:256 * j + 256],
                                 start=(k == 0 and m % 2 == 0), stop=False,
                                 skip_group_check=(m % 2 == 1))
                    for k in range(DC):
                        i = t.matmul(qr[:, 256 * m:256 * m + 256],
                                     WS[:, 1, k, 512 + 128 * m:512 + 128 * m + 128],
                                     src_prev(j, k),
                                     start=False, stop=(k == DC - 1),
                                     skip_group_check=(m % 2 == 1))
                inc(pe, i)
        cs["_ffn_h0_pe_end"] = pe.n

        @blk.scalar
        def _(s):
            for j in range(NB):
                wt(s, pe, peH + 2 * j + 2)
                if j >= 1:
                    wt(s, ve, veH + 8 * j)
                i = s.activation(F32S[:, 0:1024], PSQ[2 + j % 2][:, :], AF.Sigmoid)
                inc(se, i)
                wt(s, se)
                i = s.nop()
                inc(se, i)

        @blk.vector
        def _(v):
            for j in range(NB):
                wt(v, se, seH + 2 * j + 1)
                wt(v, pe, peH + 2 * j + 1)
                for m in range(4):
                    kc = 4 * H + m
                    i = v.tensor_tensor(F32S[:, 1024 + 256 * m:1024 + 256 * m + 256],
                                        F32S[:, 256 * m:256 * m + 256],
                                        PSQ[j % 2][:, 256 * m:256 * m + 256], op=ALU.mult)
                    inc(ve, i)
                    wt(v, ve)
                    i = v.scalar_tensor_tensor(resid[:, kc, 256 * j:256 * j + 256],
                                               F32S[:, 1024 + 256 * m:1024 + 256 * m + 256], 0.5,
                                               resid[:, kc, 256 * j:256 * j + 256],
                                               op0=ALU.mult, op1=ALU.add)
                    inc(ve, i)
                    wt(v, ve)
        cs["_ffn_h0_ve_end"] = ve.n
        cs["_ffn_h0_se_end"] = se.n
    return (ve, ve.n)



def build_prog_ffn():
    p = Prog("ffnA")
    p.din("resid_in", [D, T], F32)
    p.din("w_r", [D, D], F32)
    p.din("w_k", [D, D], F32)
    p.din("w_v", [D, D], F32)
    p.din("mu_r", [128, DC], F32)
    p.din("mu_k", [128, DC], F32)
    p.din("idx", [1, 32], I32)
    p.din("hv", [1, 1], F32)
    p.dout("resid_out", [D, T], F32)
    p.dtmp("cc_ffn_in", [128, DC], BF16)
    p.dtmp("cc_ffn_out", [NCORE * 128, DC], BF16)

    p.sb("RES", [128, DC, T], F32)
    p.sb("XLN", [128, DC, T], BF16)
    p.sb("KA", [128, DC, T], BF16)
    p.sb("WS", [128, 2, DC, D], BF16)
    p.sb("B0P", [128, DC, 256], BF16)
    p.sb("SQC", [128, 2, 2048], BF16)
    p.sb("F32S", [128, T], F32)
    p.sb("ROWS", [1, 4096], F32)
    p.sb("ones1", [1, 128], BF16)
    p.sb("onesb", [1, 128], F32)
    p.sb("onec", [128, 1], BF16)
    p.sb("CONSTS", [1, 16], F32)
    p.sb("HXC", [128, DC], BF16)
    p.sb("HVC", [128, 1], F32)
    p.sb("mu_r_sb", [128, DC], F32)
    p.sb("mu_k_sb", [128, DC], F32)
    p.sb("omu_r_sb", [128, DC], F32)
    p.sb("omu_k_sb", [128, DC], F32)
    p.sb("idx_sb", [1, 32], I32)
    p.sb("hv_sb", [1, 1], F32)
    for _q in range(4):
        p.ps(f"PSQ{_q}", [128, 1024], F32)

    register_const(p, LN_EPS)
    cs = {k: p.sem(k) for k in ("se", "ve", "pe", "po", "ld", "cc", "wA", "wB", "ini", "ldr", "st")}
    blk = p.block()

    @blk.sync
    def _(y):
        i = y.dma_start(out=p.t["RES"][:, :, :],
                        in_=p.t["resid_in"].rearrange("(c p) t -> p c t", p=128))
        inc(cs["ldr"], i, 16)
        i = y.dma_start(out=p.t["mu_r_sb"][:, :], in_=p.t["mu_r"][:, :])
        inc(cs["ldr"], i, 16)
        i = y.dma_start(out=p.t["mu_k_sb"][:, :], in_=p.t["mu_k"][:, :])
        inc(cs["ldr"], i, 16)
        i = y.dma_start(out=p.t["idx_sb"][:, :], in_=p.t["idx"][:, :])
        inc(cs["ldr"], i, 16)
        i = y.dma_start(out=p.t["hv_sb"][:, :], in_=p.t["hv"][:, :])
        inc(cs["ldr"], i, 16)
    lds = cs["ldr"].n

    emit_consts(p, cs)
    tok = emit_ln(p, cs, p.t["RES"], p.t["XLN"], (cs["ldr"], lds))
    tok = emit_l0ffn(p, cs, p.t["RES"], p.t["XLN"], tok)

    @blk.sync
    def _(y):
        wt(y, tok[0], tok[1])
        i = y.dma_start(out=p.t["resid_out"].rearrange("(c p) t -> p c t", p=128),
                        in_=p.t["RES"][:, :, :])
        inc(cs["st"], i, 16)
        wt(y, cs["st"])
    p.close()
    return p



MOE_NE = NEXP
MOE_NS = 8
MOE_NB = 8
BIGNEG = -1.0e30


def emit_moe(p, cs, resid, xln, xln_tok):
    blk = p.blk
    se, ve, pe, ld = cs["se"], cs["ve"], cs["pe"], cs["ld"]
    wA, wB, ldb = cs["wA"], cs["wB"], cs["ldb"]
    PSQ = [p.t[f"PSQ{q}"] for q in range(4)]
    WG = p.t["WG"]; WU = p.t["WU"]; WD = p.t["WD"]
    CBC = p.t["CBC"]; T1 = p.t["T1"]; HSB = p.t["HSB"]
    RT = p.t["RT"]; CT = p.t["CT"]; CTIN = p.t["CTIN"]
    vsem, vval = xln_tok
    NIT = MOE_NS * MOE_NB
    NITALL = MOE_NE * NIT

    rt_pe0 = pe.n
    rt_ve0 = ve.n
    rt_se0 = se.n
    ct_done = rt_ve0 + 9 * 16

    def rt_pe_logits(i):
        return rt_pe0 + (1 if i == 0 else 2 * i)

    def rt_pe_transpose(i):
        return rt_pe0 + (2 * i + 3 if i < 15 else 32)

    def _emit_transpose(t, i):
        wt(t, ve, rt_ve0 + 9 * i + 8)
        if i >= 4:
            wt(t, ve, rt_ve0 + 9 * (i - 4) + 9)
        psq1_bf = PSQ[1][:, :].bitcast(BF16)
        mm = t.transpose(psq1_bf[0:7, 256 * (i % 4):256 * (i % 4) + 128],
                         CTIN[:, 7 * (i % 2):7 * (i % 2) + 7],
                         p.t["idmat"][:, :])
        inc(pe, mm)

    @blk.tensor
    def _(t):
        wt(t, vsem, vval)
        wt(t, cs["ini"], cs["ini"].n)
        wt(t, cs["ld"], cs["_rw_tok"])
        for i in range(16):
            if i >= 1:
                wt(t, ve, rt_ve0 + 9 * (i - 1) + 6)
            for k in range(DC):
                mm = t.matmul(PSQ[0][:, 0:7], xln[:, k, 128 * i:128 * i + 128],
                              p.t["rw_sb"][:, k, :], start=(k == 0), stop=(k == DC - 1))
            inc(pe, mm)
            if i >= 1:
                _emit_transpose(t, i - 1)
        _emit_transpose(t, 15)

    @blk.vector
    def _(v):
        for i in range(16):
            wt(v, pe, rt_pe0 + i + 1)
            i1 = v.reduce_max(RT[:, 0:1], PSQ[0][:, 0:7], axis=mybir.AxisListType.X)
            inc(ve, i1); wt(v, ve)
            i1 = v.tensor_scalar(RT[:, 8:15], PSQ[0][:, 0:7], RT[:, 0:1], None, op0=ALU.is_equal)
            inc(ve, i1); wt(v, ve)
            i1 = v.scalar_tensor_tensor(RT[:, 16:23], RT[:, 8:15], BIGNEG, PSQ[0][:, 0:7],
                                        op0=ALU.mult, op1=ALU.add)
            inc(ve, i1); wt(v, ve)
            i1 = v.reduce_max(RT[:, 1:2], RT[:, 16:23], axis=mybir.AxisListType.X)
            inc(ve, i1); wt(v, ve)
            i1 = v.tensor_tensor(RT[:, 2:3], RT[:, 0:1], RT[:, 1:2], op=ALU.subtract)
            inc(ve, i1); wt(v, ve)
            i1 = v.tensor_scalar(RT[:, 16:23], RT[:, 16:23], RT[:, 1:2], None, op0=ALU.is_equal)
            inc(ve, i1); wt(v, ve)
            wt(v, se, rt_se0 + 3 * (i + 1))
            i1 = v.tensor_scalar(RT[:, 16:23], RT[:, 16:23], RT[:, 4:5], None, op0=ALU.mult)
            inc(ve, i1); wt(v, ve)
            i1 = v.scalar_tensor_tensor(CTIN[:, 7 * (i % 2):7 * (i % 2) + 7],
                                        RT[:, 8:15], RT[:, 3:4], RT[:, 16:23],
                                        op0=ALU.mult, op1=ALU.add)
            inc(ve, i1); wt(v, ve)
            wt(v, pe, rt_pe_transpose(i))
            psq1_bf = PSQ[1][:, :].bitcast(BF16)
            i1 = v.tensor_copy(CT[0:7, i, :], psq1_bf[0:7, 256 * (i % 4):256 * (i % 4) + 128])
            inc(ve, i1); wt(v, ve)
    assert ve.n == ct_done, (ve.n, ct_done)

    @blk.scalar
    def _(s):
        for i in range(16):
            wt(s, ve, rt_ve0 + 9 * i + 5)
            i1 = s.activation(RT[:, 3:4], RT[:, 2:3], AF.Sigmoid)
            inc(se, i1); wt(s, se)
            i1 = s.activation(RT[:, 3:4], RT[:, 3:4], AF.Copy, scale=0.5)
            inc(se, i1); wt(s, se)
            i1 = s.activation(RT[:, 4:5], RT[:, 3:4], AF.Copy, scale=-1.0, bias=0.5)
            inc(se, i1); wt(s, se)

    @blk.gpsimd
    def _(g):
        wt(g, ve, ct_done)
        i = g.dma_start(out=p.t["c_dram"][0:7, :], in_=CT[0:7, :, :])
        inc(ld, i, 16)
    c_dram_tok = ld.n

    pe0 = pe.n
    se0 = se.n
    ve0 = ve.n

    def pe_gu(git):
        return pe0 + (1 if git == 0 else 2 * git)

    def pe_down(git):
        return pe0 + (2 * git + 3 if git < NITALL - 1 else 2 * NITALL)

    @blk.gpsimd
    def _(g):
        for gs in range(MOE_NE * MOE_NS):
            e, s_ = divmod(gs, MOE_NS)
            b = gs % 2
            ws = wA if b == 0 else wB
            if gs >= 2:
                wt(g, pe, pe_down(8 * (gs - 2) + 7))
            i = g.dma_start(out=WG[:, b, :, :],
                            in_=p.t["moe_g"][e, s_].rearrange("(c p) h -> p c h", p=128))
            inc(ws, i, 16)
            i = g.dma_start(out=WU[:, b, :, :],
                            in_=p.t["moe_u"][e, s_].rearrange("(c p) h -> p c h", p=128))
            inc(ws, i, 16)
            i = g.dma_start(out=WD[:, b, :, :],
                            in_=p.t["moe_d"][e, 512 * s_:512 * s_ + 512, :].rearrange("(c p) d -> p c d", p=128))
            inc(ws, i, 16)

    @blk.sync
    def _(y):
        for e in range(MOE_NE):
            wt(y, ld, c_dram_tok)
            if e >= 2:
                wt(y, ve, ve0 + 3 * NIT * (e - 1))
            srow = p.t["c_dram"][e:e + 1, :] if e < MOE_NE - 1 else p.t["half_dram"][0:1, :]
            i = y.dma_start(out=CBC[:, e % 2, :], in_=srow.to_broadcast((128, T)))
            inc(ldb if e % 2 == 0 else cs["ldb2"], i, 16)

    def emit_gu(t, git):
        e, it = divmod(git, NIT)
        s_, tb = divmod(it, MOE_NB)
        gs = e * MOE_NS + s_
        b = gs % 2
        if it == 0 and tb == 0 and git == 0:
            wt(t, ve, ct_done)
        if tb == 0:
            wt(t, wA if b == 0 else wB, 48 * (gs // 2 + 1))
        if git >= 1:
            wt(t, ve, ve0 + 3 * (git - 1) + 1)
        for hk in range(4):
            for k in range(DC):
                mm = t.matmul(PSQ[0][:, 256 * hk:256 * hk + 256],
                              WG[:, b, k, 128 * hk:128 * hk + 128],
                              xln[:, k, 256 * tb:256 * tb + 256],
                              start=(k == 0 and hk % 2 == 0), stop=(k == 7),
                              skip_group_check=(hk % 2 == 1))
        for hk in range(4):
            for k in range(DC):
                mm = t.matmul(PSQ[1][:, 256 * hk:256 * hk + 256],
                              WU[:, b, k, 128 * hk:128 * hk + 128],
                              xln[:, k, 256 * tb:256 * tb + 256],
                              start=(k == 0 and hk % 2 == 0), stop=(k == 7),
                              skip_group_check=(hk % 2 == 1))
        inc(pe, mm)

    def emit_down(t, git):
        e, it = divmod(git, NIT)
        s_, tb = divmod(it, MOE_NB)
        b = (e * MOE_NS + s_) % 2
        wt(t, ve, ve0 + 3 * git + 2)
        for m in range(8):
            q = PSQ[2 + m // 4]
            co = 256 * (m % 4)
            for k in range(4):
                mm = t.matmul(q[:, co:co + 256],
                              WD[:, b, k, 128 * m:128 * m + 128],
                              HSB[:, git % 2, 256 * k:256 * k + 256],
                              start=(k == 0 and m % 2 == 0), stop=(k == 3),
                              skip_group_check=(m % 2 == 1))
        inc(pe, mm)

    @blk.tensor
    def _(t):
        emit_gu(t, 0)
        for git in range(1, NITALL):
            emit_gu(t, git)
            emit_down(t, git - 1)
        emit_down(t, NITALL - 1)

    @blk.scalar
    def _(s):
        for git in range(NITALL):
            wt(s, pe, pe_gu(git))
            if git >= 1:
                wt(s, ve, ve0 + 3 * (git - 1) + 2)
            i = s.activation(T1[:, :], PSQ[0][:, :], AF.Silu)
            inc(se, i)

    @blk.vector
    def _(v):
        for git in range(NITALL):
            e, it = divmod(git, NIT)
            tb = it % MOE_NB
            wt(v, se, se0 + git + 1)
            if it == 0:
                wt(v, ldb if e % 2 == 0 else cs["ldb2"], 16 * (e // 2 + 1))
            i = v.tensor_tensor(T1[:, :], T1[:, :], PSQ[1][:, :], op=ALU.mult)
            inc(ve, i)
            wt(v, ve)
            for hk in range(4):
                i = v.tensor_tensor(HSB[:, git % 2, 256 * hk:256 * hk + 256],
                                    T1[:, 256 * hk:256 * hk + 256],
                                    CBC[:, e % 2, 256 * tb:256 * tb + 256], op=ALU.mult)
            inc(ve, i)
            wt(v, ve)
            wt(v, pe, pe_down(git))
            for m in range(8):
                q = PSQ[2 + m // 4]
                co = 256 * (m % 4)
                i = v.tensor_tensor(resid[:, m, 256 * tb:256 * tb + 256],
                                    resid[:, m, 256 * tb:256 * tb + 256],
                                    q[:, co:co + 256], op=ALU.add)
            inc(ve, i)
            wt(v, ve)
    return (ve, ve.n)


def build_prog_moe():
    p = Prog("moe")
    p.din("resid_in", [D, T], F32)
    p.din("router_w", [128, DC, 7], F32)
    p.din("moe_g", [MOE_NE, MOE_NS, D, 512], F32)
    p.din("moe_u", [MOE_NE, MOE_NS, D, 512], F32)
    p.din("moe_d", [MOE_NE, HID, D], F32)
    p.din("half_dram", [1, T], BF16)
    p.dout("resid_out", [D, T], F32)
    p.dtmp("c_dram", [7, T], BF16)

    p.sb("RES", [128, DC, T], F32)
    p.sb("XLN", [128, DC, T], BF16)
    p.sb("WG", [128, 2, DC, 512], BF16)
    p.sb("WU", [128, 2, DC, 512], BF16)
    p.sb("WD", [128, 2, 4, D], BF16)
    p.sb("CBC", [128, 2, T], BF16)
    p.sb("T1", [128, 1024], F32)
    p.sb("HSB", [128, 2, 1024], BF16)
    p.sb("RT", [128, 24], F32)
    p.sb("CT", [8, 16, 128], BF16)
    p.sb("CTIN", [128, 14], BF16)
    p.sb("rw_sb", [128, DC, 7], BF16)
    p.sb("idmat", [128, 128], BF16)
    p.sb("SQC", [128, 2, 2048], BF16)
    p.sb("F32S", [128, T], F32)
    p.sb("ROWS", [1, 4096], F32)
    p.sb("ones1", [1, 128], BF16)
    p.sb("onesb", [1, 128], F32)
    p.sb("onec", [128, 1], BF16)
    p.sb("CONSTS", [1, 16], F32)
    for _q in range(4):
        p.ps(f"PSQ{_q}", [128, 1024], F32)

    register_const(p, LN_EPS)
    cs = {k: p.sem(k) for k in ("se", "ve", "pe", "po", "ld", "cc", "wA", "wB",
                                "ini", "ldr", "st", "ldb", "ldb2")}
    blk = p.block()

    @blk.sync
    def _(y):
        i = y.dma_start(out=p.t["RES"][:, :, :],
                        in_=p.t["resid_in"].rearrange("(c p) t -> p c t", p=128))
        inc(cs["ldr"], i, 16)
    cs["_ldr_all"] = cs["ldr"].n

    @blk.gpsimd
    def _(g):
        i = g.dma_start(out=p.t["rw_sb"][:, :, :], in_=p.t["router_w"][:, :, :])
        inc(cs["ld"], i, 16)
    cs["_rw_tok"] = cs["ld"].n

    emit_consts(p, cs)
    tok = emit_ln(p, cs, p.t["RES"], p.t["XLN"], (cs["ldr"], cs["_ldr_all"]))
    tok = emit_moe(p, cs, p.t["RES"], p.t["XLN"], tok)

    @blk.sync
    def _(y):
        wt(y, tok[0], tok[1])
        i = y.dma_start(out=p.t["resid_out"].rearrange("(c p) t -> p c t", p=128),
                        in_=p.t["RES"][:, :, :])
        inc(cs["st"], i, 16)
        wt(y, cs["st"])
    p.close()
    return p


def stage_moe_core(inp, c, resid):
    f32 = np.float32
    wg = np.asarray(inp["L1_ffn_w_gate"], f32)
    wu = np.asarray(inp["L1_ffn_w_up"], f32)
    wd = np.asarray(inp["L1_ffn_w_down"], f32)
    swg = np.asarray(inp["L1_ffn_sw_gate"], f32)
    swu = np.asarray(inp["L1_ffn_sw_up"], f32)
    swd = np.asarray(inp["L1_ffn_sw_down"], f32)

    def slice_gu(w):
        return np.ascontiguousarray(w.reshape(D, MOE_NS, 512).transpose(1, 0, 2))
    g_all = np.stack([slice_gu(wg[e]) for e in range(7)] + [slice_gu(swg[0])])
    u_all = np.stack([slice_gu(wu[e]) for e in range(7)] + [slice_gu(swu[0])])
    d_all = np.ascontiguousarray(np.stack(list(wd) + [swd[0]]))
    rw = np.ascontiguousarray(np.asarray(inp["L1_ffn_router_w"], f32).T.reshape(DC, 128, 7).transpose(1, 0, 2))
    d = {
        "resid_in": np.ascontiguousarray(resid[T * c:T * (c + 1)].T),
        "router_w": rw,
        "moe_g": g_all,
        "moe_u": u_all,
        "moe_d": d_all,
        "half_dram": np.full((1, T), 0.5, BF),
    }
    return d



NJC = 5632


def qblocks_of_core(c):
    return [c, 15 - c, 16 + c, 31 - c]


def jtiles_needed(b):
    jmax = (512 * (b + 1) - 1) // 3
    return (jmax + 2 + 127) // 128


SLOT_TILES = [max(jtiles_needed(qblocks_of_core(c)[s]) for c in range(NCORE))
              for s in range(4)]


def first_masked_tile(b):
    t = 0
    while 512 * b >= 3 * (128 * (t + 1) - 2) and t + 1 < jtiles_needed(b):
        t += 1
    return t


SLOT_MASK0 = [min(first_masked_tile(qblocks_of_core(c)[s]) for c in range(NCORE))
              for s in range(4)]
SWEEPS = [[0, 1, 2], [3, 4, 5], [6, 7]]
NMASK = sum(SLOT_TILES[s] - SLOT_MASK0[s] for s in range(4))
CONVB = [(512 * B, min(512, 5462 - 512 * B)) for B in range(11)]
ATT_PHASE = 9


def _win_segs(t0, length):
    segs = []
    g = max(t0, 0)
    while g < t0 + length:
        r = g // 2048
        hi = min(t0 + length, 2048 * (r + 1))
        segs.append((r, g % 2048, hi - g, g - t0))
        g = hi
    return segs


def emit_l0attn(p, cs, resid, xln, xln_tok):
    blk = p.blk
    se, ve, pe, ld, cc = cs["se"], cs["ve"], cs["pe"], cs["ld"], cs["cc"]
    ldy = cs["ldy"]
    A1 = p.t["A1"]
    A2 = p.t["A2"]
    WS = p.t["WS2"]
    KVB = p.t["KVB"]
    WCONV = p.t["wconv_sb"]
    F32S = p.t["F32S"]
    OB = [p.t[f"OB{i}"] for i in range(6)]
    SCB = p.t["SCB"]
    MXC = p.t["MXC"]
    MXB = p.t["MXB"]
    NEGB = p.t["NEGB"]
    VB = p.t["VB"]
    KSL = p.t["KSL"]
    VSL = p.t["VSL"]
    BCBV = p.t["BCBV"]
    vsem, vval = xln_tok
    KWIN = A1[:, :, 0:1544]
    QS = A2[:, :, 0:512]
    PT0 = A2[:, 0:2, 512:1536]
    banks = [SCB[:, 0:512], SCB[:, 512:1024]] + [OB[i][:, :] for i in range(6)]

    pe_p0 = pe.n
    se_p0 = se.n
    ldy_p0 = ldy.n

    @blk.gpsimd
    def _(g):
        for mi in range(3):
            if mi > 0:
                wt(g, pe, pe_p0 + 4 * mi)
            i = g.dma_start(out=WS[:, :, :],
                            in_=p.t["w_qkv0"][mi].rearrange("(c p) o -> p c o", p=128))
            inc(cs["wA"], i, 16)

    @blk.tensor
    def _(t):
        wt(t, vsem, vval)
        for mi in range(3):
            wt(t, cs["wA"], 16 * (mi + 1))
            for j in range(4):
                if not (mi == 0 and j == 0):
                    wt(t, se, se_p0 + 8 * (4 * mi + j))
                for m in range(DC):
                    for k in range(DC):
                        mm = t.matmul(banks[m][:, :], WS[:, k, 128 * m:128 * m + 128],
                                      xln[:, k, 512 * j:512 * j + 512],
                                      start=(k == 0), stop=(k == DC - 1),
                                      skip_group_check=True)
                inc(pe, mm)

    @blk.scalar
    def _(s):
        for mi in range(3):
            for j in range(4):
                wt(s, pe, pe_p0 + 4 * mi + j + 1)
                if mi > 0:
                    if j >= 2:
                        mip, jp = mi, j - 2
                    elif mi == 2:
                        mip, jp = 1, j + 2
                    else:
                        mip, jp = 0, 0
                    if mip > 0:
                        order = 2 * (mip - 1) + jp // 2 + 1
                        wt(s, ldy if jp % 2 == 0 else cs["ldyB"],
                           ldy_p0 + 16 * order)
                for m in range(DC):
                    if mi == 0:
                        i = s.activation(A2[:, m, 512 * j:512 * j + 512], banks[m][:, :], AF.Copy)
                    else:
                        i = s.activation(KVB[:, j % 2, m, :], banks[m][:, :], AF.Copy)
                    inc(se, i)
                    wt(s, se)

    @blk.sync
    def _(y):
        for mi in range(1, 3):
            cio = p.t["cc_k_in"] if mi == 1 else p.t["cc_v_in"]
            for j in range(4):
                wt(y, se, se_p0 + 8 * (4 * mi + j + 1))
                i = y.dma_start(out=cio[:, 512 * j:512 * j + 512]
                                .rearrange("(c p) t -> p c t", p=128),
                                in_=KVB[:, j % 2, :, :])
                inc(ldy if j % 2 == 0 else cs["ldyB"], i, 16)
    kv_in_tokA = ldy.n
    kv_in_tokB = cs["ldyB"].n

    if ATT_PHASE < 2:
        return (se, se_p0 + 96)

    @blk.vector
    def _(v):
        wt(v, se, se_p0 + 8 * 4)
        for m in range(DC):
            i = v.reduce_max(MXC[:, m:m + 1], A2[:, m, :], axis=mybir.AxisListType.X,
                             apply_absolute_value=True)
            inc(ve, i); wt(v, ve)
        i = v.tensor_copy(MXB[:, 0:8], MXC[:, 0:8])
        inc(ve, i); wt(v, ve)
    qmx_ve = ve.n

    @blk.gpsimd
    def _(g):
        wt(g, se, se_p0 + 8 * 4)
        i = g.dma_start(out=p.t["cc_q_in"].rearrange("(c p) t -> p c t", p=128),
                        in_=A2[:, :, :])
        inc(ld, i, 16)
    q_in_tok = ld.n

    pe_m0 = pe.n

    @blk.tensor
    def _(t):
        wt(t, ve, qmx_ve)
        wt(t, se, se_p0 + 8 * 12)
        scb_bf = SCB[:, :].bitcast(BF16)
        mm = t.transpose(scb_bf[0:8, 0:128], MXB[:, 0:8], p.t["idmat"][:, :])
        inc(pe, mm)

    @blk.vector
    def _(v):
        wt(v, pe, pe_m0 + 1)
        scb_bf = SCB[:, :].bitcast(BF16)
        i = v.reduce_max(p.t["QPM"][0:8, 0:1],
                         scb_bf[0:8, 0:128].rearrange("a (b c) -> a b c", b=2),
                         axis=mybir.AxisListType.XY)
        inc(ve, i); wt(v, ve)
    qpm_ve = ve.n

    @blk.gpsimd
    def _(g):
        wt(g, ld, q_in_tok)
        wt(g, ldy, kv_in_tokA)
        wt(g, cs["ldyB"], kv_in_tokB)
        for nm in ("q", "k", "v"):
            onm = "cc_vout" if nm == "v" else f"cc_{nm}_out"
            i = g.collective_compute(
                "AllGather", ALU.bypass,
                ins=[p.t[f"cc_{nm}_in"][:, :]], outs=[p.t[onm][:, :]],
                replica_groups=[list(range(NCORE))])
            inc(cc, i)
        wt(g, ve, qpm_ve)
        i = g.dma_start(out=p.t["cc_m_in"][:, :], in_=p.t["QPM"][0:8, 0:1])
        inc(ld, i, 16)
        wt(g, ld)
        i = g.collective_compute(
            "AllReduce", ALU.max,
            ins=[p.t["cc_m_in"][:, :]], outs=[p.t["cc_m_out"][:, :]],
            replica_groups=[list(range(NCORE))])
        inc(cc, i)
        wt(g, cc)
        i = g.dma_start(out=p.t["QPM"][0:8, 0:1], in_=p.t["cc_m_out"][:, :])
        inc(ld, i, 16)
        i = g.dma_start(out=p.t["kj_dram"][:, 0:1].rearrange("(c p) w -> p c w", p=128),
                        in_=p.t["nullk_sb"][:, :].unsqueeze(2),
                        allow_slow_non_contiguous=True)
        inc(ld, i, 16)
        i = g.memset(KVB[:, 0, :, :], 0.0)
        inc(cs["po"], i)
        wt(g, cs["po"])
        kvb_flat = KVB[:, 0, :, :].rearrange("p a b -> p (a b)")
        i = g.dma_start(out=p.t["kj_dram"][:, 5463:NJC].rearrange("(c p) w -> p c w", p=128),
                        in_=KVB[:, 0, :, 0:169])
        inc(ld, i, 16)
        i = g.dma_start(out=p.t["vj_dram"][5463:5591, :], in_=kvb_flat[:, 0:1040])
        inc(ld, i, 16)
        i = g.dma_start(out=p.t["vj_dram"][5591:NJC, :], in_=kvb_flat[0:41, 0:1040])
        inc(ld, i, 16)
        i = g.dma_start(out=p.t["vj_dram"][0:1, :], in_=p.t["nvr_sb"][0:1, :])
        inc(ld, i, 16)
    ag1_cc = cc.n
    fills_tok = ld.n

    if ATT_PHASE < 3:
        return (ld, fills_tok)

    pe_b0 = pe.n

    @blk.tensor
    def _(t):
        wt(t, cs["ldr"], cs["_ldr_all"])
        wt(t, pe, pe_m0 + 1)
        wt(t, ve, qpm_ve)
        for jj in range(2):
            mm = t.matmul(SCB[:, 512 * jj:512 * jj + 512], p.t["onesb"][0:1, :],
                          p.t["bcvrow_sb"][0:1, 512 * jj:512 * jj + 512],
                          start=True, stop=True, skip_group_check=True)
        inc(pe, mm)

    @blk.vector
    def _(v):
        wt(v, pe, pe_b0 + 1)
        i = v.tensor_copy(BCBV[:, :], SCB[:, :])
        inc(ve, i); wt(v, ve)
        i = v.memset(MXC[:, 8:16], 0.0)
        inc(ve, i); wt(v, ve)
        for h in range(16):
            for par in range(2):
                i = v.memset(VB[:, par, 65 * h + 64:65 * h + 65], 1.0)
        inc(ve, i); wt(v, ve)
    bcbv_ve = ve.n

    pe_c0 = pe.n
    VT = [min(4, (CONVB[B][1] + 127) // 128) for B in range(11)]
    vconv_end = pe_c0 + 88 + sum(VT)

    win_ldy = {}
    run = cs["ldw"].n
    for which in range(2):
        for B in range(11):
            run += 16 * len(_win_segs(3 * CONVB[B][0] - 2, 3 * CONVB[B][1]))
            win_ldy[(which, B)] = run

    def win_gate_pe(which, B):
        g = 11 * which + B
        if g == 0:
            return 0
        gp = g - 1
        if gp < 11:
            return pe_c0 + 8 * (gp + 1)
        return pe_c0 + 88 + sum(VT[:gp - 11 + 1])

    if ATT_PHASE < 4:
        return (ve, bcbv_ve)

    @blk.sync
    def _(y):
        for which in range(2 if ATT_PHASE >= 5 else 1):
            gsrc = p.t["cc_k_out"] if which == 0 else p.t["cc_vout"]
            for B in range(11):
                bk0, nb = CONVB[B]
                t0 = 3 * bk0 - 2
                wt(y, cc, ag1_cc)
                gate = win_gate_pe(which, B)
                if gate:
                    wt(y, pe, gate)
                else:
                    wt(y, ve, bcbv_ve)
                for (r, col, n, off) in _win_segs(t0, 3 * nb):
                    i = y.dma_start(
                        out=KWIN[:, :, off:off + n],
                        in_=gsrc[1024 * r:1024 * (r + 1), col:col + n]
                        .rearrange("(c p) t -> p c t", p=128))
                    inc(cs["ldw"], i, 16)

    @blk.vector
    def _(v):
        wt(v, ve, bcbv_ve)
        i = v.memset(KWIN[:, :, 0:2], 0.0)
        inc(ve, i); wt(v, ve)
    b0z_ve = ve.n
    kc_ve0 = ve.n
    def kc_ve(B, hp):
        return kc_ve0 + 3 * (8 * B + hp + 1)

    @blk.tensor
    def _(t):
        for B in range(11):
            bk0, nb = CONVB[B]
            wt(t, cs["ldw"], win_ldy[(0, B)])
            if B == 0:
                wt(t, ve, b0z_ve)
            for hp in range(8):
                g = 8 * B + hp
                half = SCB[:, 512 * (g % 2):512 * (g % 2) + 512]
                if g >= 2:
                    wt(t, ve, kc_ve0 + 3 * (g - 1))
                for hh in range(2):
                    base = 64 * hh
                    for tau in range(3):
                        mm = t.matmul(
                            half[base:base + 64, 0:nb],
                            WCONV[base:base + 64, tau, hp, base:base + 64],
                            KWIN[base:base + 64, hp, tau:tau + 3 * nb:3],
                            start=(tau == 0 and hh == 0), stop=(tau == 2),
                            skip_group_check=(hh == 1),
                            tile_position=(base, base))
                inc(pe, mm)

    @blk.vector
    def _(v):
        wt(v, ld, fills_tok)
        for B in range(11):
            bk0, nb = CONVB[B]
            for hp in range(8):
                g = 8 * B + hp
                wt(v, pe, pe_c0 + g + 1)
                if g >= 2:
                    wt(v, cs["lkA"] if g % 2 == 0 else cs["lkB"],
                       16 * ((g - 2) // 2 + 1))
                half = SCB[:, 512 * (g % 2):512 * (g % 2) + 512]
                i = v.tensor_scalar(KVB[:, g % 2, hp, 0:nb], half[:, 0:nb],
                                    p.t["bconv_sb"][:, hp:hp + 1], None, op0=ALU.add)
                inc(ve, i); wt(v, ve)
                i = v.reduce_max(MXC[:, 8 + hp:9 + hp] if False else F32S[:, 0:1],
                                 KVB[:, g % 2, hp, 0:nb], axis=mybir.AxisListType.X,
                                 apply_absolute_value=True)
                inc(ve, i); wt(v, ve)
                i = v.tensor_tensor(MXC[:, 8 + hp:9 + hp], MXC[:, 8 + hp:9 + hp],
                                    F32S[:, 0:1], op=ALU.max)
                inc(ve, i); wt(v, ve)

    @blk.gpsimd
    def _(g_):
        for B in range(11):
            bk0, nb = CONVB[B]
            for hp in range(8):
                g = 8 * B + hp
                wt(g_, ve, kc_ve0 + 3 * g + 1)
                i = g_.dma_start(
                    out=p.t["kj_dram"][128 * hp:128 * (hp + 1), 1 + bk0:1 + bk0 + nb],
                    in_=KVB[:, g % 2, hp, 0:nb])
                inc(cs["lkA"] if g % 2 == 0 else cs["lkB"], i, 16)
    kst_endA = cs["lkA"].n
    kst_endB = cs["lkB"].n

    if ATT_PHASE < 5:
        return (ve, kc_ve0 + 3 * 88)

    NTV = sum(VT)
    vc_pe0 = pe.n
    vc_ve0 = kc_ve0 + 3 * 88

    @blk.tensor
    def _(t):
        n = 0
        for B in range(11):
            bk0, nb = CONVB[B]
            wt(t, cs["ldw"], win_ldy[(1, B)])
            for tile in range(VT[B]):
                jt0 = 128 * tile
                jn = min(128, nb - jt0)
                if n == 0:
                    wt(t, ve, kc_ve0 + 3 * 88)
                if n >= 1:
                    wt(t, ve, vc_ve0 + n)
                w0 = 3 * jt0
                for hp in range(8):
                    for tau in range(3):
                        mm = t.matmul(
                            SCB[0:jn, 128 * hp:128 * hp + 128],
                            KWIN[0:128, hp, tau + w0:tau + w0 + 3 * jn:3],
                            WCONV[0:128, tau, hp, :],
                            start=(tau == 0), stop=(tau == 2),
                            skip_group_check=(hp % 4 != 0))
                inc(pe, mm)
                n += 1

    @blk.vector
    def _(v):
        n = 0
        for B in range(11):
            bk0, nb = CONVB[B]
            for tile in range(VT[B]):
                jn = min(128, nb - 128 * tile)
                wt(v, pe, vc_pe0 + n + 1)
                if n >= 2:
                    wt(v, cs["lvA"] if n % 2 == 0 else cs["lvB"],
                       16 * ((n - 2) // 2 + 1))
                i = v.tensor_tensor(
                    VB[0:jn, n % 2, :].rearrange("p (h c) -> p h c", h=16)[:, :, 0:64],
                    SCB[0:jn, :].rearrange("p (h c) -> p h c", h=16),
                    BCBV[0:jn, :].rearrange("p (h c) -> p h c", h=16),
                    op=ALU.add)
                inc(ve, i); wt(v, ve)
                n += 1

    @blk.gpsimd
    def _(g_):
        n = 0
        for B in range(11):
            bk0, nb = CONVB[B]
            for tile in range(VT[B]):
                jt0 = 128 * tile
                jn = min(128, nb - jt0)
                wt(g_, ve, vc_ve0 + n + 1)
                i = g_.dma_start(out=p.t["vj_dram"][1 + bk0 + jt0:1 + bk0 + jt0 + jn, :],
                                 in_=VB[0:jn, n % 2, :])
                inc(cs["lvA"] if n % 2 == 0 else cs["lvB"], i, 16)
                n += 1
    vst_endA = cs["lvA"].n
    vst_endB = cs["lvB"].n

    if ATT_PHASE < 6:
        return (ve, vc_ve0 + NTV)

    kconv_ve_end = kc_ve0 + 3 * 88

    @blk.vector
    def _(v):
        wt(v, ve, kconv_ve_end)
        i = v.tensor_copy(MXB[:, 8:16], MXC[:, 8:16])
        inc(ve, i); wt(v, ve)
    kmb_ve = ve.n

    pe_k0 = pe.n

    @blk.tensor
    def _(t):
        wt(t, ve, kmb_ve)
        wt(t, pe, vc_pe0 + NTV)
        wt(t, ve, vc_ve0 + NTV)
        scb_bf = SCB[:, :].bitcast(BF16)
        mm = t.transpose(scb_bf[0:8, 0:128], MXB[:, 8:16], p.t["idmat"][:, :])
        inc(pe, mm)

    @blk.vector
    def _(v):
        wt(v, pe, pe_k0 + 1)
        scb_bf = SCB[:, :].bitcast(BF16)
        i = v.reduce_max(p.t["QPM"][0:8, 1:2],
                         scb_bf[0:8, 0:128].rearrange("a (b c) -> a b c", b=2),
                         axis=mybir.AxisListType.XY)
        inc(ve, i); wt(v, ve)
        wt(v, ld, fills_tok)
        i = v.scalar_tensor_tensor(p.t["QPM"][0:8, 2:3], p.t["QPM"][0:8, 0:1], -2.0,
                                   p.t["QPM"][0:8, 1:2], op0=ALU.mult, op1=ALU.mult)
        inc(ve, i); wt(v, ve)
        i = v.tensor_copy(p.t["QPMB"][0:8, 0:1], p.t["QPM"][0:8, 2:3])
        inc(ve, i); wt(v, ve)
    negv_ve = ve.n

    pe_k1 = pe.n

    @blk.tensor
    def _(t):
        wt(t, ve, negv_ve)
        scb_bf = SCB[:, :].bitcast(BF16)
        mm = t.transpose(scb_bf[0:1, 0:8], p.t["QPMB"][0:8, 0:1], p.t["idmat"][0:8, 0:8])
        inc(pe, mm)

    @blk.vector
    def _(v):
        wt(v, pe, pe_k1 + 1)
        scb_bf = SCB[:, :].bitcast(BF16)
        i = v.tensor_copy(p.t["NEGR"][0:1, 0:8], scb_bf[0:1, 0:8])
        inc(ve, i); wt(v, ve)
    negr_ve = ve.n

    pe_k2 = pe.n

    @blk.tensor
    def _(t):
        wt(t, ve, negr_ve)
        mm = t.matmul(SCB[:, 0:8], p.t["ones1"][0:1, :], p.t["NEGR"][0:1, 0:8],
                      start=True, stop=True, skip_group_check=True)
        inc(pe, mm)

    @blk.vector
    def _(v):
        wt(v, pe, pe_k2 + 1)
        i = v.tensor_copy(NEGB[:, 0:8], SCB[:, 0:8])
        inc(ve, i); wt(v, ve)
    negb_ve = ve.n

    if ATT_PHASE < 7:
        return (ve, negb_ve)

    mtab = {}
    mi = 0
    for s in range(4):
        for tj in range(SLOT_MASK0[s], SLOT_TILES[s]):
            mtab[(s, tj)] = mi
            mi += 1

    att_pe0 = pe.n
    att_se0 = se.n
    att_po0 = cs["po"].n
    att_ldq0 = cs["ldq"].n
    att_ve0 = ve.n
    div_recip = {}
    div_copy = {}
    div_ve = {}
    _vn = att_ve0
    for _s in range(4):
        for _w, _pairs in enumerate(SWEEPS):
            for _hi in range(2 * len(_pairs)):
                div_recip[(_s, _w, _hi)] = _vn + 1
                div_copy[(_s, _w, _hi)] = _vn + 2
                _vn += 3
            div_ve[(_s, _w)] = _vn
    att_div_end_pred = _vn

    pe_pos = {}
    se_pos = {}
    po_pos = {}
    ldy_pos = {}
    pe_n = att_pe0
    se_n = att_se0
    po_n = att_po0
    ls_n = [cs["lsA"].n, cs["lsB"].n]
    for s in range(4):
        for w, pairs in enumerate(SWEEPS):
            P = len(pairs)
            for tj in range(SLOT_TILES[s]):
                ls_n[tj % 2] += 16 * (P + 1)
                ldy_pos[(s, w, tj)] = ls_n[tj % 2]
                for ci in range(P):
                    pe_n += 1
                    pe_pos[(s, w, tj, ci, 0)] = pe_n
                    se_n += 1
                    se_pos[(s, w, tj, ci)] = se_n
                    if tj >= SLOT_MASK0[s]:
                        po_n += 2
                        po_pos[(s, w, tj, ci)] = po_n
                    pe_n += 1
                    pe_pos[(s, w, tj, ci, 1)] = pe_n
            for hi in range(2 * P):
                pe_n += 1
                pe_pos[(s, w, hi, -1, 2)] = pe_n
    att_pe_end = pe_n

    def prev_q(s, w, tj, ci):
        P = len(SWEEPS[w])
        q = tj * P + ci
        if q == 0:
            return None
        return ((q - 1) // P, (q - 1) % P)

    @blk.sync
    def _(y):
        for s in range(4):
            for w, pairs in enumerate(SWEEPS):
                P = len(pairs)
                for tj in range(SLOT_TILES[s]):
                    if tj >= 2:
                        wt(y, pe, pe_pos[(s, w, tj - 2, P - 1, 1)])
                    elif (s, w) != (0, 0):
                        pw = (s, w - 1) if w > 0 else (s - 1, len(SWEEPS) - 1)
                        lp = len(SWEEPS[pw[1]]) - 1
                        wt(y, pe, pe_pos[(pw[0], pw[1], SLOT_TILES[pw[0]] - 1, lp, 1)])
                    else:
                        wt(y, cs["lvA"], vst_endA)
                        wt(y, cs["lvB"], vst_endB)
                        wt(y, cs["lkA"], kst_endA)
                        wt(y, cs["lkB"], kst_endB)
                        wt(y, ve, negb_ve)
                    lsem = cs["lsA"] if tj % 2 == 0 else cs["lsB"]
                    for ci, hp in enumerate(pairs):
                        i = y.dma_start(out=KSL[:, tj % 2, ci, :],
                                        in_=p.t["kj_dram"][128 * hp:128 * (hp + 1),
                                                           128 * tj:128 * tj + 128])
                        inc(lsem, i, 16)
                    c0 = 65 * 2 * pairs[0]
                    cn = 65 * 2 * P
                    i = y.dma_start(out=VSL[:, tj % 2, 0:cn],
                                    in_=p.t["vj_dram"][128 * tj:128 * tj + 128, c0:c0 + cn])
                    inc(lsem, i, 16)
                    assert lsem.n == ldy_pos[(s, w, tj)]

    ldm_tab = {}
    _n = 0
    for s in range(4):
        for w in range(len(SWEEPS)):
            for tj in range(SLOT_MASK0[s], SLOT_TILES[s]):
                _n += 16
                ldm_tab[(s, w, tj)] = _n

    @blk.tensor
    def _(t):
        for s in range(4):
            wt(t, cs["ldq"], att_ldq0 + 16 * (s + 1))
            for w, pairs in enumerate(SWEEPS):
                P = len(pairs)
                for tj in range(SLOT_TILES[s]):
                    wt(t, cs["lsA"] if tj % 2 == 0 else cs["lsB"], ldy_pos[(s, w, tj)])
                    for ci, hp in enumerate(pairs):
                        pq = prev_q(s, w, tj, ci)
                        if pq is not None:
                            wt(t, se, se_pos[(s, w, pq[0], pq[1])])
                        elif (s, w) != (0, 0):
                            pw = (s, w - 1) if w > 0 else (s - 1, len(SWEEPS) - 1)
                            wt(t, se, se_pos[(pw[0], pw[1], SLOT_TILES[pw[0]] - 1,
                                              len(SWEEPS[pw[1]]) - 1)])
                            wt(t, ve, div_ve[pw])
                        for e in range(2):
                            for mh in (0, 64):
                                mm = t.matmul(SCB[mh:mh + 64, 512 * e:512 * e + 512],
                                              KSL[64 * e:64 * e + 64, tj % 2, ci,
                                                  mh:mh + 64],
                                              QS[64 * e:64 * e + 64, hp, :],
                                              start=True, stop=True,
                                              skip_group_check=True,
                                              tile_position=(64 * e, mh))
                        inc(pe, mm)
                        wt(t, se, se_pos[(s, w, tj, ci)])
                        if tj >= SLOT_MASK0[s]:
                            wt(t, cs["po"], po_pos[(s, w, tj, ci)])
                        if tj == 0 and (s, w) != (0, 0):
                            dv = div_ve[(s, w - 1) if w > 0 else (s - 1, len(SWEEPS) - 1)]
                            wt(t, ve, dv)
                        pt = PT0[:, (tj * P + ci) % 2, :]
                        for e in range(2):
                            obi = 2 * ci + e
                            mm = t.matmul(OB[obi][0:65, :],
                                          VSL[:, tj % 2, 65 * (2 * ci + e):65 * (2 * ci + e) + 65],
                                          pt[:, 512 * e:512 * e + 512],
                                          start=(tj == 0), stop=(tj == SLOT_TILES[s] - 1),
                                          skip_group_check=True)
                        inc(pe, mm)
                for ci in range(P):
                    for e in range(2):
                        hi = 2 * ci + e
                        h = 2 * pairs[ci] + e
                        obi = hi
                        wt(t, ve, div_recip[(s, w, hi)])
                        mm = t.matmul(SCB[0:64, 0:512], p.t["ONER"][64:65, 0:64],
                                      F32S[64:65, 0:512],
                                      start=True, stop=True, skip_group_check=True,
                                      tile_position=(64, 0))
                        inc(pe, mm)

    @blk.scalar
    def _(s_):
        for s in range(4):
            for w, pairs in enumerate(SWEEPS):
                P = len(pairs)
                for tj in range(SLOT_TILES[s]):
                    for ci, hp in enumerate(pairs):
                        q = tj * P + ci
                        wt(s_, pe, pe_pos[(s, w, tj, ci, 0)])
                        if q >= 2:
                            wt(s_, pe, pe_pos[(s, w, (q - 2) // P, (q - 2) % P, 1)])
                        pt = PT0[:, q % 2, :]
                        s_.activation(pt[:, 0:512], SCB[:, 0:512], AF.Exp,
                                      scale=1.0 / 32.0, bias=NEGB[:, hp:hp + 1])
                        i = s_.activation(pt[:, 512:1024], SCB[:, 512:1024], AF.Exp,
                                          scale=1.0 / 32.0, bias=NEGB[:, hp:hp + 1])
                        inc(se, i)

    @blk.gpsimd
    def _(g):
        wt(g, cc, ag1_cc)
        for s in range(4):
            if s > 0:
                lw = len(SWEEPS) - 1
                lt = SLOT_TILES[s - 1] - 1
                lc = len(SWEEPS[lw]) - 1
                wt(g, pe, pe_pos[(s - 1, lw, lt, lc, 0)])
            with g.register(f"qro{s}") as r1, g.register(f"qco{s}") as r2:
                g.reg_load(r1, p.t["idx_sb"][0:1, 8 + s:9 + s])
                g.reg_load(r2, p.t["idx_sb"][0:1, 12 + s:13 + s])
                o1 = g.snap(r1)
                o2 = g.snap(r2)
                i = g.dma_start(out=QS[:, :, :],
                                in_=p.t["cc_q_out"][bass.ds(o1, 1024), bass.ds(o2, 512)]
                                .rearrange("(c p) t -> p c t", p=128))
                inc(cs["ldq"], i, 16)
            for w, pairs in enumerate(SWEEPS):
                P = len(pairs)
                for tj in range(SLOT_MASK0[s], SLOT_TILES[s]):
                    pv2 = cs.setdefault("_msk_last", {}).get(tj % 2)
                    if pv2:
                        wt(g, cs["po"], pv2)
                    i = g.dma_start(out=p.t["MSKB"][:, tj % 2, :],
                                    in_=p.t["l0_masks"][mtab[(s, tj)], :, :])
                    inc(cs["ldm"], i, 16)
                    wt(g, cs["ldm"], cs["ldm"].n)
                    for ci, hp in enumerate(pairs):
                        q = tj * P + ci
                        wt(g, se, se_pos[(s, w, tj, ci)])
                        pt = PT0[:, q % 2, :]
                        i = g.tensor_tensor(pt[:, 0:512], pt[:, 0:512],
                                            p.t["MSKB"][:, tj % 2, :], op=ALU.mult)
                        inc(cs["po"], i)
                        i = g.tensor_tensor(pt[:, 512:1024], pt[:, 512:1024],
                                            p.t["MSKB"][:, tj % 2, :], op=ALU.mult)
                        inc(cs["po"], i)
                        assert cs["po"].n == po_pos[(s, w, tj, ci)]
                        cs["_msk_last"][tj % 2] = cs["po"].n

    @blk.vector
    def _(v):
        for s in range(4):
            for w, pairs in enumerate(SWEEPS):
                P = len(pairs)
                for ci in range(P):
                    for e in range(2):
                        hi = 2 * ci + e
                        h = 2 * pairs[ci] + e
                        obi = hi
                        wt(v, pe, pe_pos[(s, w, SLOT_TILES[s] - 1, ci, 1)])
                        i = v.reciprocal(F32S[64:65, 0:512], OB[obi][64:65, :])
                        inc(ve, i); wt(v, ve)
                        assert ve.n == div_recip[(s, w, hi)]
                        wt(v, pe, pe_pos[(s, w, hi, -1, 2)])
                        i = v.tensor_copy(F32S[0:64, 0:512], SCB[0:64, 0:512])
                        inc(ve, i); wt(v, ve)
                        i = v.tensor_tensor(A1[64 * (h % 2):64 * (h % 2) + 64, h // 2,
                                               512 * s:512 * s + 512],
                                            OB[obi][0:64, :], F32S[0:64, 0:512],
                                            op=ALU.mult)
                        inc(ve, i); wt(v, ve)
                assert ve.n == div_ve[(s, w)]
    att_div_end = att_div_end_pred

    if ATT_PHASE < 8:
        return (ve, att_div_end)

    @blk.gpsimd
    def _(g):
        wt(g, ve, att_div_end)
        i = g.dma_start(out=WS[:, :, :],
                        in_=p.t["w_out0"].rearrange("(c p) o -> p c o", p=128))
        inc(cs["wA"], i, 16)
    wout_tok = cs["wA"].n

    op_pe0 = pe.n
    op_ve0 = ve.n

    @blk.tensor
    def _(t):
        wt(t, cs["wA"], wout_tok)
        for s in range(4):
            if s >= 1:
                wt(t, ve, op_ve0 + 8 * s)
            for m in range(DC):
                for k in range(DC):
                    mm = t.matmul(banks[m][:, :], WS[:, k, 128 * m:128 * m + 128],
                                  A1[:, k, 512 * s:512 * s + 512],
                                  start=(k == 0), stop=(k == DC - 1),
                                  skip_group_check=True)
            inc(pe, mm)

    @blk.vector
    def _(v):
        for s in range(4):
            wt(v, pe, op_pe0 + s + 1)
            if s >= 1:
                wt(v, cs["lo"], 16 * s)
            for m in range(DC):
                i = v.tensor_scalar(A2[:, m, 1536:2048],
                                    banks[m][:, :], p.t["bout_sb"][:, m:m + 1], None,
                                    op0=ALU.add)
                inc(ve, i); wt(v, ve)
    @blk.gpsimd
    def _(g):
        for s in range(4):
            wt(g, ve, op_ve0 + 8 * (s + 1))
            i = g.dma_start(out=p.t["cc_o_in"][:, 512 * s:512 * s + 512]
                            .rearrange("(c p) t -> p c t", p=128),
                            in_=A2[:, :, 1536:2048])
            inc(cs["lo"], i, 16)
        wt(g, cs["lo"])
        i = g.collective_compute(
            "AllGather", ALU.bypass,
            ins=[p.t["cc_o_in"][:, :]], outs=[p.t["cc_o_out"][:, :]],
            replica_groups=[list(range(NCORE))])
        inc(cc, i)
        wt(g, cc)
        xadd0 = op_ve0 + 32
        for i4 in range(4):
            with g.register(f"xr{i4}") as r1, g.register(f"xc{i4}") as r2:
                g.reg_load(r1, p.t["idx_sb"][0:1, 16 + i4:17 + i4])
                g.reg_load(r2, p.t["idx_sb"][0:1, 20 + i4:21 + i4])
                o1 = g.snap(r1)
                o2 = g.snap(r2)
                if i4 >= 2:
                    wt(g, ve, xadd0 + 8 * (i4 - 1))
                i = g.dma_start(out=p.t["A2"][:, :, 512 * (i4 % 2):512 * (i4 % 2) + 512],
                                in_=p.t["cc_o_out"][bass.ds(o1, 1024), bass.ds(o2, 512)]
                                .rearrange("(c p) t -> p c t", p=128))
                inc(cs["lxA"] if i4 % 2 == 0 else cs["lxB"], i, 16)

    @blk.vector
    def _(v):
        for i4 in range(4):
            wt(v, cs["lxA"] if i4 % 2 == 0 else cs["lxB"], 16 * (i4 // 2 + 1))
            for m in range(DC):
                i = v.tensor_tensor(resid[:, m, 512 * i4:512 * i4 + 512],
                                    resid[:, m, 512 * i4:512 * i4 + 512],
                                    p.t["A2"][:, m, 512 * (i4 % 2):512 * (i4 % 2) + 512],
                                    op=ALU.add)
                inc(ve, i); wt(v, ve)
    return (ve, ve.n)


def build_prog_att():
    p = Prog("att")
    p.din("resid_in", [D, T], F32)
    p.din("w_qkv0", [3, D, D], F32)
    p.din("w_out0", [D, D], F32)
    p.din("wconv", [128, 3, 8, 128], F32)
    p.din("bconv", [128, DC], F32)
    p.din("bconv_vrow", [1, D], F32)
    p.din("bout", [128, DC], F32)
    p.din("null_k", [128, DC], F32)
    p.din("null_v_row", [1, 1040], F32)
    p.din("l0_masks", [NMASK, 128, 512], BF16)
    p.din("idx", [1, 32], I32)
    p.dout("resid_out", [D, T], F32)
    for nm, shp, dt in (("cc_q_in", [D, T], BF16), ("cc_k_in", [D, T], BF16),
                        ("cc_v_in", [D, T], BF16), ("cc_m_in", [8, 1], F32),
                        ("cc_o_in", [D, T], BF16)):
        p.dtmp(nm, shp, dt)
    p.dtmp("cc_q_out", [NCORE * D, T], BF16, shared=True)
    p.dtmp("cc_k_out", [NCORE * D, T], BF16, shared=True)
    p.dtmp("cc_vout", [NCORE * D, T], BF16, shared=True)
    p.dtmp("cc_m_out", [8, 1], F32, shared=True)
    p.dtmp("cc_o_out", [NCORE * D, T], BF16, shared=True)
    p.dtmp("kj_dram", [D, NJC], BF16)
    p.dtmp("vj_dram", [NJC, 1040], BF16)

    p.sb("RES", [128, DC, T], F32)
    p.sb("A1", [128, DC, T], BF16)
    p.sb("A2", [128, DC, T], BF16)
    p.sb("WS2", [128, DC, 1024], BF16)
    p.sb("KVB", [128, 2, DC, 512], BF16)
    p.sb("VB", [128, 2, 1040], BF16)
    p.sb("KSL", [128, 2, 3, 128], BF16)
    p.sb("VSL", [128, 2, 392], BF16)
    p.sb("MSKB", [128, 2, 512], BF16)
    p.sb("MXC", [128, 16], F32)
    p.sb("MXB", [128, 16], BF16)
    p.sb("NEGB", [128, 8], F32)
    p.sb("QPM", [128, 4], F32)
    p.sb("QPMB", [128, 1], BF16)
    p.sb("NEGR", [1, 8], BF16)
    p.sb("ONER", [128, 64], F32)
    p.sb("BCBV", [128, 1024], F32)
    p.sb("wconv_sb", [128, 3, 8, 128], BF16)
    p.sb("bconv_sb", [128, DC], F32)
    p.sb("bcvrow_sb", [1, 1024], F32)
    p.sb("bout_sb", [128, DC], F32)
    p.sb("nullk_sb", [128, DC], BF16)
    p.sb("nvr_sb", [1, 1040], BF16)
    p.sb("idmat", [128, 128], BF16)
    p.sb("F32S", [128, 1024], F32)
    p.sb("ROWS", [1, 4096], F32)
    p.sb("ones1", [1, 128], BF16)
    p.sb("onesb", [1, 128], F32)
    p.sb("onec", [128, 1], BF16)
    p.sb("CONSTS", [1, 16], F32)
    p.sb("idx_sb", [1, 32], I32)
    p.ps("SCB", [128, 1024], F32)
    for _i in range(6):
        p.ps(f"OB{_i}", [128, 512], F32)

    register_const(p, LN_EPS)
    cs = {k: p.sem(k) for k in ("se", "ve", "pe", "po", "ld", "cc", "wA", "wB",
                                "ini", "ldr", "st", "ldy", "ldq", "ldm",
                                "ldyB", "ldw", "lsA", "lsB", "lkA", "lkB", "lvA", "lvB", "lo", "lxA", "lxB")}
    blk = p.block()

    @blk.sync
    def _(y):
        i = y.dma_start(out=p.t["RES"][:, :, :],
                        in_=p.t["resid_in"].rearrange("(c p) t -> p c t", p=128))
        inc(cs["ldr"], i, 16)
        i = y.dma_start(out=p.t["bconv_sb"][:, :], in_=p.t["bconv"][:, :])
        inc(cs["ldr"], i, 16)
        i = y.dma_start(out=p.t["bcvrow_sb"][0:1, :], in_=p.t["bconv_vrow"][0:1, :])
        inc(cs["ldr"], i, 16)
        i = y.dma_start(out=p.t["bout_sb"][:, :], in_=p.t["bout"][:, :])
        inc(cs["ldr"], i, 16)
        i = y.dma_start(out=p.t["idx_sb"][:, :], in_=p.t["idx"][:, :])
        inc(cs["ldr"], i, 16)
    cs["_ldr_all"] = cs["ldr"].n

    @blk.gpsimd
    def _(g):
        i = g.dma_start(out=p.t["wconv_sb"][:, :, :, :], in_=p.t["wconv"][:, :, :, :])
        inc(cs["ld"], i, 16)
        i = g.dma_start(out=p.t["nullk_sb"][:, :], in_=p.t["null_k"][:, :])
        inc(cs["ld"], i, 16)
        i = g.dma_start(out=p.t["nvr_sb"][0:1, :], in_=p.t["null_v_row"][0:1, :])
        inc(cs["ld"], i, 16)
        i = g.memset(p.t["ONER"][:, :], 1.0)
        inc(cs["po"], i)
    cs["_att_pre_ld"] = cs["ld"].n

    emit_consts(p, cs)
    tok = emit_ln(p, cs, p.t["RES"], p.t["A1"], (cs["ldr"], cs["_ldr_all"]),
                  banks=[p.t["SCB"][:, 0:512], p.t["SCB"][:, 512:1024]]
                  + [p.t[f"OB{i}"][:, :] for i in range(6)],
                  sqc=p.t["A2"][:, 0:2, 0:2048])
    tok = emit_l0attn(p, cs, p.t["RES"], p.t["A1"], tok)

    @blk.sync
    def _(y):
        wt(y, tok[0], tok[1])
        i = y.dma_start(out=p.t["resid_out"].rearrange("(c p) t -> p c t", p=128),
                        in_=p.t["RES"][:, :, :])
        inc(cs["st"], i, 16)
        wt(y, cs["st"])
    p.close()
    return p


def stage_att_core(inp, c, resid):
    f32 = np.float32
    wqkv = np.asarray(inp["L0_attn_w_qkv"], f32)
    wq = np.stack([np.ascontiguousarray(wqkv[1024 * i:1024 * (i + 1)].T) for i in range(3)])
    wc = np.asarray(inp["L0_attn_w_conv"], f32)
    wconv = np.zeros((128, 3, 8, 128), f32)
    for h in range(16):
        for tau in range(3):
            base = 64 * (h % 2)
            wconv[base:base + 64, tau, h // 2, base:base + 64] = \
                wc[64 * h:64 * h + 64, :, tau].T
    bconv = np.asarray(inp["L0_attn_b_conv"], f32).reshape(DC, 128).T
    bcv = np.zeros((1, 1024), f32)
    bcv[0, :] = np.asarray(inp["L0_attn_b_conv"], f32)
    bout = np.asarray(inp["L0_attn_b_out"], f32).reshape(DC, 128).T
    nullk = np.asarray(inp["L0_attn_null_k"], f32).reshape(D).reshape(DC, 128).T
    nv = np.asarray(inp["L0_attn_null_v"], f32).reshape(D)
    nvr = np.zeros((1, 1040), f32)
    for h in range(16):
        nvr[0, 65 * h:65 * h + 64] = nv[64 * h:64 * h + 64]
        nvr[0, 65 * h + 64] = 1.0
    masks = []
    for s in range(4):
        b = qblocks_of_core(c)[s]
        q0 = 512 * b
        for tj in range(SLOT_MASK0[s], SLOT_TILES[s]):
            bk = 128 * tj + np.arange(128) - 1
            qg = q0 + np.arange(512)
            m = (qg[None, :] >= 3 * bk[:, None]).astype(BF)
            masks.append(m)
    idx = np.zeros(32, np.int32)
    idx[1] = 128 * ((c + 7) % 8)
    for s, b in enumerate(qblocks_of_core(c)):
        idx[8 + s] = 1024 * (b // 4)
        idx[12 + s] = 512 * (b % 4)
    for i4 in range(4):
        g = 4 * c + i4
        if g < 16:
            rc = g if g < 8 else 15 - g
        else:
            rc = g - 16 if g - 16 < 8 else 31 - g
        sc = qblocks_of_core(rc).index(g)
        idx[16 + i4] = 1024 * rc
        idx[20 + i4] = 512 * sc
    d = {
        "resid_in": np.ascontiguousarray(resid[T * c:T * (c + 1)].T),
        "w_qkv0": wq,
        "w_out0": np.ascontiguousarray(np.asarray(inp["L0_attn_w_out"], f32).T),
        "wconv": wconv,
        "bconv": np.ascontiguousarray(bconv),
        "bconv_vrow": bcv,
        "bout": np.ascontiguousarray(bout),
        "null_k": np.ascontiguousarray(nullk),
        "null_v_row": nvr,
        "l0_masks": np.ascontiguousarray(np.stack(masks)),
        "idx": idx.reshape(1, 32),
    }
    return d



def stage_ffn_core(inp, c, resid=None):
    f32 = np.float32
    x = resid if resid is not None else np.asarray(inp["L0_attn_x"], f32).reshape(S, D)
    d = {
        "resid_in": np.ascontiguousarray(x[T * c:T * (c + 1)].T),
        "w_r": np.ascontiguousarray(np.asarray(inp["L0_ffn_W_r"], f32).T),
        "w_k": np.ascontiguousarray(np.asarray(inp["L0_ffn_W_k"], f32).T),
        "w_v": np.ascontiguousarray(np.asarray(inp["L0_ffn_W_v"], f32).T),
        "mu_r": np.ascontiguousarray(np.asarray(inp["L0_ffn_mu_r"], f32).reshape(DC, 128).T),
        "mu_k": np.ascontiguousarray(np.asarray(inp["L0_ffn_mu_k"], f32).reshape(DC, 128).T),
        "idx": np.asarray([[0, ((c + 7) % 8) * 128] + [0] * 30], np.int32),
        "hv": np.asarray([[0.0 if c == 0 else 1.0]], f32),
    }
    return d


NHB = 4
DHB = 256
DHID = 512
ATTB_NCH = 8
NS_COEFFS = ((4.0848, -6.8946, 2.927), (3.9505, -6.3029, 2.6377),
             (3.7418, -5.5913, 2.3037), (2.8769, -3.1427, 1.2046),
             (2.8366, -3.0525, 1.2012))
BASE_LR_INV = float(np.log(np.expm1(0.001)))
BASE_E = float(np.expm1(0.001))
RMS_EPS = 1e-6
FRO_EPS = 1e-7
REN_EPS = 1e-5


def emit_l1attn(p, cs, tok):
    blk = p.blk
    se, ve, pe = cs["se"], cs["ve"], cs["pe"]
    ld, ldr, lq, cc, st = cs["ld"], cs["ldr"], cs["lq"], cs["cc"], cs["st"]
    PB = [p.t[f"PB{i}"] for i in range(8)]
    RES = p.t["RES"]
    AR3 = p.t["AR3"]
    QFC = AR3[:, 0:2, :]
    KFC = AR3[:, 2:4, :]
    VFC = AR3[:, 4:6, :]
    OBUF = AR3[:, 6:8, :]
    PAB = p.t["PAB"]
    XLT = PAB[:, 0:8, :]
    QWB = PAB[:, 8:16, :]
    SQW2 = p.t["SQW2"]
    LRF = SQW2[0:3, :, :].rearrange("p a f -> p (a f)")
    WQS = p.t["WQS"]
    LRW = p.t["LRW"]
    SCR = p.t["SCR"]
    ROWS = p.t["ROWS"]
    W0S, W1S, W2S = p.t["W0S"], p.t["W1S"], p.t["W2S"]
    W0B, W1B, W2B = p.t["W0B"], p.t["W1B"], p.t["W2B"]
    W1TB = p.t["W1TB"]
    PSB = p.t["PSB"]
    MP = [p.t["MPD"], p.t["MPH"], p.t["MPB"]]
    LCOL = p.t["LCOL"]
    FSC = p.t["FSC"]
    R1C = p.t["R1C"]
    idm = p.t["idmat"]

    def XSv(m, dc):
        return PAB[:, 2 * m + dc, :]

    def XTv(m, hc):
        q = 4 * m + hc
        return PAB[:, 6 + q // 2, 256 * (q % 2):256 * (q % 2) + 256]

    def ASv(m, kc):
        q = 2 * m + kc
        return PAB[:, 12 + q // 2, 256 * (q % 2):256 * (q % 2) + 256]

    def BMv(m, kc):
        q = 2 * m + kc
        return PAB[:, 15 + q // 2, 256 * (q % 2):256 * (q % 2) + 256]

    KTM = RES[:, 0, :].bitcast(BF16).rearrange("p (c f) -> p c f", f=256)
    VTM = RES[:, 1, :].bitcast(BF16).rearrange("p (c f) -> p c f", f=256)
    rsem, rval = tok

    @blk.gpsimd
    def _(g):
        g.memset(p.t["ECONS"][:, 0:1], BASE_LR_INV)
        g.memset(p.t["ECONS"][:, 1:2], REN_EPS)
        i = g.memset(p.t["ECONS"][:, 2:3], 1.0)
        inc(cs["po"], i)
    econ_tok = cs["po"].n

    for tb in range(4):
        ts_ = slice(512 * tb, 512 * tb + 512)

        @blk.vector
        def _(v, tb=tb, ts_=ts_):
            if tb == 0:
                wt(v, rsem, rval)
                wt(v, cs["ini"], cs["ini"].n)
            else:
                wt(v, pe, cs[f"_pa_lrpe{tb-1}"])
            for k in range(DC):
                i = v.tensor_copy(XLT[:, k, :], RES[:, k, ts_])
                inc(ve, i); wt(v, ve)
        cast_done = ve.n

        ln_pe0 = pe.n

        @blk.scalar
        def _(s_, ln_pe0=ln_pe0):
            wt(s_, ve, cast_done)
            for k in range(DC):
                if k >= 2:
                    wt(s_, pe, ln_pe0 + (k - 2) + 1)
                i = s_.activation(SQW2[:, k % 2, :], XLT[:, k, :], AF.Square)
                inc(se, i)
                wt(s_, se)
        ln_se0 = se.n - DC

        @blk.tensor
        def _(t, ln_se0=ln_se0):
            wt(t, ve, cast_done)
            for k in range(DC):
                wt(t, se, ln_se0 + k + 1)
                t.matmul(PB[6][0:1, :], p.t["onec"][:, 0:1], XLT[:, k, :],
                         start=(k == 0), stop=(k == DC - 1), skip_group_check=True)
                mm = t.matmul(PB[7][0:1, :], p.t["onec"][:, 0:1], SQW2[:, k % 2, :],
                              start=(k == 0), stop=(k == DC - 1),
                              skip_group_check=True)
                inc(pe, mm)
        ln_stats = pe.n

        @blk.scalar
        def _(s_):
            wt(s_, pe, ln_stats)
            i = s_.activation(ROWS[0:1, 0:512], PB[6][0:1, :], AF.Copy, scale=1.0 / D)
            inc(se, i); wt(s_, se)
        ln_m = se.n

        @blk.vector
        def _(v):
            wt(v, se, ln_m)
            i = v.tensor_tensor(ROWS[0:1, 512:1024], ROWS[0:1, 0:512],
                                ROWS[0:1, 0:512], op=ALU.mult)
            inc(ve, i); wt(v, ve)
            i = v.scalar_tensor_tensor(ROWS[0:1, 512:1024], PB[7][0:1, :], 1.0 / D,
                                       ROWS[0:1, 512:1024], op0=ALU.mult,
                                       op1=ALU.subtract)
            inc(ve, i); wt(v, ve)
        ln_var = ve.n

        @blk.scalar
        def _(s_):
            wt(s_, ve, ln_var)
            i = s_.activation(ROWS[0:1, 512:1024], ROWS[0:1, 512:1024], AF.Ln,
                              bias=LN_EPS)
            inc(se, i); wt(s_, se)
            i = s_.activation(ROWS[0:1, 512:1024], ROWS[0:1, 512:1024], AF.Exp,
                              scale=-0.5)
            inc(se, i); wt(s_, se)
        ln_rstd = se.n

        @blk.vector
        def _(v):
            wt(v, se, ln_rstd)
            i = v.scalar_tensor_tensor(ROWS[0:1, 0:512], ROWS[0:1, 0:512], -1.0,
                                       ROWS[0:1, 512:1024], op0=ALU.mult,
                                       op1=ALU.mult)
            inc(ve, i); wt(v, ve)
        ln_mr = ve.n

        @blk.tensor
        def _(t):
            wt(t, ve, ln_mr)
            t.matmul(PB[6][:, :], p.t["onesb"][0:1, :], ROWS[0:1, 512:1024],
                     start=True, stop=True, skip_group_check=True)
            mm = t.matmul(PB[7][:, :], p.t["onesb"][0:1, :], ROWS[0:1, 0:512],
                          start=True, stop=True, skip_group_check=True)
            inc(pe, mm)
        ln_bc = pe.n

        @blk.vector
        def _(v, ts_=ts_):
            wt(v, pe, ln_bc)
            for k in range(DC):
                i = v.tensor_tensor(SCR[:, 0, :], RES[:, k, ts_], PB[6][:, :],
                                    op=ALU.mult)
                inc(ve, i); wt(v, ve)
                i = v.tensor_tensor(XLT[:, k, :], SCR[:, 0, :], PB[7][:, :],
                                    op=ALU.add)
                inc(ve, i); wt(v, ve)
        xln_done = ve.n

        for which in range(3):
            nrm = p.t["qnw_sb"] if which == 0 else (
                p.t["knw_sb"] if which == 1 else None)
            pe_p0 = pe.n
            ve_e0 = ve.n
            ve_guard = ve.n
            se_guard = se.n

            def proj_pe(oc):
                if which == 2:
                    return pe_p0 + oc + 1
                return pe_p0 + (oc + 1 if oc < 3 else 2 * oc - 1)

            def rows_pe(j):
                return pe_p0 + (4 + 2 * j if j <= 5 else 9 + j)

            @blk.tensor
            def _(t, which=which, pe_p0=pe_p0, ve_e0=ve_e0, se_guard=se_guard,
                  ve_guard=ve_guard):
                wt(t, ve, xln_done)
                wt(t, cs["ldr"], cs["_ldr_all"])
                wt(t, ld, cs["_wq_ld"])
                wt(t, se, se_guard)
                wt(t, ve, ve_guard)
                if which > 0:
                    wt(t, *cs[f"_pa_st{tb}_{which-1}"])
                elif tb > 0:
                    wt(t, *cs[f"_pa_st{tb-1}_2"])

                def emit_rows(j):
                    wt(t, se, se_guard + 2 * j + 2)
                    t.matmul(PB[7][0:1, :], p.t["onec"][:, 0:1], SQW2[:, 0, :],
                             start=(j == 0), stop=(j == DC - 1),
                             skip_group_check=True)
                    mm = t.matmul(PB[3 + j // 2][0:1, :], p.t["onec"][:, 0:1],
                                  SQW2[:, 1, :], start=(j % 2 == 0),
                                  stop=(j % 2 == 1), skip_group_check=True)
                    inc(pe, mm)

                for oc in range(DC):
                    if oc >= 3:
                        wt(t, ve, ve_e0 + (oc - 3) + 1)
                        if which != 2:
                            wt(t, se, se_guard + 2 * (oc - 3) + 2)
                    for kc in range(DC):
                        mm = t.matmul(PB[oc % 3][:, :], WQS[:, which, kc, oc, :],
                                      XLT[:, kc, :], start=(kc == 0),
                                      stop=(kc == DC - 1), skip_group_check=True)
                    inc(pe, mm)
                    if which != 2 and oc >= 2:
                        emit_rows(oc - 2)
                if which != 2:
                    emit_rows(6)
                    emit_rows(7)

            @blk.vector
            def _(v, which=which, nrm=nrm, ve_guard=ve_guard):
                for oc in range(DC):
                    wt(v, pe, proj_pe(oc))
                    if which == 2:
                        i = v.tensor_copy(QWB[:, oc, :], PB[oc % 3][:, :])
                    else:
                        i = v.tensor_scalar(QWB[:, oc, :], PB[oc % 3][:, :],
                                            nrm[:, oc:oc + 1], None, op0=ALU.mult)
                    inc(ve, i); wt(v, ve)
            evac_done = ve.n

            if which != 2:
                @blk.scalar
                def _(s_, ve_e0=ve_e0, ve_guard=ve_guard):
                    wt(s_, ve, ve_guard)
                    for oc in range(DC):
                        wt(s_, pe, proj_pe(oc))
                        if oc >= 1:
                            wt(s_, pe, rows_pe(oc - 1))
                        i = s_.activation(SQW2[:, 0, :], PB[oc % 3][:, :], AF.Square)
                        inc(se, i); wt(s_, se)
                        wt(s_, ve, ve_e0 + oc + 1)
                        i = s_.activation(SQW2[:, 1, :], QWB[:, oc, :], AF.Square)
                        inc(se, i); wt(s_, se)
                rows_done = rows_pe(7)

                @blk.scalar
                def _(s_):
                    wt(s_, pe, rows_done)
                    i = s_.activation(ROWS[0:1, 1024:1536], PB[7][0:1, :], AF.Ln,
                                      scale=1.0 / D, bias=RMS_EPS)
                    inc(se, i); wt(s_, se)
                    i = s_.activation(ROWS[0:1, 1024:1536], ROWS[0:1, 1024:1536],
                                      AF.Exp, scale=-0.5)
                    inc(se, i); wt(s_, se)
                rstd_done = se.n

                apply_done = None
                for h in range(NHB):
                    @blk.scalar
                    def _(s_, h=h):
                        if h == 0:
                            wt(s_, se, rstd_done)
                        else:
                            wt(s_, pe, cs[f"_pa_b{tb}_{which}_{h-1}"])
                        i = s_.activation(ROWS[0:1, 0:512], PB[3 + h][0:1, :],
                                          AF.Sqrt)
                        inc(se, i); wt(s_, se)
                    s_done = se.n

                    @blk.vector
                    def _(v):
                        wt(v, se, s_done)
                        i = v.tensor_tensor(ROWS[0:1, 0:512], ROWS[0:1, 0:512],
                                            ROWS[0:1, 1024:1536], op=ALU.mult)
                        inc(ve, i); wt(v, ve)
                    d_done = ve.n

                    @blk.scalar
                    def _(s_):
                        wt(s_, ve, d_done)
                        i = s_.activation(ROWS[0:1, 0:512], ROWS[0:1, 0:512],
                                          AF.Ln, bias=REN_EPS)
                        inc(se, i); wt(s_, se)
                        i = s_.activation(ROWS[0:1, 0:512], ROWS[0:1, 0:512],
                                          AF.Exp, scale=-1.0)
                        inc(se, i); wt(s_, se)
                    r_done = se.n

                    @blk.vector
                    def _(v):
                        wt(v, se, r_done)
                        i = v.tensor_tensor(ROWS[0:1, 0:512], ROWS[0:1, 0:512],
                                            ROWS[0:1, 1024:1536], op=ALU.mult)
                        inc(ve, i); wt(v, ve)
                    sc_done = ve.n

                    @blk.tensor
                    def _(t, h=h):
                        wt(t, ve, sc_done)
                        mm = t.matmul(PB[3 + h][:, :], p.t["onesb"][0:1, :],
                                      ROWS[0:1, 0:512], start=True, stop=True,
                                      skip_group_check=True)
                        inc(pe, mm)
                    cs[f"_pa_b{tb}_{which}_{h}"] = pe.n
                    bch = pe.n

                    @blk.vector
                    def _(v, h=h, bch=bch):
                        wt(v, pe, bch)
                        for k in (2 * h, 2 * h + 1):
                            i = v.tensor_tensor(QWB[:, k, :], QWB[:, k, :],
                                                PB[3 + h][:, :], op=ALU.mult)
                            inc(ve, i); wt(v, ve)
                apply_done = ve.n
            else:
                apply_done = evac_done

            stsem = cs[("stq", "stk", "stv")[which]]

            @blk.gpsimd
            def _(g, which=which, apply_done=apply_done, ts_=ts_, stsem=stsem):
                wt(g, ve, apply_done)
                nm = ("cc_q_in", "cc_k_in", "cc_v_in")[which]
                i = g.dma_start(out=p.t[nm][:, ts_].rearrange("(c p) t -> p c t",
                                                              p=128),
                                in_=QWB[:, :, :])
                inc(stsem, i, 16)
            cs[f"_pa_st{tb}_{which}"] = (stsem, stsem.n)

        lr_ve_guard = ve.n

        @blk.tensor
        def _(t, lr_ve_guard=lr_ve_guard):
            wt(t, ve, lr_ve_guard)
            wt(t, ld, cs["_wq_ld"])
            for kc in range(DC):
                mm = t.matmul(PB[0][0:12, :], LRW[:, kc, :], XLT[:, kc, :],
                              start=(kc == 0), stop=(kc == DC - 1),
                              skip_group_check=True)
            inc(pe, mm)
        lr_pe = pe.n
        cs[f"_pa_lrpe{tb}"] = lr_pe

        @blk.vector
        def _(v, tb=tb):
            wt(v, pe, lr_pe)
            if tb > 0:
                wt(v, cs["stl"], cs[f"_pa_lrst{tb-1}"])
            i = v.tensor_scalar(SCR[0:12, 0, :], PB[0][0:12, :],
                                p.t["lrb_sb"][0:12, 0:1], None, op0=ALU.add)
            inc(ve, i); wt(v, ve)
        lr_add = ve.n

        @blk.scalar
        def _(s_, lr_add=lr_add):
            wt(s_, ve, lr_add)
            wt(s_, cs["po"], econ_tok)
            i = s_.activation(SCR[0:12, 1, :], SCR[0:12, 0, :], AF.Exp)
            inc(se, i); wt(s_, se)
            i = s_.activation(SCR[0:12, 1, :], SCR[0:12, 1, :], AF.Ln,
                              scale=BASE_E, bias=p.t["ECONS"][0:12, 2:3])
            inc(se, i); wt(s_, se)
        cs[f"_pa_lr{tb}"] = se.n

        @blk.gpsimd
        def _(g, tb=tb, ts_=ts_):
            wt(g, se, cs[f"_pa_lr{tb}"])
            i = g.dma_start(out=p.t["cc_l_in"][:, ts_], in_=SCR[0:12, 1, :])
            inc(cs["stl"], i, 16)
        cs[f"_pa_lrst{tb}"] = cs["stl"].n

    @blk.vector
    def _(v):
        wt(v, ldr, cs["_ldr_all"])
        for (src, dst) in ((W0S, W0B), (W1S, W1B), (W2S, W2B)):
            for dc in range(2):
                i = v.tensor_copy(dst[:, dc, :], src[:, dc, :])
                inc(ve, i); wt(v, ve)
    wcast_done = ve.n

    @blk.scalar
    def _(s_):
        wt(s_, se, cs["_pa_lr3"])
        wt(s_, ldr, cs["_ldr_all"])
        for dc in range(2):
            i = s_.activation(SQW2[:, dc, :], W0S[:, dc, :], AF.Square)
            inc(se, i); wt(s_, se)
    n0sq = se.n

    @blk.tensor
    def _(t):
        wt(t, se, n0sq)
        wt(t, *cs["_pa_st3_2"])
        for dc in range(2):
            mm = t.matmul(PB[6][0:1, :], p.t["onec"][:, 0:1], SQW2[:, dc, :],
                          start=(dc == 0), stop=(dc == 1), skip_group_check=True)
        inc(pe, mm)
    n0mm = pe.n

    @blk.scalar
    def _(s_):
        wt(s_, pe, n0mm)
        i = s_.activation(ROWS[0:1, 0:512], PB[6][0:1, :], AF.Sqrt)
        inc(se, i); wt(s_, se)
        for dc in range(2):
            i = s_.activation(SQW2[:, dc, :], W2S[:, dc, :], AF.Square)
            inc(se, i); wt(s_, se)
    n2sq = se.n

    @blk.tensor
    def _(t):
        wt(t, se, n2sq)
        for dc in range(2):
            mm = t.matmul(PB[6][0:1, :], p.t["onec"][:, 0:1], SQW2[:, dc, :],
                          start=(dc == 0), stop=(dc == 1), skip_group_check=True)
        inc(pe, mm)
    n2mm = pe.n

    @blk.scalar
    def _(s_):
        wt(s_, pe, n2mm)
        i = s_.activation(ROWS[0:1, 512:1024], PB[6][0:1, :], AF.Sqrt)
        inc(se, i); wt(s_, se)
        for dc in range(2):
            i = s_.activation(SQW2[:, dc, :], W1S[:, dc, :], AF.Square)
            inc(se, i); wt(s_, se)
    n1sq = se.n

    @blk.vector
    def _(v):
        wt(v, se, n1sq)
        for dc in range(2):
            i = v.reduce_sum(SCR[:, 0, 0:1], SQW2[:, dc, :],
                             axis=mybir.AxisListType.X)
            inc(ve, i); wt(v, ve)
            i = v.tensor_copy(R1C[:, 2 + dc:3 + dc], SCR[:, 0, 0:1])
            inc(ve, i); wt(v, ve)
    n1red = ve.n

    @blk.scalar
    def _(s_):
        wt(s_, ve, n1red)
        i = s_.activation(R1C[:, 2:4], R1C[:, 2:4], AF.Sqrt)
        inc(se, i); wt(s_, se)
    n1_done = se.n

    def emit_w1t(gate_sems):
        @blk.tensor
        def _(t):
            for sem_, val_ in gate_sems:
                wt(t, sem_, val_)
            for hc in range(4):
                bank_bf = (PB[6] if hc < 2 else PB[7])[:, :].bitcast(BF16)
                for oc in range(2):
                    c0 = 256 * (hc % 2) + 128 * oc
                    mm = t.transpose(bank_bf[:, c0:c0 + 128],
                                     W1B[:, oc, 128 * hc:128 * hc + 128], idm[:, :])
            inc(pe, mm)
        tdone = pe.n

        @blk.vector
        def _(v, tdone=tdone):
            wt(v, pe, tdone)
            for b_ in range(2):
                bank_bf = (PB[6] if b_ == 0 else PB[7])[:, :].bitcast(BF16)
                i = v.tensor_copy(W1TB[:, 2 * b_:2 * b_ + 2, :],
                                  bank_bf[:, 0:512].rearrange("p (a f) -> p a f",
                                                              f=256))
                inc(ve, i); wt(v, ve)
        return ve.n

    w1t_done = emit_w1t([(ve, wcast_done), (se, n1_done)])

    @blk.gpsimd
    def _(g):
        wt(g, cs["stq"], cs["stq"].n)
        wt(g, cs["stk"], cs["stk"].n)
        wt(g, cs["stv"], cs["stv"].n)
        wt(g, cs["stl"], cs["stl"].n)
        for nm in ("q", "k", "v", "l"):
            onm = "cc_vout" if nm == "v" else f"cc_{nm}_out"
            i = g.collective_compute(
                "AllGather", ALU.bypass,
                ins=[p.t[f"cc_{nm}_in"][:, :]], outs=[p.t[onm][:, :]],
                replica_groups=[list(range(NCORE))])
            inc(cc, i)
    ag_done = cc.n

    @blk.gpsimd
    def _(g):
        wt(g, cc, ag_done)
        with g.register("hrow") as r1, g.register("lrow") as r2:
            g.reg_load(r1, p.t["idx_sb"][0:1, 1:2])
            g.reg_load(r2, p.t["idx_sb"][0:1, 2:3])
            o1 = g.snap(r1)
            o2 = g.snap(r2)
            for nm, dst in (("cc_q_out", "my_q"), ("cc_k_out", "my_k"),
                            ("cc_vout", "my_v")):
                i = g.dma_start(
                    out=p.t[dst].rearrange("r (c t) -> r c t", t=T),
                    in_=p.t[nm].rearrange("(c r) t -> r c t", r=D)
                    [bass.ds(o1, 256), :, :])
                inc(lq, i, 16)
            i = g.dma_start(
                out=p.t["my_l"].rearrange("r (c t) -> r c t", t=T),
                in_=p.t["cc_l_out"].rearrange("(c r) t -> r c t", r=12)
                [bass.ds(o2, 3), :, :])
            inc(lq, i, 16)
    hx_done = lq.n

    prev = {"renorm_ve": w1t_done, "dw_pe": pe.n, "ost": st.n, "osc_ve": ve.n}
    for ci in range(ATTB_NCH):
        @blk.gpsimd
        def _(g, ci=ci):
            wt(g, lq, hx_done)
            wt(g, pe, prev["dw_pe"])
            wt(g, ve, prev["renorm_ve"])
            for buf, nm in ((QFC, "my_q"), (KFC, "my_k"), (VFC, "my_v")):
                i = g.dma_start(out=buf,
                                in_=p.t[nm][:, 2048 * ci:2048 * ci + 2048]
                                .rearrange("(c p) t -> p c t", p=128))
                inc(lq, i, 16)
            i = g.dma_start(out=LRF,
                            in_=p.t["my_l"][:, 2048 * ci:2048 * ci + 1024])
            inc(lq, i, 16)
        loads_done = lq.n

        lcol_half_ve = {}
        for lh in range(2):
            if lh == 1:
                @blk.gpsimd
                def _(g, ci=ci):
                    wt(g, ve, lcol_half_ve[0])
                    i = g.dma_start(out=LRF,
                                    in_=p.t["my_l"][:, 2048 * ci + 1024:
                                                    2048 * ci + 2048])
                    inc(cs["lq2"], i, 16)
            lh_lq = (lq, loads_done) if lh == 0 else (cs["lq2"], cs["lq2"].n)

            @blk.tensor
            def _(t, lh=lh, lh_lq=lh_lq):
                wt(t, *lh_lq)
                if lh == 0:
                    wt(t, ve, prev["osc_ve"])
                for cb in range(8):
                    mm = t.matmul(PB[2][:, 3 * cb:3 * cb + 3],
                                  LRF[0:3, 128 * cb:128 * cb + 128],
                                  idm[0:3, 0:3], start=True, stop=True,
                                  skip_group_check=True)
                inc(pe, mm)
            ltr_pe = pe.n

            @blk.vector
            def _(v, lh=lh, ltr_pe=ltr_pe):
                wt(v, pe, ltr_pe)
                i = v.tensor_copy(LCOL[:, 8 * lh:8 * lh + 8, :],
                                  PB[2][:, 0:24].rearrange("p (c j) -> p c j",
                                                           j=3))
                inc(ve, i); wt(v, ve)
            lcol_half_ve[lh] = ve.n
        lcol_done = ve.n

        tr_ve0 = ve.n
        for src, dst, base in ((KFC, KTM, 0), (VFC, VTM, 1)):
            for cb2 in range(8):
                @blk.tensor
                def _(t, src=src, cb2=cb2, base=base):
                    wt(t, lq, loads_done)
                    if base == 0 and cb2 < 2:
                        wt(t, ve, prev["osc_ve"])
                    g_ = base * 8 + cb2
                    if g_ >= 2:
                        wt(t, ve, tr_ve0 + (g_ - 2) + 1)
                    bank_bf = PB[cb2 % 2][:, :].bitcast(BF16)
                    for j in range(2):
                        cb = 2 * cb2 + j
                        for dc in range(2):
                            c0 = 256 * j + 128 * dc
                            mm = t.transpose(bank_bf[:, c0:c0 + 128],
                                             src[:, dc, 128 * cb:128 * cb + 128],
                                             idm[:, :])
                    inc(pe, mm)
                tp_pe = pe.n

                @blk.vector
                def _(v, dst=dst, cb2=cb2, tp_pe=tp_pe):
                    wt(v, pe, tp_pe)
                    bank_bf = PB[cb2 % 2][:, :].bitcast(BF16)
                    i = v.tensor_copy(dst[:, 2 * cb2:2 * cb2 + 2, :],
                                      bank_bf[:, 0:512]
                                      .rearrange("p (a f) -> p a f", f=256))
                    inc(ve, i); wt(v, ve)
        tm_done = ve.n

        for tb in range(4):
            @blk.tensor
            def _(t, tb=tb):
                wt(t, lq, loads_done)
                wt(t, ve, tm_done)
                wt(t, ve, prev["renorm_ve"])
                if tb >= 1:
                    wt(t, ve, cs[f"_sc_oe{ci}_{tb-1}"])
                for r in range(4):
                    for dc in range(2):
                        t.matmul(PB[r][:, :], W0B[:, dc, 128 * r:128 * r + 128],
                                 QFC[:, dc, 512 * tb:512 * tb + 512],
                                 start=(dc == 0), stop=(dc == 1),
                                 skip_group_check=True)
                    for dc in range(2):
                        mm = t.matmul(PB[4 + r][:, :],
                                      W2B[:, dc, 128 * r:128 * r + 128],
                                      QFC[:, dc, 512 * tb:512 * tb + 512],
                                      start=(dc == 0), stop=(dc == 1),
                                      skip_group_check=True)
                inc(pe, mm)
            gqhq = pe.n

            @blk.scalar
            def _(s_):
                wt(s_, pe, gqhq)
                for r in range(4):
                    i = s_.activation(SCR[:, r, :], PB[r][:, :], AF.Silu)
                    inc(se, i)
                    wt(s_, se)
            silu_done = se.n

            @blk.vector
            def _(v, tb=tb):
                wt(v, se, silu_done)
                if tb >= 1:
                    wt(v, pe, cs[f"_sc_o{ci}_{tb-1}"])
                for r in range(4):
                    i = v.tensor_tensor(PSB[:, r, :], SCR[:, r, :],
                                        PB[4 + r][:, :], op=ALU.mult)
                    inc(ve, i); wt(v, ve)
            p_done = ve.n

            @blk.tensor
            def _(t, tb=tb, p_done=p_done):
                wt(t, ve, p_done)
                for m in range(2):
                    for kc in range(4):
                        mm = t.matmul(PB[m][:, :],
                                      W1TB[:, kc, 128 * m:128 * m + 128],
                                      PSB[:, kc, :], start=(kc == 0),
                                      stop=(kc == 3), skip_group_check=True)
                    inc(pe, mm)
            cs[f"_sc_o{ci}_{tb}"] = pe.n
            o_pe = pe.n

            @blk.vector
            def _(v, tb=tb, o_pe=o_pe, ci=ci):
                wt(v, pe, o_pe)
                if tb == 0 and ci > 0:
                    wt(v, st, prev["ost"])
                for m in range(2):
                    i = v.tensor_copy(OBUF[:, m, 512 * tb:512 * tb + 512],
                                      PB[m][:, :])
                    inc(ve, i); wt(v, ve)
            cs[f"_sc_oe{ci}_{tb}"] = ve.n
        osc_ve = ve.n
        prev["osc_ve"] = osc_ve

        @blk.gpsimd
        def _(g, ci=ci):
            wt(g, ve, osc_ve)
            i = g.dma_start(out=p.t["o_dram"][:, 2048 * ci:2048 * ci + 2048]
                            .rearrange("(c p) t -> p c t", p=128),
                            in_=OBUF)
            inc(st, i, 16)
        prev["ost"] = st.n

        dwe_ve = {}
        pass_chain_end = {}
        for h2 in range(2):
            chain0 = ve.n
            for pc in range(16):
                @blk.tensor
                def _(t, h2=h2, pc=pc, chain0=chain0):
                    if h2 == 0 and pc == 0:
                        wt(t, ve, osc_ve)
                        wt(t, ve, lcol_done)
                    if h2 == 1 and pc == 0:
                        wt(t, ve, dwe_ve[0])
                    if h2 == 1 and pc < 2:
                        wt(t, ve, pass_chain_end[0])
                    if pc >= 2:
                        wt(t, ve, chain0 + 7 * (pc - 1))
                    co = 256 * (pc % 2)
                    for dc in range(2):
                        t.matmul(PB[3][:, co:co + 256],
                                 KFC[:, dc, 128 * pc:128 * pc + 128],
                                 W0B[:, dc, 256 * h2:256 * h2 + 256],
                                 start=(dc == 0), stop=(dc == 1),
                                 skip_group_check=True)
                    for dc in range(2):
                        t.matmul(PB[4][:, co:co + 256],
                                 KFC[:, dc, 128 * pc:128 * pc + 128],
                                 W2B[:, dc, 256 * h2:256 * h2 + 256],
                                 start=(dc == 0), stop=(dc == 1),
                                 skip_group_check=True)
                    for oc in range(2):
                        mm = t.matmul(PB[5][:, co:co + 256],
                                      VFC[:, oc, 128 * pc:128 * pc + 128],
                                      W1B[:, oc, 256 * h2:256 * h2 + 256],
                                      start=(oc == 0), stop=(oc == 1),
                                      skip_group_check=True)
                    inc(pe, mm)
                maps_pe = pe.n

                @blk.scalar
                def _(s_, h2=h2, pc=pc, maps_pe=maps_pe, chain0=chain0):
                    wt(s_, pe, maps_pe)
                    if pc >= 2:
                        wt(s_, ve, chain0 + 7 * (pc - 1))
                    if h2 == 1 and pc < 2:
                        wt(s_, ve, pass_chain_end[0])
                    co = 256 * (pc % 2)
                    i = s_.activation(SCR[:, 0, co:co + 256], PB[3][:, co:co + 256],
                                      AF.Sigmoid)
                    inc(se, i)
                sig_done = se.n

                @blk.vector
                def _(v, h2=h2, pc=pc, sig_done=sig_done, chain0=chain0):
                    wt(v, se, sig_done)
                    if pc >= 2:
                        wt(v, pe, cs[f"_dw{ci}_{h2}_{pc-2}"])
                    co = 256 * (pc % 2)
                    G = PB[3][:, co:co + 256]
                    H = PB[4][:, co:co + 256]
                    DH_ = PB[5][:, co:co + 256]
                    sig = SCR[:, 0, co:co + 256]
                    sg = SCR[:, 1, co:co + 256]
                    u = SCR[:, 2, co:co + 256]
                    dg = SCR[:, 3, co:co + 256]
                    i = v.tensor_tensor(sg, G, sig, op=ALU.mult)
                    inc(ve, i); wt(v, ve)
                    i = v.scalar_tensor_tensor(MP[1][:, pc % 2, :], H,
                                               LCOL[:, pc, 1:2], sg,
                                               op0=ALU.mult, op1=ALU.mult)
                    inc(ve, i); wt(v, ve)
                    i = v.tensor_tensor(dg, H, sig, op=ALU.mult)
                    inc(ve, i); wt(v, ve)
                    i = v.scalar_tensor_tensor(u, G, 1.0, sg,
                                               op0=ALU.add, op1=ALU.subtract)
                    inc(ve, i); wt(v, ve)
                    i = v.tensor_tensor(dg, DH_, dg, op=ALU.mult)
                    inc(ve, i); wt(v, ve)
                    i = v.scalar_tensor_tensor(MP[0][:, pc % 2, :], dg,
                                               LCOL[:, pc, 0:1], u,
                                               op0=ALU.mult, op1=ALU.mult)
                    inc(ve, i); wt(v, ve)
                    i = v.scalar_tensor_tensor(MP[2][:, pc % 2, :], DH_,
                                               LCOL[:, pc, 2:3], sg,
                                               op0=ALU.mult, op1=ALU.mult)
                    inc(ve, i); wt(v, ve)
                chain_done = ve.n
                assert chain_done == chain0 + 7 * (pc + 1)

                @blk.tensor
                def _(t, pc=pc, chain_done=chain_done):
                    wt(t, ve, chain_done)
                    for mi, (lhs_tm, mpi) in enumerate(((KTM, 0), (VTM, 1),
                                                        (KTM, 2))):
                        for dc in range(2):
                            mm = t.matmul(PB[mi][:, 256 * dc:256 * dc + 256],
                                          lhs_tm[:, pc, 128 * dc:128 * dc + 128],
                                          MP[mpi][:, pc % 2, :],
                                          start=(pc == 0 and dc == 0),
                                          stop=(pc == 15),
                                          skip_group_check=(dc == 1))
                    inc(pe, mm)
                cs[f"_dw{ci}_{h2}_{pc}"] = pe.n
            pass_chain_end[h2] = ve.n
            dw_half_pe = pe.n

            @blk.vector
            def _(v, h2=h2, dw_half_pe=dw_half_pe):
                wt(v, pe, dw_half_pe)
                for m in range(3):
                    i = v.tensor_copy(
                        PAB[:, 2 * m:2 * m + 2, 256 * h2:256 * h2 + 256],
                        PB[m][:, :].rearrange("p (a f) -> p a f", f=256))
                    inc(ve, i); wt(v, ve)
            dwe_ve[h2] = ve.n
        prev["dw_pe"] = pe.n
        dw_all_ve = dwe_ve[1]

        fro_ve = {}
        for m in range(3):
            @blk.scalar
            def _(s_, m=m):
                wt(s_, ve, dw_all_ve)
                if m > 0:
                    wt(s_, pe, cs[f"_fr{ci}_{m-1}"])
                for dc in range(2):
                    i = s_.activation(SQW2[:, dc, :], XSv(m, dc), AF.Square)
                    inc(se, i); wt(s_, se)
            fsq = se.n

            @blk.tensor
            def _(t, m=m, fsq=fsq):
                wt(t, se, fsq)
                for dc in range(2):
                    mm = t.matmul(PB[6][0:1, :], p.t["onec"][:, 0:1],
                                  SQW2[:, dc, :], start=(dc == 0), stop=(dc == 1),
                                  skip_group_check=True)
                inc(pe, mm)
            fmm = pe.n

            @blk.vector
            def _(v, fmm=fmm):
                wt(v, pe, fmm)
                i = v.reduce_sum(ROWS[0:1, 1024:1025], PB[6][0:1, :],
                                 axis=mybir.AxisListType.X)
                inc(ve, i); wt(v, ve)
            fred = ve.n

            @blk.scalar
            def _(s_, fred=fred):
                wt(s_, ve, fred)
                i = s_.activation(ROWS[0:1, 1025:1026], ROWS[0:1, 1024:1025],
                                  AF.Sqrt)
                inc(se, i); wt(s_, se)
                i = s_.activation(ROWS[0:1, 1026:1027], ROWS[0:1, 1025:1026],
                                  AF.Ln, bias=FRO_EPS)
                inc(se, i); wt(s_, se)
                i = s_.activation(ROWS[0:1, 1026:1027], ROWS[0:1, 1026:1027],
                                  AF.Exp, scale=-1.0)
                inc(se, i); wt(s_, se)
            fs_done = se.n

            @blk.tensor
            def _(t, m=m, fs_done=fs_done):
                wt(t, se, fs_done)
                if m > 0:
                    wt(t, ve, fro_ve[m - 1])
                mm = t.matmul(PB[7][:, 0:1], p.t["onesb"][0:1, :],
                              ROWS[0:1, 1026:1027], start=True, stop=True,
                              skip_group_check=True)
                inc(pe, mm)
            cs[f"_fr{ci}_{m}"] = pe.n
            fbc = pe.n

            @blk.vector
            def _(v, m=m, fbc=fbc):
                wt(v, pe, fbc)
                i = v.tensor_copy(FSC[:, m:m + 1], PB[7][:, 0:1])
                inc(ve, i); wt(v, ve)
                for dc in range(2):
                    i = v.tensor_scalar(XSv(m, dc), XSv(m, dc), FSC[:, m:m + 1],
                                        None, op0=ALU.mult)
                    inc(ve, i); wt(v, ve)
            fro_ve[m] = ve.n

        xt_ve = {}
        for m in range(3):
            @blk.tensor
            def _(t, m=m):
                wt(t, ve, fro_ve[m])
                for hc in range(4):
                    bank_bf = (PB[2 * m] if hc < 2 else PB[2 * m + 1])[:, :] \
                        .bitcast(BF16)
                    for dc in range(2):
                        c0 = 256 * (hc % 2) + 128 * dc
                        mm = t.transpose(bank_bf[:, c0:c0 + 128],
                                         XSv(m, dc)[:, 128 * hc:128 * hc + 128],
                                         idm[:, :])
                inc(pe, mm)
            xtp = pe.n

            @blk.vector
            def _(v, m=m, xtp=xtp):
                wt(v, pe, xtp)
                for b_ in range(2):
                    bank_bf = (PB[2 * m] if b_ == 0 else PB[2 * m + 1])[:, :] \
                        .bitcast(BF16)
                    for j in range(2):
                        i = v.tensor_copy(XTv(m, 2 * b_ + j),
                                          bank_bf[:, 256 * j:256 * j + 256])
                        inc(ve, i); wt(v, ve)
            xt_ve[m] = ve.n

        ns_tok = {m: {"x_ve": fro_ve[m], "xt_ve": xt_ve[m]} for m in range(3)}
        for it, (a_c, b_c, c_c) in enumerate(NS_COEFFS):
            a_pe, a_ve, a2_pe, bm_ve, bx_pe = {}, {}, {}, {}, {}
            for m in range(3):
                @blk.tensor
                def _(t, m=m):
                    wt(t, ve, ns_tok[m]["xt_ve"])
                    for mi in range(2):
                        for nc_ in range(4):
                            mm = t.matmul(PB[2 * m][:, 256 * mi:256 * mi + 256],
                                          XTv(m, nc_)[:, 128 * mi:128 * mi + 128],
                                          XTv(m, nc_),
                                          start=(nc_ == 0 and mi == 0),
                                          stop=(nc_ == 3),
                                          skip_group_check=(mi == 1))
                    inc(pe, mm)
                a_pe[m] = pe.n
            for m in range(3):
                @blk.vector
                def _(v, m=m):
                    wt(v, pe, a_pe[m])
                    for kc in range(2):
                        i = v.tensor_copy(ASv(m, kc),
                                          PB[2 * m][:, 256 * kc:256 * kc + 256])
                        inc(ve, i); wt(v, ve)
                a_ve[m] = ve.n
            for m in range(3):
                @blk.tensor
                def _(t, m=m):
                    wt(t, ve, a_ve[m])
                    for mi in range(2):
                        for kc in range(2):
                            mm = t.matmul(PB[2 * m + 1][:, 256 * mi:256 * mi + 256],
                                          ASv(m, kc)[:, 128 * mi:128 * mi + 128],
                                          ASv(m, kc),
                                          start=(kc == 0 and mi == 0),
                                          stop=(kc == 1),
                                          skip_group_check=(mi == 1))
                    inc(pe, mm)
                a2_pe[m] = pe.n
            for m in range(3):
                @blk.vector
                def _(v, m=m, b_c=b_c, c_c=c_c):
                    wt(v, pe, a2_pe[m])
                    i = v.tensor_scalar(SCR[:, 0, :], PB[2 * m + 1][:, :], c_c,
                                        None, op0=ALU.mult)
                    inc(ve, i); wt(v, ve)
                    for kc in range(2):
                        i = v.scalar_tensor_tensor(
                            BMv(m, kc), PB[2 * m][:, 256 * kc:256 * kc + 256],
                            b_c, SCR[:, 0, 256 * kc:256 * kc + 256],
                            op0=ALU.mult, op1=ALU.add)
                        inc(ve, i); wt(v, ve)
                bm_ve[m] = ve.n
            for m in range(3):
                @blk.tensor
                def _(t, m=m):
                    wt(t, ve, bm_ve[m])
                    for mi in range(2):
                        for jc in range(2):
                            mm = t.matmul(PB[2 * m + mi][:, :],
                                          BMv(m, jc)[:, 128 * mi:128 * mi + 128],
                                          XSv(m, jc), start=(jc == 0),
                                          stop=(jc == 1), skip_group_check=True)
                    inc(pe, mm)
                bx_pe[m] = pe.n
            for m in range(3):
                @blk.vector
                def _(v, m=m, a_c=a_c):
                    wt(v, pe, bx_pe[m])
                    for mi in range(2):
                        i = v.scalar_tensor_tensor(XSv(m, mi), XSv(m, mi), a_c,
                                                   PB[2 * m + mi][:, :],
                                                   op0=ALU.mult, op1=ALU.add)
                        inc(ve, i); wt(v, ve)
                ns_tok[m]["x_ve"] = ve.n
            if it < len(NS_COEFFS) - 1:
                for m in range(3):
                    @blk.tensor
                    def _(t, m=m):
                        wt(t, ve, ns_tok[m]["x_ve"])
                        for hc in range(4):
                            bank_bf = (PB[2 * m] if hc < 2
                                       else PB[2 * m + 1])[:, :].bitcast(BF16)
                            for dc in range(2):
                                c0 = 256 * (hc % 2) + 128 * dc
                                mm = t.transpose(
                                    bank_bf[:, c0:c0 + 128],
                                    XSv(m, dc)[:, 128 * hc:128 * hc + 128],
                                    idm[:, :])
                        inc(pe, mm)
                    xtp = pe.n

                    @blk.vector
                    def _(v, m=m, xtp=xtp):
                        wt(v, pe, xtp)
                        for b_ in range(2):
                            bank_bf = (PB[2 * m] if b_ == 0
                                       else PB[2 * m + 1])[:, :].bitcast(BF16)
                            for j in range(2):
                                i = v.tensor_copy(
                                    XTv(m, 2 * b_ + j),
                                    bank_bf[:, 256 * j:256 * j + 256])
                                inc(ve, i); wt(v, ve)
                    ns_tok[m]["xt_ve"] = ve.n

        for m, (WS_, WB_) in enumerate(((W0S, W0B), (W1S, W1B), (W2S, W2B))):
            @blk.vector
            def _(v, m=m, WS_=WS_):
                wt(v, ve, ns_tok[m]["x_ve"])
                for dc in range(2):
                    i = v.tensor_tensor(WS_[:, dc, :], WS_[:, dc, :], XSv(m, dc),
                                        op=ALU.add)
                    inc(ve, i); wt(v, ve)
            wadd = ve.n

            @blk.scalar
            def _(s_, m=m, WS_=WS_, wadd=wadd):
                wt(s_, ve, wadd)
                if m > 0:
                    wt(s_, ve, cs[f"_renap{ci}_{m-1}"])
                for dc in range(2):
                    i = s_.activation(SQW2[:, dc, :], WS_[:, dc, :], AF.Square)
                    inc(se, i); wt(s_, se)
            wsq = se.n

            if m != 1:
                @blk.tensor
                def _(t, wsq=wsq, m=m):
                    wt(t, se, wsq)
                    if m == 2:
                        wt(t, ve, cs[f"_renap{ci}_0"])
                    for dc in range(2):
                        mm = t.matmul(PB[6][0:1, :], p.t["onec"][:, 0:1],
                                      SQW2[:, dc, :], start=(dc == 0),
                                      stop=(dc == 1), skip_group_check=True)
                    inc(pe, mm)
                rmm = pe.n

                @blk.scalar
                def _(s_, rmm=rmm):
                    wt(s_, pe, rmm)
                    i = s_.activation(ROWS[0:1, 1024:1536], PB[6][0:1, :], AF.Sqrt)
                    inc(se, i); wt(s_, se)
                    i = s_.activation(ROWS[0:1, 1024:1536], ROWS[0:1, 1024:1536],
                                      AF.Ln, bias=REN_EPS)
                    inc(se, i); wt(s_, se)
                    i = s_.activation(ROWS[0:1, 1024:1536], ROWS[0:1, 1024:1536],
                                      AF.Exp, scale=-1.0)
                    inc(se, i); wt(s_, se)
                rrec = se.n

                @blk.vector
                def _(v, m=m, rrec=rrec):
                    wt(v, se, rrec)
                    nrow = ROWS[0:1, 0:512] if m == 0 else ROWS[0:1, 512:1024]
                    i = v.tensor_tensor(ROWS[0:1, 1024:1536], ROWS[0:1, 1024:1536],
                                        nrow, op=ALU.mult)
                    inc(ve, i); wt(v, ve)
                rsc = ve.n

                @blk.tensor
                def _(t, rsc=rsc):
                    wt(t, ve, rsc)
                    mm = t.matmul(PB[6][:, :], p.t["onesb"][0:1, :],
                                  ROWS[0:1, 1024:1536], start=True, stop=True,
                                  skip_group_check=True)
                    inc(pe, mm)
                rbc = pe.n

                @blk.vector
                def _(v, m=m, WS_=WS_, WB_=WB_, rbc=rbc):
                    wt(v, pe, rbc)
                    for dc in range(2):
                        i = v.tensor_tensor(WS_[:, dc, :], WS_[:, dc, :],
                                            PB[6][:, :], op=ALU.mult)
                        inc(ve, i); wt(v, ve)
                        i = v.tensor_copy(WB_[:, dc, :], WS_[:, dc, :])
                        inc(ve, i); wt(v, ve)
                cs[f"_renap{ci}_{m}"] = ve.n
            else:
                @blk.vector
                def _(v, wsq=wsq):
                    wt(v, se, wsq)
                    for dc in range(2):
                        i = v.reduce_sum(SCR[:, 0, 0:1], SQW2[:, dc, :],
                                         axis=mybir.AxisListType.X)
                        inc(ve, i); wt(v, ve)
                        i = v.tensor_copy(R1C[:, dc:dc + 1], SCR[:, 0, 0:1])
                        inc(ve, i); wt(v, ve)
                w1red = ve.n

                @blk.scalar
                def _(s_, w1red=w1red):
                    wt(s_, ve, w1red)
                    i = s_.activation(R1C[:, 0:2], R1C[:, 0:2], AF.Sqrt)
                    inc(se, i); wt(s_, se)
                    wt(s_, cs["po"], econ_tok)
                    i = s_.activation(R1C[:, 0:2], R1C[:, 0:2], AF.Ln,
                                      bias=p.t["ECONS"][:, 1:2])
                    inc(se, i); wt(s_, se)
                    i = s_.activation(R1C[:, 0:2], R1C[:, 0:2], AF.Exp,
                                      scale=-1.0)
                    inc(se, i); wt(s_, se)
                w1rec = se.n

                @blk.vector
                def _(v, WS_=WS_, WB_=WB_, w1rec=w1rec):
                    wt(v, se, w1rec)
                    i = v.tensor_tensor(R1C[:, 0:2], R1C[:, 0:2], R1C[:, 2:4],
                                        op=ALU.mult)
                    inc(ve, i); wt(v, ve)
                    for dc in range(2):
                        i = v.tensor_scalar(WS_[:, dc, :], WS_[:, dc, :],
                                            R1C[:, dc:dc + 1], None, op0=ALU.mult)
                        inc(ve, i); wt(v, ve)
                        i = v.tensor_copy(WB_[:, dc, :], WS_[:, dc, :])
                        inc(ve, i); wt(v, ve)
                cs[f"_renap{ci}_{m}"] = ve.n

        w1t2 = emit_w1t([(ve, cs[f"_renap{ci}_1"]), (ve, cs[f"_renap{ci}_2"])])
        prev["renorm_ve"] = w1t2

    @blk.gpsimd
    def _(g):
        wt(g, st, prev["ost"])
        i = g.collective_compute(
            "AllGather", ALU.bypass,
            ins=[p.t["o_dram"][:, :]], outs=[p.t["cc_oo"][:, :]],
            replica_groups=[[0, 1, 2, 3], [4, 5, 6, 7]])
        inc(cc, i)
    oag = cc.n

    @blk.sync
    def _(y):
        wt(y, pe, prev["dw_pe"])
        wt(y, ve, prev["renorm_ve"])
        i = y.dma_start(out=RES[:, :, :],
                        in_=p.t["resid_in"].rearrange("(c p) t -> p c t", p=128))
        inc(ldr, i, 16)
    res2 = ldr.n

    add_ve0 = ve.n

    @blk.gpsimd
    def _(g):
        wt(g, cc, oag)
        wt(g, pe, prev["dw_pe"])
        wt(g, ve, prev["osc_ve"])
        with g.register("oex") as r1:
            g.reg_load(r1, p.t["idx_sb"][0:1, 0:1])
            o1 = g.snap(r1)
            for h in range(NHB):
                if h >= 2:
                    wt(g, ve, add_ve0 + 2 * (h - 1))
                i = g.dma_start(out=AR3[:, 2 * (h % 2):2 * (h % 2) + 2, :],
                                in_=p.t["cc_oo"][256 * h:256 * h + 256,
                                                 bass.ds(o1, 2048)]
                                .rearrange("(c p) t -> p c t", p=128))
                inc(cs["lxA"] if h % 2 == 0 else cs["lxB"], i, 16)

    @blk.vector
    def _(v):
        for h in range(NHB):
            wt(v, cs["lxA"] if h % 2 == 0 else cs["lxB"], 16 * (h // 2 + 1))
            wt(v, ldr, res2)
            for j in range(2):
                i = v.tensor_tensor(RES[:, 2 * h + j, :], RES[:, 2 * h + j, :],
                                    AR3[:, 2 * (h % 2) + j, :], op=ALU.add)
                inc(ve, i); wt(v, ve)
    return (ve, ve.n)


def build_prog_attB():
    p = Prog("attB")
    p.din("resid_in", [D, T], F32)
    p.din("wqkvT", [3, D, D], F32)
    p.din("lrwT", [D, 12], F32)
    p.din("lrb", [12, 1], F32)
    p.din("qnw", [128, DC], F32)
    p.din("knw", [128, DC], F32)
    p.din("w0T_in", [DHB, DHID], F32)
    p.din("w1_in", [DHB, DHID], F32)
    p.din("w2T_in", [DHB, DHID], F32)
    p.din("idx", [1, 32], I32)
    p.dout("resid_out", [D, T], F32)
    for nm in ("cc_q_in", "cc_k_in", "cc_v_in"):
        p.dtmp(nm, [D, T], BF16)
    p.dtmp("cc_l_in", [12, T], BF16)
    p.dtmp("cc_q_out", [NCORE * D, T], BF16, shared=True)
    p.dtmp("cc_k_out", [NCORE * D, T], BF16, shared=True)
    p.dtmp("cc_vout", [NCORE * D, T], BF16, shared=True)
    p.dtmp("cc_l_out", [NCORE * 12, T], BF16, shared=True)
    p.dtmp("o_dram", [DHB, S], BF16)
    p.dtmp("cc_oo", [4 * DHB, S], BF16)
    p.dtmp("my_q", [DHB, S], BF16)
    p.dtmp("my_k", [DHB, S], BF16)
    p.dtmp("my_v", [DHB, S], BF16)
    p.dtmp("my_l", [3, S], BF16)

    p.sb("RES", [128, DC, T], F32)
    p.sb("AR3", [128, DC, T], BF16)
    p.sb("WQS", [128, 3, DC, DC, 128], BF16)
    p.sb("PAB", [128, 18, 512], BF16)
    p.sb("SQW2", [128, 2, 512], BF16)
    p.sb("LRW", [128, DC, 12], BF16)
    p.sb("lrb_sb", [128, 1], F32)
    p.sb("W0S", [128, 2, 512], F32)
    p.sb("W1S", [128, 2, 512], F32)
    p.sb("W2S", [128, 2, 512], F32)
    p.sb("W0B", [128, 2, 512], BF16)
    p.sb("W1B", [128, 2, 512], BF16)
    p.sb("W2B", [128, 2, 512], BF16)
    p.sb("W1TB", [128, 4, 256], BF16)
    p.sb("PSB", [128, 4, 512], BF16)
    p.sb("MPD", [128, 2, 256], BF16)
    p.sb("MPH", [128, 2, 256], BF16)
    p.sb("MPB", [128, 2, 256], BF16)
    p.sb("SCR", [128, 4, 512], F32)
    p.sb("LCOL", [128, 16, 3], F32)
    p.sb("FSC", [128, 4], F32)
    p.sb("R1C", [128, 4], F32)
    p.sb("ECONS", [128, 4], F32)
    p.sb("qnw_sb", [128, DC], F32)
    p.sb("knw_sb", [128, DC], F32)
    p.sb("ROWS", [1, 1536], F32)
    p.sb("idmat", [128, 128], BF16)
    p.sb("ones1", [1, 128], BF16)
    p.sb("onesb", [1, 128], F32)
    p.sb("onec", [128, 1], BF16)
    p.sb("CONSTS", [1, 16], F32)
    p.sb("idx_sb", [1, 32], I32)
    for _i in range(8):
        p.ps(f"PB{_i}", [128, 512], F32)

    register_const(p, LN_EPS)
    register_const(p, RMS_EPS)
    register_const(p, FRO_EPS)
    register_const(p, REN_EPS)
    register_const(p, BASE_LR_INV)
    cs = {k: p.sem(k) for k in ("se", "ve", "pe", "po", "ld", "ldr", "lq",
                                "lq2", "cc", "st", "stf", "ini", "stq", "stk",
                                "stv", "stl", "lxA", "lxB")}
    blk = p.block()

    @blk.sync
    def _(y):
        i = y.dma_start(out=p.t["RES"][:, :, :],
                        in_=p.t["resid_in"].rearrange("(c p) t -> p c t", p=128))
        inc(cs["ldr"], i, 16)
        i = y.dma_start(out=p.t["lrb_sb"][0:12, 0:1], in_=p.t["lrb"][:, :])
        inc(cs["ldr"], i, 16)
        i = y.dma_start(out=p.t["qnw_sb"][:, :], in_=p.t["qnw"][:, :])
        inc(cs["ldr"], i, 16)
        i = y.dma_start(out=p.t["knw_sb"][:, :], in_=p.t["knw"][:, :])
        inc(cs["ldr"], i, 16)
        for nm, dst in (("w0T_in", "W0S"), ("w1_in", "W1S"), ("w2T_in", "W2S")):
            i = y.dma_start(out=p.t[dst][:, :, :],
                            in_=p.t[nm].rearrange("(a p) h -> p a h", p=128))
            inc(cs["ldr"], i, 16)
        i = y.dma_start(out=p.t["idx_sb"][:, :], in_=p.t["idx"][:, :])
        inc(cs["ldr"], i, 16)
    cs["_ldr_all"] = cs["ldr"].n

    @blk.gpsimd
    def _(g):
        for wi in range(3):
            for kc in range(DC):
                i = g.dma_start(
                    out=p.t["WQS"][:, wi, kc, :, :].rearrange("p a f -> p (a f)"),
                    in_=p.t["wqkvT"][wi, 128 * kc:128 * kc + 128, :])
                inc(cs["ld"], i, 16)
        i = g.dma_start(out=p.t["LRW"][:, :, :],
                        in_=p.t["lrwT"].rearrange("(c p) j -> p c j", p=128))
        inc(cs["ld"], i, 16)
    cs["_wq_ld"] = cs["ld"].n

    emit_consts(p, cs)
    tok = emit_l1attn(p, cs, (cs["ldr"], cs["_ldr_all"]))

    @blk.sync
    def _(y):
        wt(y, tok[0], tok[1])
        i = y.dma_start(out=p.t["resid_out"].rearrange("(c p) t -> p c t", p=128),
                        in_=p.t["RES"][:, :, :])
        inc(cs["stf"], i, 16)
        wt(y, cs["stf"])
    p.close()
    return p


def stage_attB_core(inp, c, resid):
    f32 = np.float32
    hd = c % NHB
    qkvw = np.asarray(inp["L1_attn_qkv_w"], f32)
    wq = np.stack([np.ascontiguousarray(qkvw[1024 * i:1024 * (i + 1)].T)
                   for i in range(3)])
    lrw = np.asarray(inp["L1_attn_lr_proj_w"], f32)
    lrb = np.asarray(inp["L1_attn_lr_proj_b"], f32).reshape(12, 1)
    d = {
        "resid_in": np.ascontiguousarray(resid[T * c:T * (c + 1)].T),
        "wqkvT": wq,
        "lrwT": np.ascontiguousarray(lrw.T),
        "lrb": lrb,
        "qnw": np.ascontiguousarray(
            np.asarray(inp["L1_attn_q_norm_w"], f32).reshape(DC, 128).T),
        "knw": np.ascontiguousarray(
            np.asarray(inp["L1_attn_k_norm_w"], f32).reshape(DC, 128).T),
        "w0T_in": np.ascontiguousarray(np.asarray(inp["L1_attn_w0"], f32)[hd].T),
        "w1_in": np.ascontiguousarray(np.asarray(inp["L1_attn_w1"], f32)[hd]),
        "w2T_in": np.ascontiguousarray(np.asarray(inp["L1_attn_w2"], f32)[hd].T),
        "idx": np.asarray([[2048 * c, 256 * hd, 3 * hd] + [0] * 29],
                          np.int32),
    }
    return d


_PROGS = {}


def _get_prog(name):
    if name not in _PROGS:
        _PROGS[name] = {"ffn": build_prog_ffn, "att": build_prog_att,
                        "moe": build_prog_moe, "attB": build_prog_attB}[name]()
    return _PROGS[name]


def _launch(name, stage_fn, inputs, resid):
    p = _get_prog(name)
    in_maps = [stage_fn(inputs, c, resid) for c in range(NCORE)]
    res = run_bass_kernel_spmd(p.nc, in_maps, core_ids=list(range(NCORE)))
    return np.concatenate(
        [np.asarray(res.results[c]["resid_out"]).T for c in range(NCORE)],
        axis=0)


def kernel(**inputs):
    x = np.asarray(inputs["L0_attn_x"], np.float32).reshape(S, D)
    s1 = _launch("ffn", lambda i, c, r: stage_ffn_core(i, c, r), inputs, x)
    s2 = _launch("att", stage_att_core, inputs, s1)
    s3 = _launch("ffn", lambda i, c, r: stage_ffn_core(i, c, r), inputs, s2)
    s4 = _launch("moe", stage_moe_core, inputs, s3)
    s5 = _launch("attB", stage_attB_core, inputs, s4)
    s6 = _launch("moe", stage_moe_core, inputs, s5)
    return np.ascontiguousarray(s6.reshape(1, S, D), dtype=np.float32)


if __name__ == "__main__":
    pass
